# version 12
# baseline (speedup 1.0000x reference)
import math
import numpy as np
from contextlib import ExitStack

import concourse.bass as bass
import concourse.mybir as mybir
from concourse.bass_utils import run_bass_kernel_spmd

F32 = mybir.dt.float32
BF16 = mybir.dt.bfloat16
I32 = mybir.dt.int32
AF = mybir.ActivationFunctionType
ALU = mybir.AluOpType
AX = mybir.AxisListType

ENGINES = ("sync", "scalar", "vector", "tensor", "gpsimd")
N_DMA_SEMS = 24


class MK:

    def __init__(self, nc):
        self.nc = nc
        self.es = ExitStack()
        self.sems = {e: self.es.enter_context(nc.semaphore(f"mk_{e}")) for e in ENGINES}
        for i in range(N_DMA_SEMS):
            self.sems[i] = self.es.enter_context(nc.semaphore(f"mkd_{i}"))
        self.cnt = {k: 0 for k in self.sems}
        self.dma_rr = 0
        self.dma_eng = {}
        self.waited = {}
        self.phases = []
        self.cur = None
        self.phase()

    def phase(self):
        if self.cur is not None:
            self.fence_dma()
        self.cur = {e: [] for e in ENGINES}
        self.phases.append(self.cur)
        for cns in ENGINES:
            for k in self.sems:
                self.waited[(cns, k)] = self.cnt[k]

    def _wait(self, eng, ev):
        if ev is None:
            return
        k, val = ev
        if self.waited.get((eng, k), 0) >= val:
            return
        self.waited[(eng, k)] = val
        sem = self.sems[k]
        self.cur[eng].append(lambda E, sem=sem, val=val: E.wait_ge(sem, val))

    def step(self, eng, deps, emit, inc=1, key=None):
        for d in deps:
            self._wait(eng, d)
        k = eng if key is None else key
        self.cnt[k] += inc
        sem = self.sems[k]

        def run(E, emit=emit, sem=sem, inc=inc):
            last = emit(E)
            if isinstance(last, (list, tuple)):
                last = last[-1]
            last.then_inc(sem, inc)
        self.cur[eng].append(run)
        return (k, self.cnt[k])

    def chain(self, eng, deps, emits, extra=None):
        ev = None
        for i, em in enumerate(emits):
            d = list(deps if i == 0 else []) + ([ev] if ev else [])
            if extra and extra[i]:
                d += list(extra[i])
            ev = self.step(eng, d, em)
        return ev

    def fence_dma(self):
        for k, eng in self.dma_eng.items():
            self._wait(eng, (k, self.cnt[k]))

    def fence_all(self):
        self.fence_dma()
        for e in ENGINES:
            if self.cnt[e]:
                self._wait("sync", (e, self.cnt[e]))

    def dma(self, eng, deps, out, in_, **kw):
        k = self.dma_rr % N_DMA_SEMS
        self.dma_rr += 1
        self.dma_eng[k] = eng
        self._wait(eng, (k, self.cnt[k]))
        return self.step(eng, deps,
                         lambda E: E.dma_start(out=out, in_=in_, **kw),
                         inc=16, key=k)

    def build_blocks(self):
        for ph in self.phases:
            with self.nc.Block() as blk:
                for e in ENGINES:
                    ops = ph[e]
                    if not ops:
                        continue
                    getattr(blk, e)(lambda E, ops=ops: [f(E) for f in ops])


class CFG:
    def __init__(self, D=2048, NH=16, NKV=4, FEXP=768, NEXP=32, FSH=1536,
                 TOPK=6, OWN=2048, LB=6144, NQ=4, B=2, S=8192, debug=False):
        self.debug = debug
        self.D, self.NH, self.NKV = D, NH, NKV
        self.FEXP, self.NEXP, self.FSH, self.TOPK = FEXP, NEXP, FSH, TOPK
        self.OWN, self.LB, self.NQ, self.B, self.S = OWN, LB, NQ, B, S
        self.DH = 128
        self.NDC = D // 128
        self.NFC = FEXP // 128
        self.NSC = FSH // 128
        self.GR = NH // NKV
        self.NG = NH // 4
        assert self.GR == 4 and self.NG == NKV
        self.SPT = OWN + LB
        self.NT = self.SPT // 128
        self.NT_OWN = OWN // 128
        self.NST = self.SPT // 512
        self.NST_OWN = OWN // 512
        self.HALF = OWN // 2
        self.NTT = self.HALF // 128
        self.NTG = self.HALF // 512
        self.NDG = D // 512
        assert self.NTG >= 1 and LB == (NQ - 1) * OWN
        self.lam = [2.0 ** (-8.0 * (h + 1) / NH) for h in range(NH)]
        self.blk = [math.exp(-l * 128.0) for l in self.lam]


def build_program(nc, c):
    D, NDC, NH, NKV = c.D, c.NDC, c.NH, c.NKV
    NG, NFC, NEXP, FEXP = c.NG, c.NFC, c.NEXP, c.FEXP
    NTT, NDG, NTG, HALF = c.NTT, c.NDG, c.NTG, c.HALF
    KW = NKV * 128
    QW = NH * 128

    X = nc.dram_tensor("x", [c.SPT, D], F32, kind="ExternalInput").ap()
    POS = nc.dram_tensor("pos", [1, c.SPT], F32, kind="ExternalInput").ap()
    WQT = nc.dram_tensor("wqT", [D, QW], F32, kind="ExternalInput").ap()
    WKT = nc.dram_tensor("wkT", [D, KW], F32, kind="ExternalInput").ap()
    WVT = nc.dram_tensor("wvT", [D, KW], F32, kind="ExternalInput").ap()
    WOT = nc.dram_tensor("woT", [QW, D], F32, kind="ExternalInput").ap()
    GATE = nc.dram_tensor("gate_r", [128, NDC * NEXP], F32, kind="ExternalInput").ap()
    WG = nc.dram_tensor("wg", [NEXP, D, FEXP], F32, kind="ExternalInput").ap()
    WU = nc.dram_tensor("wu", [NEXP, D, FEXP], F32, kind="ExternalInput").ap()
    WD = nc.dram_tensor("wd", [NEXP, FEXP, D], F32, kind="ExternalInput").ap()
    SG = nc.dram_tensor("sg", [D, c.FSH], F32, kind="ExternalInput").ap()
    SU = nc.dram_tensor("su", [D, c.FSH], F32, kind="ExternalInput").ap()
    SD = nc.dram_tensor("sd", [c.FSH, D], F32, kind="ExternalInput").ap()
    OUT = nc.dram_tensor("out", [c.OWN, D], F32, kind="ExternalOutput").ap()

    scr_kind = "ExternalOutput" if c.debug else "Internal"
    WG_BF = nc.dram_tensor("wg_bf_scr", [NEXP, NDC, 128, FEXP], BF16, kind="Internal").ap()
    WU_BF = nc.dram_tensor("wu_bf_scr", [NEXP, NDC, 128, FEXP], BF16, kind="Internal").ap()
    WD_BF = nc.dram_tensor("wd_bf_scr", [NEXP, NFC, 128, D], BF16, kind="Internal").ap()
    NSH2 = c.NSC // 2
    FSH2c = NSH2 * 128
    SG_BF = nc.dram_tensor("sg_bf_scr", [2, NDC, 128, FSH2c], BF16, kind="Internal").ap()
    SU_BF = nc.dram_tensor("su_bf_scr", [2, NDC, 128, FSH2c], BF16, kind="Internal").ap()
    SD_BF = nc.dram_tensor("sd_bf_scr", [c.NSC, 128, D], BF16, kind="Internal").ap()
    XN_SCR = nc.dram_tensor("xn_scr", [128, NDC * c.OWN], BF16, kind=scr_kind).ap()
    KVSNAP = nc.dram_tensor("kvsnap", [c.NT_OWN, 128, NH * 128], BF16, kind=scr_kind).ap()
    OFEAT = nc.dram_tensor("ofeat", [c.NT_OWN, 128, NH * 128], BF16, kind=scr_kind).ap()
    ATTN = nc.dram_tensor("attn_scr", [c.OWN, D], F32, kind=scr_kind).ap()

    ident_bf = nc.alloc_sbuf_tensor("ident_bf", [128, 128], BF16)
    ident_f32 = nc.alloc_sbuf_tensor("ident_f32", [128, 128], F32)
    iota_p = nc.alloc_sbuf_tensor("iota_p", [128, 1], F32)
    iota_row = nc.alloc_sbuf_tensor("iota_row", [128, 128], F32)
    ones_row = nc.alloc_sbuf_tensor("ones_row", [1, 128], F32)
    pi_col = nc.alloc_sbuf_tensor("pi_col", [128, 1], F32)
    eps_col = nc.alloc_sbuf_tensor("eps_col", [128, 1], F32)

    pstg = [nc.alloc_sbuf_tensor(f"pump_f{i}", [128, 1024], F32) for i in range(2)]
    pbf = [nc.alloc_sbuf_tensor(f"pump_b{i}", [128, 1024], BF16) for i in range(2)]
    ps = [nc.alloc_psum_tensor(f"psb{i}", [128, 512], F32) for i in range(7)]
    ps7 = nc.alloc_psum_tensor("psb7", [128, 1024], BF16)

    mk = MK(nc)
    V, S, T, G, Y = "vector", "scalar", "tensor", "gpsimd", "sync"

    class Pump:
        def __init__(self):
            self.jobs = []
            self.done = {}
            self.i = 0
            self.stg_free = [None, None]
            self.bf_free = [None, None]
            self.ncast = 0

        def add(self, key, blocks):
            for b_ in blocks:
                self.jobs.append((key,) + b_)

        def tick(self, n):
            for _ in range(n):
                if self.i >= len(self.jobs):
                    return
                key, sap, dap, wdt = self.jobs[self.i]
                s = self.i % 2
                ld = mk.dma(Y, [self.stg_free[s]], pstg[s][:, 0:wdt], sap)
                ce = mk.step(V, [ld, self.bf_free[s]],
                             lambda E, s=s, wdt=wdt:
                             E.tensor_copy(pbf[s][:, 0:wdt], pstg[s][:, 0:wdt]))
                self.stg_free[s] = ce
                st = mk.dma(S, [ce], dap, pbf[s][:, 0:wdt])
                self.bf_free[s] = st
                self.done[key] = st
                self.i += 1

        def finish(self):
            self.tick(len(self.jobs))

    pump = Pump()
    GB = max(1, min(NDC, 1024 // FEXP))
    for e in range(NEXP):
        pump.add(("wg", e), [
            (WG[e, dc * 128:(dc + GB) * 128, :].rearrange("(b p) f -> p b f", p=128),
             WG_BF[e, dc:dc + GB].transpose([1, 0, 2]),
             GB * FEXP)
            for dc in range(0, NDC, GB)])
        pump.add(("wu", e), [
            (WU[e, dc * 128:(dc + GB) * 128, :].rearrange("(b p) f -> p b f", p=128),
             WU_BF[e, dc:dc + GB].transpose([1, 0, 2]),
             GB * FEXP)
            for dc in range(0, NDC, GB)])
        WDW = min(1024, D)
        pump.add(("wd", e), [(WD[e, fc * 128:(fc + 1) * 128, hh * WDW:(hh + 1) * WDW],
                              WD_BF[e, fc, :, hh * WDW:(hh + 1) * WDW], WDW)
                             for fc in range(NFC) for hh in range(D // WDW)])
    for sh in range(2):
        pump.add(("sg", sh), [(SG[dc * 128:(dc + 1) * 128, sh * FSH2c:(sh + 1) * FSH2c],
                               SG_BF[sh, dc], FSH2c) for dc in range(NDC)])
        pump.add(("su", sh), [(SU[dc * 128:(dc + 1) * 128, sh * FSH2c:(sh + 1) * FSH2c],
                               SU_BF[sh, dc], FSH2c) for dc in range(NDC)])
    WDW2 = min(1024, D)
    pump.add(("sd", 0), [(SD[fc * 128:(fc + 1) * 128, hh * WDW2:(hh + 1) * WDW2],
                          SD_BF[fc, :, hh * WDW2:(hh + 1) * WDW2], WDW2)
                         for fc in range(c.NSC) for hh in range(D // WDW2)])
    A1_BLOCKS = 0
    for j in pump.jobs:
        if j[0][0] in ("wg", "wu", "wd") and isinstance(j[0][1], int) and j[0][1] >= 14:
            break
        A1_BLOCKS += 1
    PUMP_PER_TILE = (A1_BLOCKS + c.NT - 1) // c.NT

    a_scope = ExitStack()

    def sba(name, shape, dtype):
        return a_scope.enter_context(nc.sbuf_tensor(name, shape, dtype))

    cos_own = sba("cos_o", [128, c.OWN], BF16)
    sin_own = sba("sin_o", [128, c.OWN], BF16)
    rotm = sba("rotm", [128, 128], BF16)
    kdec_cols = sba("kdec_cols", [128, NH], F32)
    k_own = sba("k_own", [128, NKV * c.OWN], BF16)
    v_own = sba("v_own", [128, c.NT_OWN * KW], BF16)

    lb_scope = ExitStack()
    cos_lb = lb_scope.enter_context(nc.sbuf_tensor("cos_lb", [128, c.LB], BF16))
    sin_lb = lb_scope.enter_context(nc.sbuf_tensor("sin_lb", [128, c.LB], BF16))

    def tabsl(which, lo, hi):
        if hi <= c.LB:
            return (cos_lb if which == 0 else sin_lb)[:, lo:hi]
        return (cos_own if which == 0 else sin_own)[:, lo - c.LB:hi - c.LB]

    p0 = ExitStack()
    pos_sb = p0.enter_context(nc.sbuf_tensor("pos_sb", [1, c.SPT], F32))
    invfreq = p0.enter_context(nc.sbuf_tensor("invfreq", [1, 128], F32))
    iv_raw = p0.enter_context(nc.sbuf_tensor("iv_raw", [1, 128], F32))
    iv_g = p0.enter_context(nc.sbuf_tensor("iv_g", [1, 128], F32))
    rampb = p0.enter_context(nc.sbuf_tensor("rampb", [128, 128], F32))
    scr0 = p0.enter_context(nc.sbuf_tensor("p0scr0", [128, 512], F32))
    scr1 = p0.enter_context(nc.sbuf_tensor("p0scr1", [128, 512], F32))
    scri = p0.enter_context(nc.sbuf_tensor("p0scri", [128, 512], I32))
    scrH = p0.enter_context(nc.sbuf_tensor("p0scrH", [128, 512], F32))
    scrS = p0.enter_context(nc.sbuf_tensor("p0scrS", [128, 512], F32))

    ev_iota = mk.step(G, [], lambda E: [
        E.iota(iota_p[:], [[1, 1]], base=0, channel_multiplier=1,
               allow_small_or_imprecise_dtypes=True),
        E.iota(iota_row[:], [[1, 128]], base=0, channel_multiplier=0,
               allow_small_or_imprecise_dtypes=True),
        E.iota(iv_raw[:], [[1, 128]], base=0, channel_multiplier=0,
               allow_small_or_imprecise_dtypes=True),
    ])
    ev_pos = mk.dma(Y, [], pos_sb[:], POS)
    ev_const = mk.step(V, [ev_iota], lambda E: [
        E.tensor_single_scalar(ident_bf[:], iota_row[:], iota_p[:, 0:1], ALU.is_equal),
        E.tensor_single_scalar(ident_f32[:], iota_row[:], iota_p[:, 0:1], ALU.is_equal),
        E.memset(ones_row[:], 1.0),
        E.memset(pi_col[:], math.pi),
        E.memset(eps_col[:], 1e-5),
    ])
    ev_ivm = mk.chain(V, [ev_iota, ev_const], [
        lambda E: E.tensor_single_scalar(iv_g[:], iv_raw[:], 64.0, ALU.is_ge),
        lambda E: E.scalar_tensor_tensor(iv_g[:], iv_g[:], -64.0, iv_raw[:],
                                         ALU.mult, ALU.add),
    ])
    ev_if = mk.step(S, [ev_ivm], lambda E:
                    E.activation(out=invfreq[:], in_=iv_g[:], func=AF.Exp,
                                 scale=-math.log(10000.0) / 64.0))
    ev_rotm = mk.chain(V, [ev_iota, ev_const], [
        lambda E: E.tensor_scalar(scr0[:, 0:1], iota_p[:, 0:1], 1.0, 64.0,
                                  ALU.mult, ALU.add),
        lambda E: E.tensor_single_scalar(scr1[:, 0:128], iota_row[:], scr0[:, 0:1],
                                         ALU.is_equal),
        lambda E: E.tensor_scalar(scr0[:, 0:1], iota_p[:, 0:1], 1.0, -64.0,
                                  ALU.mult, ALU.add),
        lambda E: E.tensor_single_scalar(scr1[:, 128:256], iota_row[:], scr0[:, 0:1],
                                         ALU.is_equal),
        lambda E: E.tensor_sub(rotm[:], scr1[:, 0:128], scr1[:, 128:256]),
    ])

    ev_tab_chain = []
    for i in range(c.SPT // 512):
        sl = slice(i * 512, (i + 1) * 512)
        bank = ps[i % 2]
        dep_bank = ev_tab_chain[-2][0] if len(ev_tab_chain) >= 2 else None
        evm = mk.step(T, [ev_if, ev_pos, dep_bank, ev_rotm],
                      lambda E, sl=sl, bank=bank:
                      E.matmul(bank[:], invfreq[0:1, :], pos_sb[0:1, sl],
                               start=True, stop=True))
        prev_s = ev_tab_chain[-1][1] if ev_tab_chain else None
        ev_last = None
        for which in (1, 0):
            shift = 0.0 if which == 1 else 0.5
            evf = mk.chain(V, [evm, ev_const, prev_s if which == 1 else ev_last], [
                lambda E, bank=bank, shift=shift: E.tensor_scalar(
                    scrH[:, 0:512], bank[:], 1.0 / math.pi, shift, ALU.mult, ALU.add),
                lambda E: E.tensor_copy(scri[:, 0:512], scrH[:, 0:512]),
                lambda E: E.tensor_copy(scr1[:, 0:512], scri[:, 0:512]),
                lambda E: E.tensor_sub(scr0[:, 0:512], scrH[:, 0:512], scr1[:, 0:512]),
                lambda E: E.tensor_single_scalar(scrS[:, 0:512], scr0[:, 0:512], 0.5,
                                                 ALU.is_gt),
                lambda E: E.tensor_single_scalar(scrH[:, 0:512], scr0[:, 0:512], -0.5,
                                                 ALU.is_lt),
                lambda E: E.tensor_sub(scrS[:, 0:512], scrS[:, 0:512], scrH[:, 0:512]),
                lambda E: E.tensor_sub(scr0[:, 0:512], scr0[:, 0:512], scrS[:, 0:512]),
                lambda E: E.tensor_add(scr1[:, 0:512], scr1[:, 0:512], scrS[:, 0:512]),
                lambda E: E.tensor_scalar(scrH[:, 0:512], scr1[:, 0:512], 0.5,
                                          None, ALU.mult),
                lambda E: E.tensor_copy(scri[:, 0:512], scrH[:, 0:512]),
                lambda E: E.tensor_copy(scrH[:, 0:512], scri[:, 0:512]),
                lambda E: E.scalar_tensor_tensor(scrH[:, 0:512], scrH[:, 0:512], -2.0,
                                                 scr1[:, 0:512], ALU.mult, ALU.add),
                lambda E: E.tensor_mul(scrH[:, 0:512], scrH[:, 0:512], scrH[:, 0:512]),
                lambda E: E.tensor_scalar(scrH[:, 0:512], scrH[:, 0:512], -2.0, 1.0,
                                          ALU.mult, ALU.add),
            ])
            evs0 = mk.step(S, [evf], lambda E:
                           E.activation(out=scrS[:, 0:512], in_=scr0[:, 0:512],
                                        func=AF.Sin, scale=math.pi))
            tdst = tabsl(which, sl.start, sl.stop)
            ev_last = mk.step(V, [evs0], lambda E, tdst=tdst:
                              E.tensor_mul(tdst, scrS[:, 0:512], scrH[:, 0:512]))
        ev_tab_chain.append((ev_last, ev_last))
    ev_tab_all = ev_tab_chain[-1][1]

    ev_h = ev_tab_chain[-1][0]
    for h in range(NH):
        lam = c.lam[h]
        e6 = mk.step(V, [ev_h], lambda E:
                     E.tensor_scalar(scr1[:, 0:1], iota_p[:, 0:1], -1.0, 127.0,
                                     ALU.mult, ALU.add))
        e7 = mk.step(S, [e6], lambda E, h=h, lam=lam: E.activation(
            out=kdec_cols[:, h:h + 1], in_=scr1[:, 0:1], func=AF.Exp, scale=-lam))
        ev_h = e7
    p0.close()

    mk.phase()
    a1 = ExitStack()

    def sb1(name, shape, dtype):
        return a1.enter_context(nc.sbuf_tensor(name, shape, dtype))

    nbn = 6 * ((D + 511) // 512)
    kv_sb = sb1("kv_sb", [128, NH * 128], F32)
    x_ring = [sb1(f"x_ring{i}", [128, D], F32) for i in range(2)]
    stats = sb1("stats", [128, 8], F32)
    bnrows = sb1("bnrows", [128, nbn], F32)
    xn_tok = [sb1(f"xn_tok{i}", [128, D], BF16) for i in range(1)]
    xn_st = [sb1(f"xn_st{i}", [128, NDC * 512], BF16) for i in range(2)]
    kraw = sb1("kraw", [128, NKV * 512], BF16)
    krot_scr = sb1("krot_scr", [128, NKV * 512], BF16)
    rsc = sb1("rsc", [128, 1024], BF16)
    kpos_t = [sb1(f"kpos{i}", [128, KW], BF16) for i in range(2)]
    kprime = sb1("kprime", [128, NH * 128], BF16)
    wk_bf = sb1("wk_bf", [128, NDC * KW], BF16)
    wv_bf = sb1("wv_bf", [128, NDC * KW], BF16)
    wstg = [sb1(f"wstg{i}", [128, KW], F32) for i in range(2)]
    snap = [sb1(f"snap{i}", [128, NH * 128], BF16) for i in range(1)]
    v_scr = [sb1(f"vscr{i}", [128, KW], BF16) for i in range(2)]

    ev_kv0 = mk.step(V, [], lambda E: E.memset(kv_sb[:], 0.0))

    wfree = [None, None]
    last_cast = None
    for i, (W_, dst) in enumerate([(WKT, wk_bf), (WVT, wv_bf)]):
        for dc in range(NDC):
            s = (i * NDC + dc) % 2
            ld = mk.dma(Y, [wfree[s]], wstg[s][:], W_[dc * 128:(dc + 1) * 128, :])
            ce = mk.step(V, [ld], lambda E, s=s, dst=dst, dc=dc:
                         E.tensor_copy(dst[:, dc * KW:(dc + 1) * KW], wstg[s][:]))
            wfree[s] = ce
            last_cast = ce
    ev_wkv = last_cast

    x_free = [None] * 2
    xn_free = [None]
    xnst_free = [[], []]
    kpos_free = [None, None]
    vbank_free = [None, None]
    vscr_free = [None, None]
    snap_free = [None]
    ps7_free = None
    ps6_free = None
    rot_bank_free = [None, None]
    kprime_free = None
    rsc_free = None
    ev_kv_upd = ev_kv0
    ev_k_prev_st = None
    stats_free = None
    kraw_free = None
    krot_free = None

    for st in range(c.NST):
        own_st = st >= c.NST - c.NST_OWN
        so = st - (c.NST - c.NST_OWN)
        xs = xn_st[st % 2]
        ev_tp_tiles = []
        for tl in range(4):
            t = st * 4 + tl
            pump.tick(PUMP_PER_TILE)
            xslot = t % 2
            ev_x = mk.dma(Y, [x_free[xslot]], x_ring[xslot][:],
                          X[t * 128:(t + 1) * 128, :])
            ngrp = (D + 511) // 512
            ev_bn = mk.chain(V, [ev_x, stats_free], [
                lambda E, xslot=xslot, ngrp=ngrp: [
                    E.bn_stats(bnrows[:, g * 6:(g + 1) * 6],
                               x_ring[xslot][:, g * 512:min((g + 1) * 512, D)])
                    for g in range(ngrp)],
                lambda E, ngrp=ngrp: E.bn_aggr(stats[:, 0:2], bnrows[:, 0:6 * ngrp]),
            ])
            ev_sq = mk.step(S, [ev_bn], lambda E: E.activation(
                out=stats[:, 5:6], in_=stats[:, 1:2], func=AF.Sqrt,
                bias=eps_col[:, 0:1]))
            ev_nmr = mk.chain(V, [ev_sq], [
                lambda E: E.reciprocal(stats[:, 2:3], stats[:, 5:6]),
                lambda E: E.tensor_scalar(stats[:, 3:4], stats[:, 0:1], -1.0, None, ALU.mult),
                lambda E: E.tensor_mul(stats[:, 4:5], stats[:, 3:4], stats[:, 2:3]),
            ])
            xnslot = 0
            ev_xn = mk.step(S, [ev_nmr, xn_free[xnslot]],
                            lambda E, xslot=xslot, xnslot=xnslot: E.activation(
                out=xn_tok[xnslot][:], in_=x_ring[xslot][:], func=AF.Identity,
                scale=stats[:, 2:3], bias=stats[:, 4:5]))
            x_free[xslot] = ev_xn
            stats_free = ev_xn
            ev_ev2 = None
            for hf in range(2):
                ndc_h = min(8, NDC - hf * 8)
                if ndc_h <= 0:
                    continue
                ev_tp = mk.step(T, [ev_xn, ps7_free] +
                                (xnst_free[st % 2] if (hf == 0 and tl == 0) else []),
                                lambda E, xnslot=xnslot, hf=hf, ndc_h=ndc_h: [
                    E.transpose(ps7[:, j * 128:(j + 1) * 128],
                                xn_tok[xnslot][:, (hf * 8 + j) * 128:(hf * 8 + j + 1) * 128],
                                ident_bf[:])
                    for j in range(ndc_h)])
                eng = V if hf == 0 else S
                if eng == V:
                    ev_ev2 = mk.step(V, [ev_tp],
                                     lambda E, xs=xs, tl=tl, hf=hf, ndc_h=ndc_h:
                                     E.tensor_copy(
                        xs[:].rearrange("p (dc t) -> p dc t", t=512)
                          [:, hf * 8:hf * 8 + ndc_h, tl * 128:(tl + 1) * 128],
                        ps7[:].rearrange("p (dc t) -> p dc t", t=128)[:, 0:ndc_h, :]))
                else:
                    ev_ev2 = mk.step(S, [ev_tp],
                                     lambda E, xs=xs, tl=tl, hf=hf, ndc_h=ndc_h:
                                     E.copy(
                        xs[:].rearrange("p (dc t) -> p dc t", t=512)
                          [:, hf * 8:hf * 8 + ndc_h, tl * 128:(tl + 1) * 128],
                        ps7[:].rearrange("p (dc t) -> p dc t", t=128)[:, 0:ndc_h, :]))
                ps7_free = ev_ev2
            xn_free[xnslot] = ev_ev2
            ev_tp_tiles.append(ev_ev2)
        ev_xn_dma = None
        if own_st:
            ev_xn_dma = mk.dma(Y, ev_tp_tiles,
                               XN_SCR[:].rearrange("p (dc t) -> p dc t", t=c.OWN)
                               [:, :, so * 512:(so + 1) * 512],
                               xs[:].rearrange("p (dc t) -> p dc t", t=512))

        ev_k_oc = []
        for oc in range(NKV):
            bank = ps[oc % 2]
            prev = ev_k_oc[oc - 2] if oc >= 2 else ev_k_prev_st

            def kmm(E, oc=oc, bank=bank, xs=xs):
                out = None
                for dc in range(NDC):
                    out = E.matmul(bank[:],
                                   wk_bf[:, dc * KW + oc * 128: dc * KW + (oc + 1) * 128],
                                   xs[:, dc * 512:(dc + 1) * 512],
                                   start=(dc == 0), stop=(dc == NDC - 1))
                return out
            evm = mk.step(T, ev_tp_tiles + [ev_wkv, prev], kmm)
            eve = mk.step(V, [evm, kraw_free if oc == 0 else None],
                          lambda E, oc=oc, bank=bank:
                          E.tensor_copy(kraw[:, oc * 512:(oc + 1) * 512], bank[:]))
            ev_k_oc.append(eve)
        if own_st:
            kout = k_own[:].rearrange("p (kv t) -> p kv t", t=c.OWN)[:, :, so * 512:(so + 1) * 512]
        else:
            kout = krot_scr[:].rearrange("p (kv t) -> p kv t", t=512)
        ev_krot = None
        for kv in range(NKV):
            cosl = tabsl(0, st * 512, (st + 1) * 512)
            sinl = tabsl(1, st * 512, (st + 1) * 512)
            p7s = ps[4 + kv % 2][:]
            ev_rt = mk.step(T, [ev_k_oc[kv], ev_rotm, rot_bank_free[kv % 2]],
                            lambda E, kv=kv, p7s=p7s:
                            E.matmul(p7s, rotm[:], kraw[:, kv * 512:(kv + 1) * 512],
                                     start=True, stop=True))
            ev_krot = mk.chain(V, [ev_rt, ev_tab_all, rsc_free, krot_free], [
                lambda E, kv=kv, cosl=cosl:
                    E.tensor_tensor(rsc[:, 0:512], kraw[:, kv * 512:(kv + 1) * 512],
                                    cosl, ALU.mult),
                lambda E, p7s=p7s, sinl=sinl:
                    E.tensor_tensor(rsc[:, 512:1024], p7s, sinl, ALU.mult),
                lambda E, kv=kv, kout=kout:
                    E.tensor_add(kout[:, kv], rsc[:, 0:512], rsc[:, 512:1024]),
            ])
            rot_bank_free[kv % 2] = ev_krot
            rsc_free = ev_krot
        kraw_free = ev_krot
        ev_k_prev_st = ev_k_oc[-1]

        ev_last_delta_tile = None
        for tl in range(4):
            t = st * 4 + tl
            vb = tl % 2
            vbank = ps[2 + vb]

            def vmm(E, xs=xs, tl=tl, vbank=vbank):
                out = None
                for dc in range(NDC):
                    out = E.matmul(vbank[:, 0:KW],
                                   xs[:, dc * 512 + tl * 128: dc * 512 + (tl + 1) * 128],
                                   wv_bf[:, dc * KW:(dc + 1) * KW],
                                   start=(dc == 0), stop=(dc == NDC - 1))
                return out
            evv = mk.step(T, [ev_tp_tiles[tl], ev_wkv, vbank_free[vb]], vmm)
            ev_last_vmm = evv
            if own_st:
                vcur = v_own[:, (so * 4 + tl) * KW:(so * 4 + tl + 1) * KW]
                evve = mk.step(V, [evv], lambda E, vcur=vcur, vbank=vbank:
                               E.tensor_copy(vcur, vbank[:, 0:KW]))
            else:
                vs_ = t % 2
                vcur = v_scr[vs_][:, 0:KW]
                evve = mk.step(V, [evv, vscr_free[vs_]],
                               lambda E, vcur=vcur, vbank=vbank:
                               E.tensor_copy(vcur, vbank[:, 0:KW]))
            vbank_free[vb] = evve

            kslot = t % 2
            ev_kt = mk.step(T, [ev_krot, kpos_free[kslot], ps7_free],
                            lambda E, kout=kout, tl=tl: [
                E.transpose(ps7[:, j * 128:(j + 1) * 128],
                            kout[:, j, tl * 128:(tl + 1) * 128], ident_bf[:])
                for j in range(NKV)])
            ev_kte = mk.step(S, [ev_kt], lambda E, kslot=kslot:
                             E.copy(kpos_t[kslot][:, 0:KW], ps7[:, 0:KW]))
            ps7_free = ev_kte
            ev_kp = mk.step(V, [ev_kte, kprime_free], lambda E, kslot=kslot: [
                E.tensor_single_scalar(kprime[:, h * 128:(h + 1) * 128],
                                       kpos_t[kslot][:, (h // 4) * 128:(h // 4 + 1) * 128],
                                       kdec_cols[:, h:h + 1], ALU.mult)
                for h in range(NH)])
            kpos_free[kslot] = ev_kp
            last_kt = ev_kt

            ev_sn = None
            if own_st:
                co = so * 4 + tl
                sslot = 0
                ev_sn = mk.step(V, [ev_kv_upd, snap_free[sslot]],
                                lambda E, sslot=sslot:
                                E.tensor_copy(snap[sslot][:], kv_sb[:]))
                ev_snd = mk.dma(Y, [ev_sn], KVSNAP[co], snap[sslot][:])
                snap_free[sslot] = ev_snd

            for g in range(NG):
                def dmm(E, g=g, vcur=vcur):
                    out = None
                    for hp in range(4):
                        out = E.matmul(ps[6][:, hp * 128:(hp + 1) * 128],
                                       kprime[:, (g * 4 + hp) * 128:(g * 4 + hp + 1) * 128],
                                       vcur[:, g * 128:(g + 1) * 128],
                                       start=True, stop=True)
                    return out
                evd = mk.step(T, [ev_kp, evve, ps6_free], dmm)
                ev_up = mk.step(V, [evd] + ([ev_sn] if ev_sn else []),
                                lambda E, g=g: [
                    E.scalar_tensor_tensor(
                        kv_sb[:, (g * 4 + hp) * 128:(g * 4 + hp + 1) * 128],
                        kv_sb[:, (g * 4 + hp) * 128:(g * 4 + hp + 1) * 128],
                        c.blk[g * 4 + hp],
                        ps[6][:, hp * 128:(hp + 1) * 128],
                        ALU.mult, ALU.add)
                    for hp in range(4)])
                ps6_free = ev_up
                ev_kv_upd = ev_up
                ev_last_delta_tile = ev_up
            kprime_free = ev_last_delta_tile
            if not own_st:
                vscr_free[t % 2] = ev_last_delta_tile
        xnst_free[st % 2] = [ev_last_vmm] + ([ev_xn_dma] if ev_xn_dma else [])
        krot_free = last_kt

    a1.close()
    lb_scope.close()

    mk.phase()
    a2 = ExitStack()

    def sb2(name, shape, dtype):
        return a2.enter_context(nc.sbuf_tensor(name, shape, dtype))

    mask_pack = sb2("mask_pack", [128, NG * 512], BF16)
    qdec_pack = sb2("qdec_pack", [128, NG * 512], BF16)

    mp0 = ExitStack()
    rampb = mp0.enter_context(nc.sbuf_tensor("rampb2", [128, 128], F32))
    mscr = mp0.enter_context(nc.sbuf_tensor("mscr", [128, 512], F32))
    ev_rb = mk.step(T, [], lambda E:
                    E.matmul(ps[2][:, 0:128], ones_row[0:1, :], iota_row[0:1, :],
                             start=True, stop=True))
    ev_rbs = mk.step(V, [ev_rb], lambda E:
                     E.tensor_copy(rampb[:], ps[2][:, 0:128]))
    ev_diff = mk.step(V, [ev_rbs], lambda E:
                      E.tensor_single_scalar(mscr[:, 0:128], rampb[:], iota_p[:, 0:1],
                                             ALU.subtract))
    ev_h2 = ev_diff
    for h in range(NH):
        lam = c.lam[h]
        g, hp = h // 4, h % 4
        msl = slice(g * 512 + hp * 128, g * 512 + (hp + 1) * 128)
        e1 = mk.step(V, [ev_h2], lambda E: [
            E.tensor_single_scalar(mscr[:, 128:256], mscr[:, 0:128], 0.0, ALU.is_ge),
            E.tensor_single_scalar(mscr[:, 256:384], mscr[:, 0:128], 0.0, ALU.max),
        ])
        e2 = mk.step(S, [e1], lambda E, lam=lam: E.activation(
            out=mscr[:, 384:512], in_=mscr[:, 256:384], func=AF.Exp, scale=-lam))
        e3 = mk.step(V, [e2], lambda E, msl=msl:
                     E.tensor_mul(mask_pack[:, msl], mscr[:, 384:512], mscr[:, 128:256]))
        e4 = mk.step(S, [ev_rbs, e3], lambda E, lam=lam: E.activation(
            out=mscr[:, 384:512], in_=rampb[:], func=AF.Exp, scale=-lam))
        e5 = mk.step(V, [e4], lambda E, msl=msl, lam=lam:
                     E.tensor_scalar(qdec_pack[:, msl], mscr[:, 384:512],
                                     math.exp(-lam), None, ALU.mult))
        ev_h2 = e5
    mp0.close()

    wq_bf = sb2("wq_bf", [128, NDC * QW], BF16)
    wstg2 = [sb2(f"wstg2_{i}", [128, QW // 2], F32) for i in range(2)]
    xn_q = [sb2(f"xn_q{i}", [128, NDC * 512], BF16) for i in range(1)]
    q_st = sb2("q_st", [128, NH * 512], BF16)
    q_rot = sb2("q_rot", [128, NH * 512], BF16)
    rsc2 = sb2("rsc2", [128, 1024], BF16)
    qpk = [sb2(f"qpk{i}", [128, 512], BF16) for i in range(2)]
    scm = [sb2(f"scm{i}", [128, 512], BF16) for i in range(2)]
    kvsn = [sb2(f"kvsn{i}", [128, NH * 128], BF16) for i in range(2)]
    o_stage = [sb2(f"ostage{i}", [128, NH * 128], BF16) for i in range(1)]

    wfree2 = [None, None]
    last2 = None
    for dc2 in range(2 * NDC):
        dc, hh = dc2 // 2, dc2 % 2
        s = dc2 % 2
        HW = QW // 2
        ld = mk.dma(Y, [wfree2[s]], wstg2[s][:],
                    WQT[dc * 128:(dc + 1) * 128, hh * HW:(hh + 1) * HW])
        ce = mk.step(V, [ld], lambda E, s=s, dc=dc, hh=hh, HW=HW:
                     E.tensor_copy(wq_bf[:, dc * QW + hh * HW: dc * QW + (hh + 1) * HW],
                                   wstg2[s][:]))
        wfree2[s] = ce
        last2 = ce
    ev_wq = last2

    xnq_free = [None]
    kvsn_free = [None, None]
    ost_free = [None]
    qpk_free = [None, None]
    scm_free = [None, None]
    opack_free = [None, None]
    ev_q_last = None
    ev_qrot_free = None
    ps6a_free = None
    rot2_free = [None, None]

    for so in range(c.NST_OWN):
        xq = xn_q[0]
        ev_xq = mk.dma(Y, [xnq_free[0]],
                       xq[:].rearrange("p (dc t) -> p dc t", t=512),
                       XN_SCR[:].rearrange("p (dc t) -> p dc t", t=c.OWN)
                       [:, :, so * 512:(so + 1) * 512])
        ev_q = []
        for oc in range(NH):
            bank = ps[oc % 2]
            prev = ev_q[oc - 2][1] if oc >= 2 else ev_q_last

            def qmm(E, oc=oc, bank=bank, xq=xq):
                out = None
                for dc in range(NDC):
                    out = E.matmul(
                        bank[:],
                        wq_bf[:, dc * QW + oc * 128: dc * QW + (oc + 1) * 128],
                        xq[:, dc * 512:(dc + 1) * 512],
                        start=(dc == 0), stop=(dc == NDC - 1))
                return out
            evm = mk.step(T, [ev_xq, ev_wq, prev], qmm)
            eve = mk.step(S, [evm, ev_qrot_free if oc == 0 else None],
                          lambda E, oc=oc, bank=bank:
                          E.copy(q_st[:, oc * 512:(oc + 1) * 512], bank[:]))
            ev_q.append((evm, eve))
        xnq_free[0] = ev_q[-1][0]
        ev_q_last = ev_q[-1][1]

        cosl = cos_own[:, so * 512:(so + 1) * 512]
        sinl = sin_own[:, so * 512:(so + 1) * 512]
        ev_qrot = None
        for h in range(NH):
            p7s = ps[4 + h % 2][:]
            ev_rt = mk.step(T, [ev_q[h][1], ev_rotm, rot2_free[h % 2]],
                            lambda E, h=h, p7s=p7s:
                            E.matmul(p7s, rotm[:], q_st[:, h * 512:(h + 1) * 512],
                                     start=True, stop=True))
            ev_qrot = mk.chain(V, [ev_rt, ev_qrot, ev_qrot_free if h < 2 else None], [
                lambda E, h=h, cosl=cosl:
                    E.tensor_tensor(rsc2[:, 0:512], q_st[:, h * 512:(h + 1) * 512],
                                    cosl, ALU.mult),
                lambda E, p7s=p7s, sinl=sinl:
                    E.tensor_tensor(rsc2[:, 512:1024], p7s, sinl, ALU.mult),
                lambda E, h=h:
                    E.tensor_add(q_rot[:, h * 512:(h + 1) * 512],
                                 rsc2[:, 0:512], rsc2[:, 512:1024]),
            ])
            rot2_free[h % 2] = ev_qrot
        ev_qrot_free = None

        last_mix = None
        for tl in range(4):
            co = so * 4 + tl
            kslot = co % 2
            ev_ksd = mk.dma(Y, [kvsn_free[kslot]], kvsn[kslot][:], KVSNAP[co])
            oslot = 0
            ev_o_evs = []
            for g in range(NG):
                qslot = g % 2
                ev_qp = mk.step(V, [ev_qrot, qpk_free[qslot]],
                                lambda E, qslot=qslot, g=g, tl=tl:
                                E.tensor_tensor(
                    qpk[qslot][:].rearrange("p (a t) -> p a t", t=128),
                    q_rot[:].rearrange("p (h t) -> p h t", t=512)
                    [:, 4 * g:4 * g + 4, tl * 128:(tl + 1) * 128],
                    qdec_pack[:, g * 512:(g + 1) * 512].rearrange("p (a t) -> p a t", t=128),
                    ALU.mult))
                ev_sc = mk.step(T, [ev_qrot, ps6a_free],
                                lambda E, g=g, co=co, tl=tl:
                                E.matmul(ps[6][:],
                                         k_own[:].rearrange("p (kv t) -> p kv t", t=c.OWN)
                                         [:, g, co * 128:(co + 1) * 128],
                                         q_rot[:].rearrange("p (h t) -> p h t", t=512)
                                         [:, 4 * g:4 * g + 4, tl * 128:(tl + 1) * 128],
                                         start=True, stop=True))
                sslot = g % 2
                ev_ms = mk.step(V, [ev_sc, scm_free[sslot]],
                                lambda E, sslot=sslot, g=g:
                                E.tensor_mul(scm[sslot][:], ps[6][:],
                                             mask_pack[:, g * 512:(g + 1) * 512]))
                ps6a_free = ev_ms
                obank = ps[2 + g % 2]

                def omm(E, g=g, co=co, sslot=sslot, qslot=qslot, kslot=kslot, obank=obank):
                    E.matmul(obank[:],
                             v_own[:, co * KW + g * 128: co * KW + (g + 1) * 128],
                             scm[sslot][:], start=True, stop=False)
                    out = None
                    for hp in range(4):
                        h = 4 * g + hp
                        out = E.matmul(obank[:, hp * 128:(hp + 1) * 128],
                                       kvsn[kslot][:, h * 128:(h + 1) * 128],
                                       qpk[qslot][:, hp * 128:(hp + 1) * 128],
                                       start=False, stop=(hp == 3))
                    return out
                ev_om = mk.step(T, [ev_ms, ev_qp, ev_ksd, opack_free[g % 2]], omm)
                scm_free[sslot] = ev_om
                qpk_free[qslot] = ev_om
                ev_oe = mk.step(S, [ev_om] + ([ost_free[oslot]] if g == 0 else []),
                                lambda E, g=g, oslot=oslot, obank=obank:
                                E.copy(o_stage[oslot][:, g * 512:(g + 1) * 512],
                                       obank[:]))
                opack_free[g % 2] = ev_oe
                ev_o_evs.append(ev_oe)
                last_mix = ev_om
            kvsn_free[kslot] = ev_o_evs[-1]
            ev_od = mk.dma(Y, ev_o_evs, OFEAT[co], o_stage[oslot][:])
            ost_free[oslot] = ev_od
        ev_qrot_free = last_mix
    a2.close()
    a_scope.close()

    mk.phase()
    a2b = ExitStack()

    def sb3(name, shape, dtype):
        return a2b.enter_context(nc.sbuf_tensor(name, shape, dtype))

    wo_bf = sb3("wo_bf", [128, NH * D], BF16)
    wstg3 = [sb3(f"wstg3_{i}", [128, D], F32) for i in range(2)]
    o_buf = [sb3(f"obuf{i}", [128, 4 * NH * 128], BF16) for i in range(2)]
    att_f = sb3("att_f", [128, 512], F32)
    att_stage = [sb3(f"attst{i}", [128, D], F32) for i in range(4)]

    wfree3 = [None, None]
    last3 = None
    for ec in range(NH):
        s = ec % 2
        ld = mk.dma(Y, [wfree3[s]], wstg3[s][:], WOT[ec * 128:(ec + 1) * 128, :])
        ce = mk.step(V, [ld], lambda E, s=s, ec=ec:
                     E.tensor_copy(wo_bf[:, ec * D:(ec + 1) * D], wstg3[s][:]))
        wfree3[s] = ce
        last3 = ce
    ev_wo = last3

    obuf_free = [None, None]
    attst_free = [None] * 4
    attf_free = None
    pstp_free = None
    abank_free = [None, None]
    for so in range(c.NST_OWN):
        oslot = so % 2
        ev_ob = mk.dma(Y, [obuf_free[oslot]],
                       o_buf[oslot][:].rearrange("p (cc f) -> p cc f", f=NH * 128),
                       OFEAT[so * 4:(so + 1) * 4].transpose([1, 0, 2]))
        last_am = None
        for oc in range(NDC):
            bank = ps[oc % 2]

            def amm(E, oc=oc, oslot=oslot, bank=bank):
                out = None
                for ec in range(NH):
                    out = E.matmul(bank[:],
                                   wo_bf[:, ec * D + oc * 128: ec * D + (oc + 1) * 128],
                                   o_buf[oslot][:].rearrange("p (cc f) -> p cc f", f=NH * 128)
                                   [:, :, ec * 128:(ec + 1) * 128],
                                   start=(ec == 0), stop=(ec == NH - 1))
                return out
            ev_am = mk.step(T, [ev_ob, ev_wo, abank_free[oc % 2]], amm)
            last_am = ev_am
            ev_ae = mk.step(V, [ev_am, attf_free], lambda E, bank=bank:
                            E.tensor_copy(att_f[:], bank[:]))
            abank_free[oc % 2] = ev_ae
            ev_tp = mk.step(T, [ev_ae, pstp_free], lambda E: [
                E.transpose(ps[4][:, j * 128:(j + 1) * 128],
                            att_f[:, j * 128:(j + 1) * 128], ident_f32[:])
                for j in range(4)])
            attf_free = ev_tp
            ev_te = mk.step(S, [ev_tp] + (attst_free if oc == 0 else []),
                            lambda E, oc=oc: [
                E.copy(att_stage[j][:, oc * 128:(oc + 1) * 128],
                       ps[4][:, j * 128:(j + 1) * 128])
                for j in range(4)])
            pstp_free = ev_te
        for j in range(4):
            t = so * 4 + j
            evd = mk.dma(Y, [pstp_free], ATTN[t * 128:(t + 1) * 128, :],
                         att_stage[j][:])
            attst_free[j] = evd
        obuf_free[oslot] = last_am
    a2b.close()

    TOPK = c.TOPK
    yacc_scope = ExitStack()
    y_acc = yacc_scope.enter_context(nc.sbuf_tensor("y_acc", [128, NTT * D], F32))
    xnh_t = yacc_scope.enter_context(nc.sbuf_tensor("xnh", [128, NDC * HALF], BF16))

    ev_out_final = []

    def build_half(half):
        mk.phase()
        b = ExitStack()

        def sbb(name, shape, dtype):
            return b.enter_context(nc.sbuf_tensor(name + str(half), shape, dtype))

        combf = sbb("combf", [max(NEXP, 2), HALF], BF16)
        selm = sbb("selm", [128, 128], BF16)
        gsc_scope = ExitStack()

        def sbg(name, shape, dtype):
            return gsc_scope.enter_context(nc.sbuf_tensor(name + str(half), shape, dtype))

        gstg = sbg("gstg", [128, NDC * NEXP], F32)
        gate_bf = sbg("gate_bf", [128, NDC * NEXP], BF16)
        sgt = sbg("sgt", [128, NEXP], F32)
        gsc = sbg("gsc", [128, NEXP], F32)
        gm = sbg("gm", [128, 4], F32)
        comb_t = sbg("comb_t", [128, NEXP], F32)

        ev_xnh = mk.dma(Y, [],
                        xnh_t[:].rearrange("p (dc t) -> p dc t", t=HALF),
                        XN_SCR[:].rearrange("p (dc t) -> p dc t", t=c.OWN)
                        [:, :, half * HALF:(half + 1) * HALF])
        ev_gld = mk.dma(Y, [], gstg[:], GATE)
        ev_gc = mk.step(V, [ev_gld], lambda E: E.tensor_copy(gate_bf[:], gstg[:]))

        ps6b_free = None
        ps5_free = None
        ev_tk_prev = None
        ev_tk_prev2 = None
        ev_comb_all = None
        for tt in range(NTT):
            def gmm(E, tt=tt):
                out = None
                for dc in range(NDC):
                    out = E.matmul(ps[6][:, 0:NEXP],
                                   xnh_t[:, dc * HALF + tt * 128: dc * HALF + (tt + 1) * 128],
                                   gate_bf[:, dc * NEXP:(dc + 1) * NEXP],
                                   start=(dc == 0), stop=(dc == NDC - 1))
                return out
            ev_gm = mk.step(T, [ev_xnh, ev_gc, ps6b_free], gmm)
            ev_sig = mk.step(S, [ev_gm, ev_tk_prev], lambda E: E.activation(
                out=sgt[:], in_=ps[6][:, 0:NEXP], func=AF.Sigmoid))
            ps6b_free = ev_sig

            emits = [lambda E: E.tensor_copy(gsc[:], sgt[:])]
            for r in range(TOPK):
                emits.append(lambda E: E.reduce_max(gm[:, 0:1], gsc[:], axis=AX.X))
                if r < TOPK - 1:
                    emits.append(lambda E: E.tensor_single_scalar(
                        comb_t[:], gsc[:], gm[:, 0:1], ALU.is_ge))
                    emits.append(lambda E: E.scalar_tensor_tensor(
                        gsc[:], comb_t[:], -1e30, gsc[:], ALU.mult, ALU.add))
            emits += [
                lambda E: E.tensor_single_scalar(comb_t[:], sgt[:], gm[:, 0:1], ALU.is_ge),
                lambda E: E.tensor_mul(comb_t[:], comb_t[:], sgt[:]),
                lambda E: E.reduce_sum(gm[:, 1:2], comb_t[:], axis=AX.X),
                lambda E: E.tensor_scalar(gm[:, 2:3], gm[:, 1:2], 1e-20, None, ALU.add),
                lambda E: E.reciprocal(gm[:, 3:4], gm[:, 2:3]),
                lambda E: E.tensor_single_scalar(comb_t[:], comb_t[:], gm[:, 3:4], ALU.mult),
            ]
            ev_tk = mk.chain(V, [ev_sig, ev_tk_prev2], emits)
            ev_tk_prev = ev_tk
            ev_tk_prev2 = ev_ct if False else None
            ev_ct = mk.step(T, [ev_tk, ps5_free],
                            lambda E: E.transpose(ps[5][0:NEXP, 0:128], comb_t[:],
                                                  ident_f32[:]))
            ev_tk_prev2 = ev_ct
            ev_ce = mk.step(V, [ev_ct], lambda E, tt=tt:
                            E.tensor_copy(combf[0:NEXP, tt * 128:(tt + 1) * 128],
                                          ps[5][0:NEXP, 0:128]))
            ps5_free = ev_ce
            ev_comb_all = ev_ce

        if half == 0:
            pass
        mk.fence_all()
        gsc_scope.close()
        wgu = [sbb(f"wgu{i}", [128, NDC * FEXP], BF16) for i in range(2)]
        wdsl = [sbb(f"wdsl{i}", [128, NFC * 512], BF16) for i in range(2)]
        actp = [sbb(f"actp{i}", [128, NFC * HALF], BF16) for i in range(1)]
        sgb = sbb("sgb", [128, NTG * 512], BF16)
        ub = sbb("ub", [128, 512], BF16)
        cbc = [sbb(f"cbc{i}", [128, HALF], BF16) for i in range(1)]
        ev_sel = mk.step(V, [], lambda E: E.memset(selm[:], 0.0))
        sel_cur = [None]
        ev_selmm = [None]

        wgu_free = [None, None]
        wd_free = [None] * 2
        cbc_free = [None]
        actp_free = [None]
        gu_bank_free = [None] * 4
        ybank_free = [None, None]
        ybank_free[1] = ps5_free
        ev_y_evac = {}

        for e in range(NEXP):
            if half == 0:
                pump.tick((len(pump.jobs) - pump.i + (NEXP - e) - 1)
                          // max(1, (NEXP - e)))
            first_pass = (e == 0)
            nq = max(1, NDC // 4)
            ev_wg, ev_wu = [], []
            for ix, qq in enumerate(range(0, NDC, nq)):
                ev_wg.append(mk.dma(Y if ix % 2 == 0 else S,
                                    [wgu_free[0], pump.done[("wg", e)]],
                                    wgu[0][:].rearrange("p (dc f) -> p dc f", f=FEXP)
                                    [:, qq:qq + nq, :],
                                    WG_BF[e, qq:qq + nq].transpose([1, 0, 2])))
            for ix, qq in enumerate(range(0, NDC, nq)):
                ev_wu.append(mk.dma(Y if ix % 2 == 1 else S,
                                    [wgu_free[1], pump.done[("wu", e)]],
                                    wgu[1][:].rearrange("p (dc f) -> p dc f", f=FEXP)
                                    [:, qq:qq + nq, :],
                                    WU_BF[e, qq:qq + nq].transpose([1, 0, 2])))
            ev_selrow = mk.step(V, [ev_sel, ev_selmm[0]],
                                lambda E, e=e: E.tensor_scalar(
                selm[:], iota_p[:, 0:1].to_broadcast([128, 128]),
                float(e), None, ALU.is_equal))
            last_gmm = None
            for fc in range(NFC):
                for tg in range(NTG):
                    bk = (NTG * fc + tg) % 4
                    bank = ps[bk]

                    def gmm(E, fc=fc, tg=tg, bank=bank):
                        out = None
                        for dc in range(NDC):
                            out = E.matmul(
                                bank[:],
                                wgu[0][:, dc * FEXP + fc * 128: dc * FEXP + (fc + 1) * 128],
                                xnh_t[:, dc * HALF + tg * 512: dc * HALF + (tg + 1) * 512],
                                start=(dc == 0), stop=(dc == NDC - 1))
                        return out
                    evb = mk.step(T, ev_wg + [ev_xnh, gu_bank_free[bk]], gmm)
                    last_gmm = evb
                    ev_si = mk.step(S, [evb] +
                                    ([actp_free[0]] if (fc == 0 and tg == 0) else []),
                                    lambda E, fc=fc, tg=tg, bank=bank:
                                    E.activation(
                        out=actp[0][:, fc * HALF + tg * 512: fc * HALF + (tg + 1) * 512],
                        in_=bank[:], func=AF.Silu))
                    gu_bank_free[bk] = ev_si
            wgu_free[0] = last_gmm
            ev_cb = None
            for tg in range(NTG):
                evb = mk.step(T, [ev_comb_all, ps6b_free, ev_selrow] +
                              ([cbc_free[0]] if tg == 0 else []),
                              lambda E, tg=tg:
                              E.matmul(ps[6][:], selm[0:NEXP, :],
                                       combf[0:NEXP, tg * 512:(tg + 1) * 512],
                                       start=True, stop=True))
                ev_cb = mk.step(V, [evb], lambda E, tg=tg:
                                E.tensor_copy(cbc[0][:, tg * 512:(tg + 1) * 512],
                                              ps[6][:]))
                ps6b_free = ev_cb
            ev_selmm[0] = evb
            last_uv = None
            for fc in range(NFC):
                for tg in range(NTG):
                    bk = (NTG * fc + tg) % 4
                    bank = ps[bk]

                    def umm(E, fc=fc, tg=tg, bank=bank):
                        out = None
                        for dc in range(NDC):
                            out = E.matmul(
                                bank[:],
                                wgu[1][:, dc * FEXP + fc * 128: dc * FEXP + (fc + 1) * 128],
                                xnh_t[:, dc * HALF + tg * 512: dc * HALF + (tg + 1) * 512],
                                start=(dc == 0), stop=(dc == NDC - 1))
                        return out
                    evb = mk.step(T, ev_wu + [ev_xnh, gu_bank_free[bk]], umm)
                    asl = actp[0][:, fc * HALF + tg * 512: fc * HALF + (tg + 1) * 512]
                    ev_uc = mk.chain(V, [evb, ev_cb, gu_bank_free[bk], last_uv], [
                        lambda E, tg=tg, bank=bank:
                            E.tensor_mul(ub[:, 0:512], bank[:],
                                         cbc[0][:, tg * 512:(tg + 1) * 512]),
                        lambda E, asl=asl:
                            E.tensor_mul(asl, asl, ub[:, 0:512]),
                    ])
                    gu_bank_free[bk] = ev_uc
                    last_uv = ev_uc
            wgu_free[1] = evb
            ev_act = last_uv
            cbc_free[0] = last_uv
            lastd = None
            for dg in range(NDG):
                slot = dg % 2
                wd_free[slot] = mk.dma(
                    Y if dg % 2 == 0 else S, [wd_free[slot], pump.done[("wd", e)]],
                    wdsl[slot][:].rearrange("p (fc f) -> p fc f", f=512),
                    WD_BF[e, :, :, dg * 512:(dg + 1) * 512].transpose([1, 0, 2]))
                for tt in range(NTT):
                    ybank = ps[4 + tt % 2]

                    def ymm(E, dg=dg, tt=tt, ybank=ybank, slot=slot):
                        out = None
                        for fc in range(NFC):
                            out = E.matmul(
                                ybank[:],
                                actp[0][:, fc * HALF + tt * 128: fc * HALF + (tt + 1) * 128],
                                wdsl[slot][:, fc * 512:(fc + 1) * 512],
                                start=(fc == 0), stop=(fc == NFC - 1))
                        return out
                    ev_ym = mk.step(T, [ev_act, wd_free[slot], ybank_free[tt % 2]], ymm)
                    ysl = y_acc[:, tt * D + dg * 512: tt * D + (dg + 1) * 512]
                    if first_pass:
                        ev_ye = mk.step(V, [ev_ym], lambda E, ysl=ysl, ybank=ybank:
                                        E.tensor_copy(ysl, ybank[:]))
                    else:
                        ev_ye = mk.step(V, [ev_ym, ev_y_evac.get((tt, dg))],
                                        lambda E, ysl=ysl, ybank=ybank:
                                        E.tensor_add(ysl, ysl, ybank[:]))
                    ev_y_evac[(tt, dg)] = ev_ye
                    ybank_free[tt % 2] = ev_ye
                    lastd = ev_ym
                wd_free[slot] = lastd
            actp_free[0] = lastd
        pump.finish()
        mk.fence_all()
        b.close()

        bs = ExitStack()

        def sbs(name, shape, dtype):
            return bs.enter_context(nc.sbuf_tensor(name + str(half), shape, dtype))

        NSH = c.NSC // 2
        FSH2 = NSH * 128
        sgh = sbs("sgh", [128, NDC * FSH2], BF16)
        suh = sbs("suh", [128, NDC * FSH2], BF16)
        act_sh = sbs("act_sh", [128, c.NSC * HALF], BF16)
        sdsl = [sbs(f"sdsl{i}", [128, c.NSC * 512], BF16) for i in range(2)]

        sh_bank_free = [None] * 4
        shscr_free = None
        ev_shact = None
        sgh_free = None
        for sh in range(2):
            nq = max(1, NDC // 4)
            ev_sg, ev_su = [], []
            for qq in range(0, NDC, nq):
                ev_sg.append(mk.dma(Y, [sgh_free, pump.done[("sg", sh)]],
                                    sgh[:].rearrange("p (dc f) -> p dc f", f=FSH2)
                                    [:, qq:qq + nq, :],
                                    SG_BF[sh, qq:qq + nq].transpose([1, 0, 2])))
            for qq in range(0, NDC, nq):
                ev_su.append(mk.dma(Y, [sgh_free, pump.done[("su", sh)]],
                                    suh[:].rearrange("p (dc f) -> p dc f", f=FSH2)
                                    [:, qq:qq + nq, :],
                                    SU_BF[sh, qq:qq + nq].transpose([1, 0, 2])))
            last_h_mm = None
            for fc in range(NSH):
                sc = sh * NSH + fc
                bank_evs = []
                for mi in range(2):
                    wsb = sgh if mi == 0 else suh
                    wdep = ev_sg if mi == 0 else ev_su
                    for tg in range(NTG):
                        bank = ps[2 * mi + (tg % 2)]

                        def hmm(E, wsb=wsb, fc=fc, tg=tg, bank=bank):
                            out = None
                            for dc in range(NDC):
                                out = E.matmul(
                                    bank[:],
                                    wsb[:, dc * FSH2 + fc * 128: dc * FSH2 + (fc + 1) * 128],
                                    xnh_t[:, dc * HALF + tg * 512: dc * HALF + (tg + 1) * 512],
                                    start=(dc == 0), stop=(dc == NDC - 1))
                            return out
                        evb = mk.step(T, wdep + [ev_xnh,
                                                  sh_bank_free[2 * mi + (tg % 2)]], hmm)
                        bank_evs.append(evb)
                        last_h_mm = evb
                ev_si = mk.step(S, bank_evs[0:NTG] +
                                ([shscr_free] if shscr_free else []),
                                lambda E, sc=sc: [
                    E.activation(out=act_sh[:, sc * HALF + tg * 512: sc * HALF + (tg + 1) * 512],
                                 in_=ps[tg % 2][:], func=AF.Silu)
                    for tg in range(NTG)])
                ev_ac = mk.step(V, bank_evs[NTG:2 * NTG] + [ev_si],
                                lambda E, sc=sc: [
                    E.tensor_mul(act_sh[:, sc * HALF + tg * 512: sc * HALF + (tg + 1) * 512],
                                 act_sh[:, sc * HALF + tg * 512: sc * HALF + (tg + 1) * 512],
                                 ps[2 + (tg % 2)][:])
                    for tg in range(NTG)])
                for z in range(2):
                    sh_bank_free[z] = ev_si
                    sh_bank_free[2 + z] = ev_ac
                shscr_free = ev_ac
                ev_shact = ev_ac
            sgh_free = last_h_mm
        sd_free = [None, None]
        for dg in range(NDG):
            slot = dg % 2
            nq = max(1, c.NSC // 3)
            ev_sdw = []
            for qq in range(0, c.NSC, nq):
                ev_sdw.append(mk.dma(Y, [sd_free[slot], pump.done[("sd", 0)]],
                                     sdsl[slot][:].rearrange("p (fc f) -> p fc f", f=512)
                                     [:, qq:qq + nq, :],
                                     SD_BF[qq:qq + nq, :, dg * 512:(dg + 1) * 512]
                                     .transpose([1, 0, 2])))
            lastsd = None
            for tt in range(NTT):
                ybank = ps[4 + tt % 2]

                def smm(E, dg=dg, tt=tt, ybank=ybank, slot=slot):
                    out = None
                    for fc in range(c.NSC):
                        out = E.matmul(
                            ybank[:],
                            act_sh[:, fc * HALF + tt * 128: fc * HALF + (tt + 1) * 128],
                            sdsl[slot][:, fc * 512:(fc + 1) * 512],
                            start=(fc == 0), stop=(fc == c.NSC - 1))
                    return out
                ev_ym = mk.step(T, [ev_shact] + ev_sdw + [ybank_free[tt % 2]], smm)
                ysl = y_acc[:, tt * D + dg * 512: tt * D + (dg + 1) * 512]
                ev_ye = mk.step(V, [ev_ym, ev_y_evac.get((tt, dg))],
                                lambda E, ysl=ysl, ybank=ybank:
                                E.tensor_add(ysl, ysl, ybank[:]))
                ev_y_evac[(tt, dg)] = ev_ye
                ybank_free[tt % 2] = ev_ye
                lastsd = ev_ym
            sd_free[slot] = lastsd
        mk.fence_all()
        bs.close()

        cc = ExitStack()

        def sbc(name, shape, dtype):
            return cc.enter_context(nc.sbuf_tensor(name + str(half), shape, dtype))

        x_t = [sbc(f"cx{i}", [128, D], F32) for i in range(2)]
        at_t = [sbc(f"cat{i}", [128, D], F32) for i in range(2)]
        o_t = [sbc(f"cot{i}", [128, D], F32) for i in range(2)]
        xt_free = [None, None]
        at_free = [None, None]
        ot_free = [None, None]
        for tt in range(NTT):
            sl = tt % 2
            row0 = half * HALF + tt * 128
            ev_x = mk.dma(Y, [xt_free[sl]], x_t[sl][:],
                          X[c.LB + row0: c.LB + row0 + 128, :])
            ev_a = mk.dma(Y, [at_free[sl]], at_t[sl][:], ATTN[row0:row0 + 128, :])
            deps = [ev_x, ev_a, ot_free[sl]] + \
                   [ev_y_evac[(tt, dg)] for dg in range(NDG)]
            ev_s = mk.chain(V, deps, [
                lambda E, sl=sl: E.tensor_add(o_t[sl][:], x_t[sl][:], at_t[sl][:]),
                lambda E, sl=sl, tt=tt: E.tensor_add(o_t[sl][:], o_t[sl][:],
                                                     y_acc[:, tt * D:(tt + 1) * D]),
            ])
            xt_free[sl] = ev_s
            at_free[sl] = ev_s
            ev_od = mk.dma(Y, [ev_s], OUT[row0:row0 + 128, :], o_t[sl][:])
            ot_free[sl] = ev_od
            ev_out_final.append(ev_od)
        cc.close()

    for _h in range(2):
        build_half(_h)

    yacc_scope.close()

    for k, eng in mk.dma_eng.items():
        mk._wait(eng, (k, mk.cnt[k]))
    mk.step(Y, ev_out_final, lambda E: E.nop())
    mk.build_blocks()
    return nc


def shard_inputs(cfg, x, wqT, wkT, wvT, woT, gate_r, wg, wu, wd, sg, su, sd):
    in_maps = []
    ncores = cfg.B * cfg.NQ
    for core in range(ncores):
        bidx, q = core // cfg.NQ, core % cfg.NQ
        xp = np.zeros((cfg.SPT, cfg.D), np.float32)
        xp[cfg.LB - q * cfg.OWN:] = x[bidx, :(q + 1) * cfg.OWN]
        pos = np.zeros((1, cfg.SPT), np.float32)
        pos[0, cfg.LB - q * cfg.OWN:] = np.arange((q + 1) * cfg.OWN, dtype=np.float32)
        in_maps.append({
            "x": xp, "pos": pos, "wqT": wqT, "wkT": wkT, "wvT": wvT, "woT": woT,
            "gate_r": gate_r, "wg": wg, "wu": wu, "wd": wd,
            "sg": sg, "su": su, "sd": sd,
        })
    return in_maps


def prep_weights(cfg, wq, wk, wv, wo, gate_w, wg, wu, wd, sg, su, sd):
    f = np.float32
    gate = np.asarray(gate_w, f)
    gate_r = np.ascontiguousarray(
        gate.reshape(cfg.NDC, 128, cfg.NEXP).transpose(1, 0, 2)
        .reshape(128, cfg.NDC * cfg.NEXP))
    return dict(
        wqT=np.ascontiguousarray(np.asarray(wq, f).T),
        wkT=np.ascontiguousarray(np.asarray(wk, f).T),
        wvT=np.ascontiguousarray(np.asarray(wv, f).T),
        woT=np.ascontiguousarray(np.asarray(wo, f).T),
        gate_r=gate_r,
        wg=np.ascontiguousarray(np.asarray(wg, f)),
        wu=np.ascontiguousarray(np.asarray(wu, f)),
        wd=np.ascontiguousarray(np.asarray(wd, f)),
        sg=np.ascontiguousarray(np.asarray(sg, f)),
        su=np.ascontiguousarray(np.asarray(su, f)),
        sd=np.ascontiguousarray(np.asarray(sd, f)),
    )


def kernel(L0_attn_x, L0_attn_wq, L0_attn_wk, L0_attn_wv, L0_attn_wo,
           L0_ffn_gate_w, L0_ffn_wg, L0_ffn_wu, L0_ffn_wd,
           L0_ffn_sg, L0_ffn_su, L0_ffn_sd):
    cfg = CFG()
    x = np.asarray(L0_attn_x, dtype=np.float32)
    w = prep_weights(cfg, L0_attn_wq, L0_attn_wk, L0_attn_wv, L0_attn_wo,
                     L0_ffn_gate_w, L0_ffn_wg, L0_ffn_wu, L0_ffn_wd,
                     L0_ffn_sg, L0_ffn_su, L0_ffn_sd)
    nc = bass.Bass("TRN2", target_bir_lowering=False)
    build_program(nc, cfg)
    in_maps = shard_inputs(cfg, x, **w)
    res = run_bass_kernel_spmd(nc, in_maps, core_ids=list(range(8)))
    out = np.empty((cfg.B, cfg.S, cfg.D), np.float32)
    for core in range(8):
        bidx, q = core // cfg.NQ, core % cfg.NQ
        out[bidx, q * cfg.OWN:(q + 1) * cfg.OWN] = res.results[core]["out"]
    return out


# version 13
# speedup vs baseline: 1.0133x; 1.0133x over previous
import math
import numpy as np
from contextlib import ExitStack

import concourse.bass as bass
import concourse.mybir as mybir
from concourse.bass_utils import run_bass_kernel_spmd

F32 = mybir.dt.float32
BF16 = mybir.dt.bfloat16
I32 = mybir.dt.int32
AF = mybir.ActivationFunctionType
ALU = mybir.AluOpType
AX = mybir.AxisListType

ENGINES = ("sync", "scalar", "vector", "tensor", "gpsimd")
N_DMA_SEMS = 24


class MK:

    def __init__(self, nc):
        self.nc = nc
        self.es = ExitStack()
        self.sems = {e: self.es.enter_context(nc.semaphore(f"mk_{e}")) for e in ENGINES}
        for i in range(N_DMA_SEMS):
            self.sems[i] = self.es.enter_context(nc.semaphore(f"mkd_{i}"))
        self.cnt = {k: 0 for k in self.sems}
        self.dma_rr = 0
        self.dma_eng = {}
        self.waited = {}
        self.phases = []
        self.cur = None
        self.phase()

    def phase(self):
        if self.cur is not None:
            self.fence_dma()
        self.cur = {e: [] for e in ENGINES}
        self.phases.append(self.cur)
        for cns in ENGINES:
            for k in self.sems:
                self.waited[(cns, k)] = self.cnt[k]

    def _wait(self, eng, ev):
        if ev is None:
            return
        k, val = ev
        if self.waited.get((eng, k), 0) >= val:
            return
        self.waited[(eng, k)] = val
        sem = self.sems[k]
        self.cur[eng].append(lambda E, sem=sem, val=val: E.wait_ge(sem, val))

    def step(self, eng, deps, emit, inc=1, key=None):
        for d in deps:
            self._wait(eng, d)
        k = eng if key is None else key
        self.cnt[k] += inc
        sem = self.sems[k]

        def run(E, emit=emit, sem=sem, inc=inc):
            last = emit(E)
            if isinstance(last, (list, tuple)):
                last = last[-1]
            last.then_inc(sem, inc)
        self.cur[eng].append(run)
        return (k, self.cnt[k])

    def chain(self, eng, deps, emits, extra=None):
        ev = None
        for i, em in enumerate(emits):
            d = list(deps if i == 0 else []) + ([ev] if ev else [])
            if extra and extra[i]:
                d += list(extra[i])
            ev = self.step(eng, d, em)
        return ev

    def fence_dma(self):
        for k, eng in self.dma_eng.items():
            self._wait(eng, (k, self.cnt[k]))

    def fence_all(self):
        self.fence_dma()
        for e in ENGINES:
            if self.cnt[e]:
                self._wait("sync", (e, self.cnt[e]))

    def dma(self, eng, deps, out, in_, **kw):
        k = self.dma_rr % N_DMA_SEMS
        self.dma_rr += 1
        self.dma_eng[k] = eng
        self._wait(eng, (k, self.cnt[k]))
        return self.step(eng, deps,
                         lambda E: E.dma_start(out=out, in_=in_, **kw),
                         inc=16, key=k)

    def build_blocks(self):
        for ph in self.phases:
            with self.nc.Block() as blk:
                for e in ENGINES:
                    ops = ph[e]
                    if not ops:
                        continue
                    getattr(blk, e)(lambda E, ops=ops: [f(E) for f in ops])


class CFG:
    def __init__(self, D=2048, NH=16, NKV=4, FEXP=768, NEXP=32, FSH=1536,
                 TOPK=6, OWN=2048, LB=6144, NQ=4, B=2, S=8192, debug=False):
        self.debug = debug
        self.D, self.NH, self.NKV = D, NH, NKV
        self.FEXP, self.NEXP, self.FSH, self.TOPK = FEXP, NEXP, FSH, TOPK
        self.OWN, self.LB, self.NQ, self.B, self.S = OWN, LB, NQ, B, S
        self.DH = 128
        self.NDC = D // 128
        self.NFC = FEXP // 128
        self.NSC = FSH // 128
        self.GR = NH // NKV
        self.NG = NH // 4
        assert self.GR == 4 and self.NG == NKV
        self.SPT = OWN + LB
        self.NT = self.SPT // 128
        self.NT_OWN = OWN // 128
        self.NST = self.SPT // 512
        self.NST_OWN = OWN // 512
        self.HALF = OWN // 2
        self.NTT = self.HALF // 128
        self.NTG = self.HALF // 512
        self.NDG = D // 512
        assert self.NTG >= 1 and LB == (NQ - 1) * OWN
        self.lam = [2.0 ** (-8.0 * (h + 1) / NH) for h in range(NH)]
        self.blk = [math.exp(-l * 128.0) for l in self.lam]


def build_program(nc, c):
    D, NDC, NH, NKV = c.D, c.NDC, c.NH, c.NKV
    NG, NFC, NEXP, FEXP = c.NG, c.NFC, c.NEXP, c.FEXP
    NTT, NDG, NTG, HALF = c.NTT, c.NDG, c.NTG, c.HALF
    KW = NKV * 128
    QW = NH * 128

    X = nc.dram_tensor("x", [c.SPT, D], F32, kind="ExternalInput").ap()
    POS = nc.dram_tensor("pos", [1, c.SPT], F32, kind="ExternalInput").ap()
    WQT = nc.dram_tensor("wqT", [D, QW], F32, kind="ExternalInput").ap()
    WKT = nc.dram_tensor("wkT", [D, KW], F32, kind="ExternalInput").ap()
    WVT = nc.dram_tensor("wvT", [D, KW], F32, kind="ExternalInput").ap()
    WOT = nc.dram_tensor("woT", [QW, D], F32, kind="ExternalInput").ap()
    GATE = nc.dram_tensor("gate_r", [128, NDC * NEXP], F32, kind="ExternalInput").ap()
    WG = nc.dram_tensor("wg", [NEXP, D, FEXP], F32, kind="ExternalInput").ap()
    WU = nc.dram_tensor("wu", [NEXP, D, FEXP], F32, kind="ExternalInput").ap()
    WD = nc.dram_tensor("wd", [NEXP, FEXP, D], F32, kind="ExternalInput").ap()
    SG = nc.dram_tensor("sg", [D, c.FSH], F32, kind="ExternalInput").ap()
    SU = nc.dram_tensor("su", [D, c.FSH], F32, kind="ExternalInput").ap()
    SD = nc.dram_tensor("sd", [c.FSH, D], F32, kind="ExternalInput").ap()
    OUT = nc.dram_tensor("out", [c.OWN, D], F32, kind="ExternalOutput").ap()

    scr_kind = "ExternalOutput" if c.debug else "Internal"
    WG_BF = nc.dram_tensor("wg_bf_scr", [NEXP, NDC, 128, FEXP], BF16, kind="Internal").ap()
    WU_BF = nc.dram_tensor("wu_bf_scr", [NEXP, NDC, 128, FEXP], BF16, kind="Internal").ap()
    WD_BF = nc.dram_tensor("wd_bf_scr", [NEXP, NFC, 128, D], BF16, kind="Internal").ap()
    NSH2 = c.NSC // 2
    FSH2c = NSH2 * 128
    SG_BF = nc.dram_tensor("sg_bf_scr", [2, NDC, 128, FSH2c], BF16, kind="Internal").ap()
    SU_BF = nc.dram_tensor("su_bf_scr", [2, NDC, 128, FSH2c], BF16, kind="Internal").ap()
    SD_BF = nc.dram_tensor("sd_bf_scr", [c.NSC, 128, D], BF16, kind="Internal").ap()
    XN_SCR = nc.dram_tensor("xn_scr", [128, NDC * c.OWN], BF16, kind=scr_kind).ap()
    KVSNAP = nc.dram_tensor("kvsnap", [c.NT_OWN, 128, NH * 128], BF16, kind=scr_kind).ap()
    OFEAT = nc.dram_tensor("ofeat", [c.NT_OWN, 128, NH * 128], BF16, kind=scr_kind).ap()
    ATTN = nc.dram_tensor("attn_scr", [c.OWN, D], F32, kind=scr_kind).ap()

    ident_bf = nc.alloc_sbuf_tensor("ident_bf", [128, 128], BF16)
    ident_f32 = nc.alloc_sbuf_tensor("ident_f32", [128, 128], F32)
    iota_p = nc.alloc_sbuf_tensor("iota_p", [128, 1], F32)
    iota_row = nc.alloc_sbuf_tensor("iota_row", [128, 128], F32)
    ones_row = nc.alloc_sbuf_tensor("ones_row", [1, 128], F32)
    pi_col = nc.alloc_sbuf_tensor("pi_col", [128, 1], F32)
    eps_col = nc.alloc_sbuf_tensor("eps_col", [128, 1], F32)

    pstg = [nc.alloc_sbuf_tensor(f"pump_f{i}", [128, 1024], F32) for i in range(2)]
    pbf = [nc.alloc_sbuf_tensor(f"pump_b{i}", [128, 1024], BF16) for i in range(2)]
    ps = [nc.alloc_psum_tensor(f"psb{i}", [128, 512], F32) for i in range(7)]
    ps7 = nc.alloc_psum_tensor("psb7", [128, 1024], BF16)

    mk = MK(nc)
    V, S, T, G, Y = "vector", "scalar", "tensor", "gpsimd", "sync"

    class Pump:
        def __init__(self):
            self.jobs = []
            self.done = {}
            self.i = 0
            self.stg_free = [None, None]
            self.bf_free = [None, None]
            self.ncast = 0

        def add(self, key, blocks):
            for b_ in blocks:
                self.jobs.append((key,) + b_)

        def tick(self, n):
            for _ in range(n):
                if self.i >= len(self.jobs):
                    return
                key, sap, dap, wdt = self.jobs[self.i]
                s = self.i % 2
                ld = mk.dma(Y, [self.stg_free[s]], pstg[s][:, 0:wdt], sap)
                ce = mk.step(V, [ld, self.bf_free[s]],
                             lambda E, s=s, wdt=wdt:
                             E.tensor_copy(pbf[s][:, 0:wdt], pstg[s][:, 0:wdt]))
                self.stg_free[s] = ce
                st = mk.dma(S, [ce], dap, pbf[s][:, 0:wdt])
                self.bf_free[s] = st
                self.done[key] = st
                self.i += 1

        def finish(self):
            self.tick(len(self.jobs))

    pump = Pump()
    GB = max(1, min(NDC, 1024 // FEXP))
    for e in range(NEXP):
        pump.add(("wg", e), [
            (WG[e, dc * 128:(dc + GB) * 128, :].rearrange("(b p) f -> p b f", p=128),
             WG_BF[e, dc:dc + GB].transpose([1, 0, 2]),
             GB * FEXP)
            for dc in range(0, NDC, GB)])
        pump.add(("wu", e), [
            (WU[e, dc * 128:(dc + GB) * 128, :].rearrange("(b p) f -> p b f", p=128),
             WU_BF[e, dc:dc + GB].transpose([1, 0, 2]),
             GB * FEXP)
            for dc in range(0, NDC, GB)])
        WDW = min(1024, D)
        pump.add(("wd", e), [(WD[e, fc * 128:(fc + 1) * 128, hh * WDW:(hh + 1) * WDW],
                              WD_BF[e, fc, :, hh * WDW:(hh + 1) * WDW], WDW)
                             for fc in range(NFC) for hh in range(D // WDW)])
    for sh in range(2):
        pump.add(("sg", sh), [(SG[dc * 128:(dc + 1) * 128, sh * FSH2c:(sh + 1) * FSH2c],
                               SG_BF[sh, dc], FSH2c) for dc in range(NDC)])
        pump.add(("su", sh), [(SU[dc * 128:(dc + 1) * 128, sh * FSH2c:(sh + 1) * FSH2c],
                               SU_BF[sh, dc], FSH2c) for dc in range(NDC)])
    WDW2 = min(1024, D)
    pump.add(("sd", 0), [(SD[fc * 128:(fc + 1) * 128, hh * WDW2:(hh + 1) * WDW2],
                          SD_BF[fc, :, hh * WDW2:(hh + 1) * WDW2], WDW2)
                         for fc in range(c.NSC) for hh in range(D // WDW2)])
    A1_BLOCKS = 0
    for j in pump.jobs:
        if j[0][0] in ("wg", "wu", "wd") and isinstance(j[0][1], int) and j[0][1] >= 14:
            break
        A1_BLOCKS += 1
    PUMP_PER_TILE = (A1_BLOCKS + c.NT - 1) // c.NT

    a_scope = ExitStack()

    def sba(name, shape, dtype):
        return a_scope.enter_context(nc.sbuf_tensor(name, shape, dtype))

    cos_own = sba("cos_o", [128, c.OWN], BF16)
    sin_own = sba("sin_o", [128, c.OWN], BF16)
    rotm = sba("rotm", [128, 128], BF16)
    kdec_cols = sba("kdec_cols", [128, NH], F32)
    k_own = sba("k_own", [128, NKV * c.OWN], BF16)
    v_own = sba("v_own", [128, c.NT_OWN * KW], BF16)

    lb_scope = ExitStack()
    cos_lb = lb_scope.enter_context(nc.sbuf_tensor("cos_lb", [128, c.LB], BF16))
    sin_lb = lb_scope.enter_context(nc.sbuf_tensor("sin_lb", [128, c.LB], BF16))

    def tabsl(which, lo, hi):
        if hi <= c.LB:
            return (cos_lb if which == 0 else sin_lb)[:, lo:hi]
        return (cos_own if which == 0 else sin_own)[:, lo - c.LB:hi - c.LB]

    p0 = ExitStack()
    pos_sb = p0.enter_context(nc.sbuf_tensor("pos_sb", [1, c.SPT], F32))
    invfreq = p0.enter_context(nc.sbuf_tensor("invfreq", [1, 128], F32))
    iv_raw = p0.enter_context(nc.sbuf_tensor("iv_raw", [1, 128], F32))
    iv_g = p0.enter_context(nc.sbuf_tensor("iv_g", [1, 128], F32))
    rampb = p0.enter_context(nc.sbuf_tensor("rampb", [128, 128], F32))
    scr0 = p0.enter_context(nc.sbuf_tensor("p0scr0", [128, 512], F32))
    scr1 = p0.enter_context(nc.sbuf_tensor("p0scr1", [128, 512], F32))
    scri = p0.enter_context(nc.sbuf_tensor("p0scri", [128, 512], I32))
    scrH = p0.enter_context(nc.sbuf_tensor("p0scrH", [128, 512], F32))
    scrS = p0.enter_context(nc.sbuf_tensor("p0scrS", [128, 512], F32))

    ev_iota = mk.step(G, [], lambda E: [
        E.iota(iota_p[:], [[1, 1]], base=0, channel_multiplier=1,
               allow_small_or_imprecise_dtypes=True),
        E.iota(iota_row[:], [[1, 128]], base=0, channel_multiplier=0,
               allow_small_or_imprecise_dtypes=True),
        E.iota(iv_raw[:], [[1, 128]], base=0, channel_multiplier=0,
               allow_small_or_imprecise_dtypes=True),
    ])
    ev_pos = mk.dma(Y, [], pos_sb[:], POS)
    ev_const = mk.step(V, [ev_iota], lambda E: [
        E.tensor_single_scalar(ident_bf[:], iota_row[:], iota_p[:, 0:1], ALU.is_equal),
        E.tensor_single_scalar(ident_f32[:], iota_row[:], iota_p[:, 0:1], ALU.is_equal),
        E.memset(ones_row[:], 1.0),
        E.memset(pi_col[:], math.pi),
        E.memset(eps_col[:], 1e-5),
    ])
    ev_ivm = mk.chain(V, [ev_iota, ev_const], [
        lambda E: E.tensor_single_scalar(iv_g[:], iv_raw[:], 64.0, ALU.is_ge),
        lambda E: E.scalar_tensor_tensor(iv_g[:], iv_g[:], -64.0, iv_raw[:],
                                         ALU.mult, ALU.add),
    ])
    ev_if = mk.step(S, [ev_ivm], lambda E:
                    E.activation(out=invfreq[:], in_=iv_g[:], func=AF.Exp,
                                 scale=-math.log(10000.0) / 64.0))
    ev_rotm = mk.chain(V, [ev_iota, ev_const], [
        lambda E: E.tensor_scalar(scr0[:, 0:1], iota_p[:, 0:1], 1.0, 64.0,
                                  ALU.mult, ALU.add),
        lambda E: E.tensor_single_scalar(scr1[:, 0:128], iota_row[:], scr0[:, 0:1],
                                         ALU.is_equal),
        lambda E: E.tensor_scalar(scr0[:, 0:1], iota_p[:, 0:1], 1.0, -64.0,
                                  ALU.mult, ALU.add),
        lambda E: E.tensor_single_scalar(scr1[:, 128:256], iota_row[:], scr0[:, 0:1],
                                         ALU.is_equal),
        lambda E: E.tensor_sub(rotm[:], scr1[:, 0:128], scr1[:, 128:256]),
    ])

    ev_tab_chain = []
    for i in range(c.SPT // 512):
        sl = slice(i * 512, (i + 1) * 512)
        bank = ps[i % 2]
        dep_bank = ev_tab_chain[-2][0] if len(ev_tab_chain) >= 2 else None
        evm = mk.step(T, [ev_if, ev_pos, dep_bank, ev_rotm],
                      lambda E, sl=sl, bank=bank:
                      E.matmul(bank[:], invfreq[0:1, :], pos_sb[0:1, sl],
                               start=True, stop=True))
        prev_s = ev_tab_chain[-1][1] if ev_tab_chain else None
        ev_last = None
        for which in (1, 0):
            shift = 0.0 if which == 1 else 0.5
            evf = mk.chain(V, [evm, ev_const, prev_s if which == 1 else ev_last], [
                lambda E, bank=bank, shift=shift: E.tensor_scalar(
                    scrH[:, 0:512], bank[:], 1.0 / math.pi, shift, ALU.mult, ALU.add),
                lambda E: E.tensor_copy(scri[:, 0:512], scrH[:, 0:512]),
                lambda E: E.tensor_copy(scr1[:, 0:512], scri[:, 0:512]),
                lambda E: E.tensor_sub(scr0[:, 0:512], scrH[:, 0:512], scr1[:, 0:512]),
                lambda E: E.tensor_single_scalar(scrS[:, 0:512], scr0[:, 0:512], 0.5,
                                                 ALU.is_gt),
                lambda E: E.tensor_single_scalar(scrH[:, 0:512], scr0[:, 0:512], -0.5,
                                                 ALU.is_lt),
                lambda E: E.tensor_sub(scrS[:, 0:512], scrS[:, 0:512], scrH[:, 0:512]),
                lambda E: E.tensor_sub(scr0[:, 0:512], scr0[:, 0:512], scrS[:, 0:512]),
                lambda E: E.tensor_add(scr1[:, 0:512], scr1[:, 0:512], scrS[:, 0:512]),
                lambda E: E.tensor_scalar(scrH[:, 0:512], scr1[:, 0:512], 0.5,
                                          None, ALU.mult),
                lambda E: E.tensor_copy(scri[:, 0:512], scrH[:, 0:512]),
                lambda E: E.tensor_copy(scrH[:, 0:512], scri[:, 0:512]),
                lambda E: E.scalar_tensor_tensor(scrH[:, 0:512], scrH[:, 0:512], -2.0,
                                                 scr1[:, 0:512], ALU.mult, ALU.add),
                lambda E: E.tensor_mul(scrH[:, 0:512], scrH[:, 0:512], scrH[:, 0:512]),
                lambda E: E.tensor_scalar(scrH[:, 0:512], scrH[:, 0:512], -2.0, 1.0,
                                          ALU.mult, ALU.add),
            ])
            evs0 = mk.step(S, [evf], lambda E:
                           E.activation(out=scrS[:, 0:512], in_=scr0[:, 0:512],
                                        func=AF.Sin, scale=math.pi))
            tdst = tabsl(which, sl.start, sl.stop)
            ev_last = mk.step(V, [evs0], lambda E, tdst=tdst:
                              E.tensor_mul(tdst, scrS[:, 0:512], scrH[:, 0:512]))
        ev_tab_chain.append((ev_last, ev_last))
    ev_tab_all = ev_tab_chain[-1][1]

    ev_h = ev_tab_chain[-1][0]
    for h in range(NH):
        lam = c.lam[h]
        e6 = mk.step(V, [ev_h], lambda E:
                     E.tensor_scalar(scr1[:, 0:1], iota_p[:, 0:1], -1.0, 127.0,
                                     ALU.mult, ALU.add))
        e7 = mk.step(S, [e6], lambda E, h=h, lam=lam: E.activation(
            out=kdec_cols[:, h:h + 1], in_=scr1[:, 0:1], func=AF.Exp, scale=-lam))
        ev_h = e7
    p0.close()

    mk.phase()
    a1 = ExitStack()

    def sb1(name, shape, dtype):
        return a1.enter_context(nc.sbuf_tensor(name, shape, dtype))

    nbn = 6 * ((D + 511) // 512)
    kv_sb = sb1("kv_sb", [128, NH * 128], F32)
    x_ring = [sb1(f"x_ring{i}", [128, D], F32) for i in range(2)]
    stats = sb1("stats", [128, 8], F32)
    bnrows = sb1("bnrows", [128, nbn], F32)
    xn_tok = [sb1(f"xn_tok{i}", [128, D], BF16) for i in range(1)]
    xn_st = [sb1(f"xn_st{i}", [128, NDC * 512], BF16) for i in range(2)]
    kraw = sb1("kraw", [128, NKV * 512], BF16)
    krot_scr = sb1("krot_scr", [128, NKV * 512], BF16)
    rsc = sb1("rsc", [128, 1024], BF16)
    kpos_t = [sb1(f"kpos{i}", [128, KW], BF16) for i in range(2)]
    kprime = sb1("kprime", [128, NH * 128], BF16)
    wk_bf = sb1("wk_bf", [128, NDC * KW], BF16)
    wv_bf = sb1("wv_bf", [128, NDC * KW], BF16)
    wstg = [sb1(f"wstg{i}", [128, KW], F32) for i in range(2)]
    snap = [sb1(f"snap{i}", [128, NH * 128], BF16) for i in range(1)]
    v_scr = [sb1(f"vscr{i}", [128, KW], BF16) for i in range(2)]

    ev_kv0 = mk.step(V, [], lambda E: E.memset(kv_sb[:], 0.0))

    wfree = [None, None]
    last_cast = None
    for i, (W_, dst) in enumerate([(WKT, wk_bf), (WVT, wv_bf)]):
        for dc in range(NDC):
            s = (i * NDC + dc) % 2
            ld = mk.dma(Y, [wfree[s]], wstg[s][:], W_[dc * 128:(dc + 1) * 128, :])
            ce = mk.step(V, [ld], lambda E, s=s, dst=dst, dc=dc:
                         E.tensor_copy(dst[:, dc * KW:(dc + 1) * KW], wstg[s][:]))
            wfree[s] = ce
            last_cast = ce
    ev_wkv = last_cast

    x_free = [None] * 2
    xn_free = [None]
    xnst_free = [[], []]
    kpos_free = [None, None]
    vbank_free = [None, None]
    vscr_free = [None, None]
    snap_free = [None]
    ps7_free = None
    ps6_free = None
    rot_bank_free = [None, None]
    kprime_free = None
    rsc_free = None
    ev_kv_upd = ev_kv0
    ev_k_prev_st = None
    stats_free = None
    kraw_free = None
    krot_free = None

    for st in range(c.NST):
        own_st = st >= c.NST - c.NST_OWN
        so = st - (c.NST - c.NST_OWN)
        xs = xn_st[st % 2]
        ev_tp_tiles = []
        for tl in range(4):
            t = st * 4 + tl
            pump.tick(PUMP_PER_TILE)
            xslot = t % 2
            ev_x = mk.dma(Y, [x_free[xslot]], x_ring[xslot][:],
                          X[t * 128:(t + 1) * 128, :])
            ngrp = (D + 511) // 512
            ev_bn = mk.chain(V, [ev_x, stats_free], [
                lambda E, xslot=xslot, ngrp=ngrp: [
                    E.bn_stats(bnrows[:, g * 6:(g + 1) * 6],
                               x_ring[xslot][:, g * 512:min((g + 1) * 512, D)])
                    for g in range(ngrp)],
                lambda E, ngrp=ngrp: E.bn_aggr(stats[:, 0:2], bnrows[:, 0:6 * ngrp]),
            ])
            ev_sq = mk.step(S, [ev_bn], lambda E: E.activation(
                out=stats[:, 5:6], in_=stats[:, 1:2], func=AF.Sqrt,
                bias=eps_col[:, 0:1]))
            ev_nmr = mk.chain(V, [ev_sq], [
                lambda E: E.reciprocal(stats[:, 2:3], stats[:, 5:6]),
                lambda E: E.tensor_scalar(stats[:, 3:4], stats[:, 0:1], -1.0, None, ALU.mult),
                lambda E: E.tensor_mul(stats[:, 4:5], stats[:, 3:4], stats[:, 2:3]),
            ])
            xnslot = 0
            ev_xn = mk.step(S, [ev_nmr, xn_free[xnslot]],
                            lambda E, xslot=xslot, xnslot=xnslot: E.activation(
                out=xn_tok[xnslot][:], in_=x_ring[xslot][:], func=AF.Identity,
                scale=stats[:, 2:3], bias=stats[:, 4:5]))
            x_free[xslot] = ev_xn
            stats_free = ev_xn
            ev_ev2 = None
            for hf in range(2):
                ndc_h = min(8, NDC - hf * 8)
                if ndc_h <= 0:
                    continue
                ev_tp = mk.step(T, [ev_xn, ps7_free] +
                                (xnst_free[st % 2] if (hf == 0 and tl == 0) else []),
                                lambda E, xnslot=xnslot, hf=hf, ndc_h=ndc_h: [
                    E.transpose(ps7[:, j * 128:(j + 1) * 128],
                                xn_tok[xnslot][:, (hf * 8 + j) * 128:(hf * 8 + j + 1) * 128],
                                ident_bf[:])
                    for j in range(ndc_h)])
                eng = V if hf == 0 else S
                if eng == V:
                    ev_ev2 = mk.step(V, [ev_tp],
                                     lambda E, xs=xs, tl=tl, hf=hf, ndc_h=ndc_h:
                                     E.tensor_copy(
                        xs[:].rearrange("p (dc t) -> p dc t", t=512)
                          [:, hf * 8:hf * 8 + ndc_h, tl * 128:(tl + 1) * 128],
                        ps7[:].rearrange("p (dc t) -> p dc t", t=128)[:, 0:ndc_h, :]))
                else:
                    ev_ev2 = mk.step(S, [ev_tp],
                                     lambda E, xs=xs, tl=tl, hf=hf, ndc_h=ndc_h:
                                     E.copy(
                        xs[:].rearrange("p (dc t) -> p dc t", t=512)
                          [:, hf * 8:hf * 8 + ndc_h, tl * 128:(tl + 1) * 128],
                        ps7[:].rearrange("p (dc t) -> p dc t", t=128)[:, 0:ndc_h, :]))
                ps7_free = ev_ev2
            xn_free[xnslot] = ev_ev2
            ev_tp_tiles.append(ev_ev2)
        ev_xn_dma = None
        if own_st:
            ev_xn_dma = mk.dma(Y, ev_tp_tiles,
                               XN_SCR[:].rearrange("p (dc t) -> p dc t", t=c.OWN)
                               [:, :, so * 512:(so + 1) * 512],
                               xs[:].rearrange("p (dc t) -> p dc t", t=512))

        ev_k_oc = []
        for oc in range(NKV):
            bank = ps[oc % 2]
            prev = ev_k_oc[oc - 2] if oc >= 2 else ev_k_prev_st

            def kmm(E, oc=oc, bank=bank, xs=xs):
                out = None
                for dc in range(NDC):
                    out = E.matmul(bank[:],
                                   wk_bf[:, dc * KW + oc * 128: dc * KW + (oc + 1) * 128],
                                   xs[:, dc * 512:(dc + 1) * 512],
                                   start=(dc == 0), stop=(dc == NDC - 1))
                return out
            evm = mk.step(T, ev_tp_tiles + [ev_wkv, prev], kmm)
            eve = mk.step(V, [evm, kraw_free if oc == 0 else None],
                          lambda E, oc=oc, bank=bank:
                          E.tensor_copy(kraw[:, oc * 512:(oc + 1) * 512], bank[:]))
            ev_k_oc.append(eve)
        if own_st:
            kout = k_own[:].rearrange("p (kv t) -> p kv t", t=c.OWN)[:, :, so * 512:(so + 1) * 512]
        else:
            kout = krot_scr[:].rearrange("p (kv t) -> p kv t", t=512)
        ev_krot = None
        for kv in range(NKV):
            cosl = tabsl(0, st * 512, (st + 1) * 512)
            sinl = tabsl(1, st * 512, (st + 1) * 512)
            p7s = ps[4 + kv % 2][:]
            ev_rt = mk.step(T, [ev_k_oc[kv], ev_rotm, rot_bank_free[kv % 2]],
                            lambda E, kv=kv, p7s=p7s:
                            E.matmul(p7s, rotm[:], kraw[:, kv * 512:(kv + 1) * 512],
                                     start=True, stop=True))
            ev_krot = mk.chain(V, [ev_rt, ev_tab_all, rsc_free, krot_free], [
                lambda E, kv=kv, cosl=cosl:
                    E.tensor_tensor(rsc[:, 0:512], kraw[:, kv * 512:(kv + 1) * 512],
                                    cosl, ALU.mult),
                lambda E, p7s=p7s, sinl=sinl:
                    E.tensor_tensor(rsc[:, 512:1024], p7s, sinl, ALU.mult),
                lambda E, kv=kv, kout=kout:
                    E.tensor_add(kout[:, kv], rsc[:, 0:512], rsc[:, 512:1024]),
            ])
            rot_bank_free[kv % 2] = ev_krot
            rsc_free = ev_krot
        kraw_free = ev_krot
        ev_k_prev_st = ev_k_oc[-1]

        ev_last_delta_tile = None
        for tl in range(4):
            t = st * 4 + tl
            vb = tl % 2
            vbank = ps[2 + vb]

            def vmm(E, xs=xs, tl=tl, vbank=vbank):
                out = None
                for dc in range(NDC):
                    out = E.matmul(vbank[:, 0:KW],
                                   xs[:, dc * 512 + tl * 128: dc * 512 + (tl + 1) * 128],
                                   wv_bf[:, dc * KW:(dc + 1) * KW],
                                   start=(dc == 0), stop=(dc == NDC - 1))
                return out
            evv = mk.step(T, [ev_tp_tiles[tl], ev_wkv, vbank_free[vb]], vmm)
            ev_last_vmm = evv
            if own_st:
                vcur = v_own[:, (so * 4 + tl) * KW:(so * 4 + tl + 1) * KW]
                evve = mk.step(V, [evv], lambda E, vcur=vcur, vbank=vbank:
                               E.tensor_copy(vcur, vbank[:, 0:KW]))
            else:
                vs_ = t % 2
                vcur = v_scr[vs_][:, 0:KW]
                evve = mk.step(V, [evv, vscr_free[vs_]],
                               lambda E, vcur=vcur, vbank=vbank:
                               E.tensor_copy(vcur, vbank[:, 0:KW]))
            vbank_free[vb] = evve

            kslot = t % 2
            ev_kt = mk.step(T, [ev_krot, kpos_free[kslot], ps7_free],
                            lambda E, kout=kout, tl=tl: [
                E.transpose(ps7[:, j * 128:(j + 1) * 128],
                            kout[:, j, tl * 128:(tl + 1) * 128], ident_bf[:])
                for j in range(NKV)])
            ev_kte = mk.step(S, [ev_kt], lambda E, kslot=kslot:
                             E.copy(kpos_t[kslot][:, 0:KW], ps7[:, 0:KW]))
            ps7_free = ev_kte
            ev_kp = mk.step(V, [ev_kte, kprime_free], lambda E, kslot=kslot: [
                E.tensor_single_scalar(kprime[:, h * 128:(h + 1) * 128],
                                       kpos_t[kslot][:, (h // 4) * 128:(h // 4 + 1) * 128],
                                       kdec_cols[:, h:h + 1], ALU.mult)
                for h in range(NH)])
            kpos_free[kslot] = ev_kp
            last_kt = ev_kt

            ev_sn = None
            if own_st:
                co = so * 4 + tl
                sslot = 0
                ev_sn = mk.step(V, [ev_kv_upd, snap_free[sslot]],
                                lambda E, sslot=sslot:
                                E.tensor_copy(snap[sslot][:], kv_sb[:]))
                ev_snd = mk.dma(Y, [ev_sn], KVSNAP[co], snap[sslot][:])
                snap_free[sslot] = ev_snd

            for g in range(NG):
                def dmm(E, g=g, vcur=vcur):
                    out = None
                    for hp in range(4):
                        out = E.matmul(ps[6][:, hp * 128:(hp + 1) * 128],
                                       kprime[:, (g * 4 + hp) * 128:(g * 4 + hp + 1) * 128],
                                       vcur[:, g * 128:(g + 1) * 128],
                                       start=True, stop=True)
                    return out
                evd = mk.step(T, [ev_kp, evve, ps6_free], dmm)
                ev_up = mk.step(V, [evd] + ([ev_sn] if ev_sn else []),
                                lambda E, g=g: [
                    E.scalar_tensor_tensor(
                        kv_sb[:, (g * 4 + hp) * 128:(g * 4 + hp + 1) * 128],
                        kv_sb[:, (g * 4 + hp) * 128:(g * 4 + hp + 1) * 128],
                        c.blk[g * 4 + hp],
                        ps[6][:, hp * 128:(hp + 1) * 128],
                        ALU.mult, ALU.add)
                    for hp in range(4)])
                ps6_free = ev_up
                ev_kv_upd = ev_up
                ev_last_delta_tile = ev_up
            kprime_free = ev_last_delta_tile
            if not own_st:
                vscr_free[t % 2] = ev_last_delta_tile
        xnst_free[st % 2] = [ev_last_vmm] + ([ev_xn_dma] if ev_xn_dma else [])
        krot_free = last_kt

    a1.close()
    lb_scope.close()

    mk.phase()
    a2 = ExitStack()

    def sb2(name, shape, dtype):
        return a2.enter_context(nc.sbuf_tensor(name, shape, dtype))

    mask_pack = sb2("mask_pack", [128, NG * 512], BF16)
    qdec_pack = sb2("qdec_pack", [128, NG * 512], BF16)

    mp0 = ExitStack()
    rampb = mp0.enter_context(nc.sbuf_tensor("rampb2", [128, 128], F32))
    mscr = mp0.enter_context(nc.sbuf_tensor("mscr", [128, 512], F32))
    ev_rb = mk.step(T, [], lambda E:
                    E.matmul(ps[2][:, 0:128], ones_row[0:1, :], iota_row[0:1, :],
                             start=True, stop=True))
    ev_rbs = mk.step(V, [ev_rb], lambda E:
                     E.tensor_copy(rampb[:], ps[2][:, 0:128]))
    ev_diff = mk.step(V, [ev_rbs], lambda E:
                      E.tensor_single_scalar(mscr[:, 0:128], rampb[:], iota_p[:, 0:1],
                                             ALU.subtract))
    ev_h2 = ev_diff
    for h in range(NH):
        lam = c.lam[h]
        g, hp = h // 4, h % 4
        msl = slice(g * 512 + hp * 128, g * 512 + (hp + 1) * 128)
        e1 = mk.step(V, [ev_h2], lambda E: [
            E.tensor_single_scalar(mscr[:, 128:256], mscr[:, 0:128], 0.0, ALU.is_ge),
            E.tensor_single_scalar(mscr[:, 256:384], mscr[:, 0:128], 0.0, ALU.max),
        ])
        e2 = mk.step(S, [e1], lambda E, lam=lam: E.activation(
            out=mscr[:, 384:512], in_=mscr[:, 256:384], func=AF.Exp, scale=-lam))
        e3 = mk.step(V, [e2], lambda E, msl=msl:
                     E.tensor_mul(mask_pack[:, msl], mscr[:, 384:512], mscr[:, 128:256]))
        e4 = mk.step(S, [ev_rbs, e3], lambda E, lam=lam: E.activation(
            out=mscr[:, 384:512], in_=rampb[:], func=AF.Exp, scale=-lam))
        e5 = mk.step(V, [e4], lambda E, msl=msl, lam=lam:
                     E.tensor_scalar(qdec_pack[:, msl], mscr[:, 384:512],
                                     math.exp(-lam), None, ALU.mult))
        ev_h2 = e5
    mp0.close()

    wq_bf = sb2("wq_bf", [128, NDC * QW], BF16)
    wstg2 = [sb2(f"wstg2_{i}", [128, QW // 2], F32) for i in range(2)]
    xn_q = [sb2(f"xn_q{i}", [128, NDC * 512], BF16) for i in range(1)]
    q_st = sb2("q_st", [128, NH * 512], BF16)
    q_rot = sb2("q_rot", [128, NH * 512], BF16)
    rsc2 = sb2("rsc2", [128, 1024], BF16)
    qpk = [sb2(f"qpk{i}", [128, 512], BF16) for i in range(2)]
    scm = [sb2(f"scm{i}", [128, 512], BF16) for i in range(2)]
    kvsn = [sb2(f"kvsn{i}", [128, NH * 128], BF16) for i in range(2)]
    o_stage = [sb2(f"ostage{i}", [128, NH * 128], BF16) for i in range(1)]

    wfree2 = [None, None]
    last2 = None
    for dc2 in range(2 * NDC):
        dc, hh = dc2 // 2, dc2 % 2
        s = dc2 % 2
        HW = QW // 2
        ld = mk.dma(Y, [wfree2[s]], wstg2[s][:],
                    WQT[dc * 128:(dc + 1) * 128, hh * HW:(hh + 1) * HW])
        ce = mk.step(V, [ld], lambda E, s=s, dc=dc, hh=hh, HW=HW:
                     E.tensor_copy(wq_bf[:, dc * QW + hh * HW: dc * QW + (hh + 1) * HW],
                                   wstg2[s][:]))
        wfree2[s] = ce
        last2 = ce
    ev_wq = last2

    xnq_free = [None]
    kvsn_free = [None, None]
    ost_free = [None]
    qpk_free = [None, None]
    scm_free = [None, None]
    opack_free = [None, None]
    ev_q_last = None
    ev_qrot_free = None
    ps6a_free = None
    rot2_free = [None, None]

    for so in range(c.NST_OWN):
        xq = xn_q[0]
        ev_xq = mk.dma(Y, [xnq_free[0]],
                       xq[:].rearrange("p (dc t) -> p dc t", t=512),
                       XN_SCR[:].rearrange("p (dc t) -> p dc t", t=c.OWN)
                       [:, :, so * 512:(so + 1) * 512])
        ev_q = []
        for oc in range(NH):
            bank = ps[oc % 2]
            prev = ev_q[oc - 2][1] if oc >= 2 else ev_q_last

            def qmm(E, oc=oc, bank=bank, xq=xq):
                out = None
                for dc in range(NDC):
                    out = E.matmul(
                        bank[:],
                        wq_bf[:, dc * QW + oc * 128: dc * QW + (oc + 1) * 128],
                        xq[:, dc * 512:(dc + 1) * 512],
                        start=(dc == 0), stop=(dc == NDC - 1))
                return out
            evm = mk.step(T, [ev_xq, ev_wq, prev], qmm)
            eve = mk.step(S, [evm, ev_qrot_free if oc == 0 else None],
                          lambda E, oc=oc, bank=bank:
                          E.copy(q_st[:, oc * 512:(oc + 1) * 512], bank[:]))
            ev_q.append((evm, eve))
        xnq_free[0] = ev_q[-1][0]
        ev_q_last = ev_q[-1][1]

        cosl = cos_own[:, so * 512:(so + 1) * 512]
        sinl = sin_own[:, so * 512:(so + 1) * 512]
        ev_qrot = None
        for h in range(NH):
            p7s = ps[4 + h % 2][:]
            ev_rt = mk.step(T, [ev_q[h][1], ev_rotm, rot2_free[h % 2]],
                            lambda E, h=h, p7s=p7s:
                            E.matmul(p7s, rotm[:], q_st[:, h * 512:(h + 1) * 512],
                                     start=True, stop=True))
            ev_qrot = mk.chain(V, [ev_rt, ev_qrot, ev_qrot_free if h < 2 else None], [
                lambda E, h=h, cosl=cosl:
                    E.tensor_tensor(rsc2[:, 0:512], q_st[:, h * 512:(h + 1) * 512],
                                    cosl, ALU.mult),
                lambda E, p7s=p7s, sinl=sinl:
                    E.tensor_tensor(rsc2[:, 512:1024], p7s, sinl, ALU.mult),
                lambda E, h=h:
                    E.tensor_add(q_rot[:, h * 512:(h + 1) * 512],
                                 rsc2[:, 0:512], rsc2[:, 512:1024]),
            ])
            rot2_free[h % 2] = ev_qrot
        ev_qrot_free = None

        last_mix = None
        for tl in range(4):
            co = so * 4 + tl
            kslot = co % 2
            ev_ksd = mk.dma(Y, [kvsn_free[kslot]], kvsn[kslot][:], KVSNAP[co])
            oslot = 0
            ev_o_evs = []
            for g in range(NG):
                qslot = g % 2
                ev_qp = mk.step(V, [ev_qrot, qpk_free[qslot]],
                                lambda E, qslot=qslot, g=g, tl=tl:
                                E.tensor_tensor(
                    qpk[qslot][:].rearrange("p (a t) -> p a t", t=128),
                    q_rot[:].rearrange("p (h t) -> p h t", t=512)
                    [:, 4 * g:4 * g + 4, tl * 128:(tl + 1) * 128],
                    qdec_pack[:, g * 512:(g + 1) * 512].rearrange("p (a t) -> p a t", t=128),
                    ALU.mult))
                ev_sc = mk.step(T, [ev_qrot, ps6a_free],
                                lambda E, g=g, co=co, tl=tl:
                                E.matmul(ps[6][:],
                                         k_own[:].rearrange("p (kv t) -> p kv t", t=c.OWN)
                                         [:, g, co * 128:(co + 1) * 128],
                                         q_rot[:].rearrange("p (h t) -> p h t", t=512)
                                         [:, 4 * g:4 * g + 4, tl * 128:(tl + 1) * 128],
                                         start=True, stop=True))
                sslot = g % 2
                ev_ms = mk.step(V, [ev_sc, scm_free[sslot]],
                                lambda E, sslot=sslot, g=g:
                                E.tensor_mul(scm[sslot][:], ps[6][:],
                                             mask_pack[:, g * 512:(g + 1) * 512]))
                ps6a_free = ev_ms
                obank = ps[2 + g % 2]

                def omm(E, g=g, co=co, sslot=sslot, qslot=qslot, kslot=kslot, obank=obank):
                    E.matmul(obank[:],
                             v_own[:, co * KW + g * 128: co * KW + (g + 1) * 128],
                             scm[sslot][:], start=True, stop=False)
                    out = None
                    for hp in range(4):
                        h = 4 * g + hp
                        out = E.matmul(obank[:, hp * 128:(hp + 1) * 128],
                                       kvsn[kslot][:, h * 128:(h + 1) * 128],
                                       qpk[qslot][:, hp * 128:(hp + 1) * 128],
                                       start=False, stop=(hp == 3))
                    return out
                ev_om = mk.step(T, [ev_ms, ev_qp, ev_ksd, opack_free[g % 2]], omm)
                scm_free[sslot] = ev_om
                qpk_free[qslot] = ev_om
                ev_oe = mk.step(S, [ev_om] + ([ost_free[oslot]] if g == 0 else []),
                                lambda E, g=g, oslot=oslot, obank=obank:
                                E.copy(o_stage[oslot][:, g * 512:(g + 1) * 512],
                                       obank[:]))
                opack_free[g % 2] = ev_oe
                ev_o_evs.append(ev_oe)
                last_mix = ev_om
            kvsn_free[kslot] = ev_o_evs[-1]
            ev_od = mk.dma(Y, ev_o_evs, OFEAT[co], o_stage[oslot][:])
            ost_free[oslot] = ev_od
        ev_qrot_free = last_mix
    a2.close()
    a_scope.close()

    mk.phase()
    a2b = ExitStack()

    def sb3(name, shape, dtype):
        return a2b.enter_context(nc.sbuf_tensor(name, shape, dtype))

    wo_bf = sb3("wo_bf", [128, NH * D], BF16)
    wstg3 = [sb3(f"wstg3_{i}", [128, D], F32) for i in range(2)]
    o_buf = [sb3(f"obuf{i}", [128, 4 * NH * 128], BF16) for i in range(2)]
    att_f = sb3("att_f", [128, 512], F32)
    att_stage = [sb3(f"attst{i}", [128, D], F32) for i in range(4)]

    wfree3 = [None, None]
    last3 = None
    for ec in range(NH):
        s = ec % 2
        ld = mk.dma(Y, [wfree3[s]], wstg3[s][:], WOT[ec * 128:(ec + 1) * 128, :])
        ce = mk.step(V, [ld], lambda E, s=s, ec=ec:
                     E.tensor_copy(wo_bf[:, ec * D:(ec + 1) * D], wstg3[s][:]))
        wfree3[s] = ce
        last3 = ce
    ev_wo = last3

    obuf_free = [None, None]
    attst_free = [None] * 4
    attf_free = None
    pstp_free = None
    abank_free = [None, None]
    for so in range(c.NST_OWN):
        oslot = so % 2
        ev_ob = mk.dma(Y, [obuf_free[oslot]],
                       o_buf[oslot][:].rearrange("p (cc f) -> p cc f", f=NH * 128),
                       OFEAT[so * 4:(so + 1) * 4].transpose([1, 0, 2]))
        last_am = None
        for oc in range(NDC):
            bank = ps[oc % 2]

            def amm(E, oc=oc, oslot=oslot, bank=bank):
                out = None
                for ec in range(NH):
                    out = E.matmul(bank[:],
                                   wo_bf[:, ec * D + oc * 128: ec * D + (oc + 1) * 128],
                                   o_buf[oslot][:].rearrange("p (cc f) -> p cc f", f=NH * 128)
                                   [:, :, ec * 128:(ec + 1) * 128],
                                   start=(ec == 0), stop=(ec == NH - 1))
                return out
            ev_am = mk.step(T, [ev_ob, ev_wo, abank_free[oc % 2]], amm)
            last_am = ev_am
            ev_ae = mk.step(V, [ev_am, attf_free], lambda E, bank=bank:
                            E.tensor_copy(att_f[:], bank[:]))
            abank_free[oc % 2] = ev_ae
            ev_tp = mk.step(T, [ev_ae, pstp_free], lambda E: [
                E.transpose(ps[4][:, j * 128:(j + 1) * 128],
                            att_f[:, j * 128:(j + 1) * 128], ident_f32[:])
                for j in range(4)])
            attf_free = ev_tp
            ev_te = mk.step(S, [ev_tp] + (attst_free if oc == 0 else []),
                            lambda E, oc=oc: [
                E.copy(att_stage[j][:, oc * 128:(oc + 1) * 128],
                       ps[4][:, j * 128:(j + 1) * 128])
                for j in range(4)])
            pstp_free = ev_te
        for j in range(4):
            t = so * 4 + j
            evd = mk.dma(Y, [pstp_free], ATTN[t * 128:(t + 1) * 128, :],
                         att_stage[j][:])
            attst_free[j] = evd
        obuf_free[oslot] = last_am
    a2b.close()

    TOPK = c.TOPK
    yacc_scope = ExitStack()
    y_acc = yacc_scope.enter_context(nc.sbuf_tensor("y_acc", [128, NTT * D], F32))
    xnh_t = yacc_scope.enter_context(nc.sbuf_tensor("xnh", [128, NDC * HALF], BF16))

    ev_out_final = []

    def build_half(half):
        mk.phase()
        b = ExitStack()

        def sbb(name, shape, dtype):
            return b.enter_context(nc.sbuf_tensor(name + str(half), shape, dtype))

        combf = sbb("combf", [max(NEXP, 2), HALF], BF16)
        selm = sbb("selm", [128, 128], BF16)
        gsc_scope = ExitStack()

        def sbg(name, shape, dtype):
            return gsc_scope.enter_context(nc.sbuf_tensor(name + str(half), shape, dtype))

        gstg = sbg("gstg", [128, NDC * NEXP], F32)
        gate_bf = sbg("gate_bf", [128, NDC * NEXP], BF16)
        sgt = sbg("sgt", [128, NEXP], F32)
        gsc = sbg("gsc", [128, NEXP], F32)
        gm = sbg("gm", [128, 4], F32)
        comb_t = sbg("comb_t", [128, NEXP], F32)

        ev_xnh = mk.dma(Y, [],
                        xnh_t[:].rearrange("p (dc t) -> p dc t", t=HALF),
                        XN_SCR[:].rearrange("p (dc t) -> p dc t", t=c.OWN)
                        [:, :, half * HALF:(half + 1) * HALF])
        ev_gld = mk.dma(Y, [], gstg[:], GATE)
        ev_gc = mk.step(V, [ev_gld], lambda E: E.tensor_copy(gate_bf[:], gstg[:]))

        ps6b_free = None
        ps5_free = None
        ev_tk_prev = None
        ev_tk_prev2 = None
        ev_comb_all = None
        for tt in range(NTT):
            def gmm(E, tt=tt):
                out = None
                for dc in range(NDC):
                    out = E.matmul(ps[6][:, 0:NEXP],
                                   xnh_t[:, dc * HALF + tt * 128: dc * HALF + (tt + 1) * 128],
                                   gate_bf[:, dc * NEXP:(dc + 1) * NEXP],
                                   start=(dc == 0), stop=(dc == NDC - 1))
                return out
            ev_gm = mk.step(T, [ev_xnh, ev_gc, ps6b_free], gmm)
            ev_sig = mk.step(S, [ev_gm, ev_tk_prev], lambda E: E.activation(
                out=sgt[:], in_=ps[6][:, 0:NEXP], func=AF.Sigmoid))
            ps6b_free = ev_sig

            emits = [lambda E: E.tensor_copy(gsc[:], sgt[:])]
            for r in range(TOPK):
                emits.append(lambda E: E.reduce_max(gm[:, 0:1], gsc[:], axis=AX.X))
                if r < TOPK - 1:
                    emits.append(lambda E: E.tensor_single_scalar(
                        comb_t[:], gsc[:], gm[:, 0:1], ALU.is_ge))
                    emits.append(lambda E: E.scalar_tensor_tensor(
                        gsc[:], comb_t[:], -1e30, gsc[:], ALU.mult, ALU.add))
            emits += [
                lambda E: E.tensor_single_scalar(comb_t[:], sgt[:], gm[:, 0:1], ALU.is_ge),
                lambda E: E.tensor_mul(comb_t[:], comb_t[:], sgt[:]),
                lambda E: E.reduce_sum(gm[:, 1:2], comb_t[:], axis=AX.X),
                lambda E: E.tensor_scalar(gm[:, 2:3], gm[:, 1:2], 1e-20, None, ALU.add),
                lambda E: E.reciprocal(gm[:, 3:4], gm[:, 2:3]),
                lambda E: E.tensor_single_scalar(comb_t[:], comb_t[:], gm[:, 3:4], ALU.mult),
            ]
            ev_tk = mk.chain(V, [ev_sig, ev_tk_prev2], emits)
            ev_tk_prev = ev_tk
            ev_tk_prev2 = ev_ct if False else None
            ev_ct = mk.step(T, [ev_tk, ps5_free],
                            lambda E: E.transpose(ps[5][0:NEXP, 0:128], comb_t[:],
                                                  ident_f32[:]))
            ev_tk_prev2 = ev_ct
            ev_ce = mk.step(V, [ev_ct], lambda E, tt=tt:
                            E.tensor_copy(combf[0:NEXP, tt * 128:(tt + 1) * 128],
                                          ps[5][0:NEXP, 0:128]))
            ps5_free = ev_ce
            ev_comb_all = ev_ce

        if half == 0:
            pass
        mk.fence_all()
        gsc_scope.close()
        wgu = [sbb(f"wgu{i}", [128, NDC * FEXP], BF16) for i in range(2)]
        wdsl = [sbb(f"wdsl{i}", [128, NFC * 512], BF16) for i in range(2)]
        actp = [sbb(f"actp{i}", [128, NFC * HALF], BF16) for i in range(2)]
        sgb = sbb("sgb", [128, NTG * 512], BF16)
        ub = sbb("ub", [128, 512], BF16)
        cbc = [sbb(f"cbc{i}", [128, HALF], BF16) for i in range(1)]
        ev_sel = mk.step(V, [], lambda E: E.memset(selm[:], 0.0))
        sel_cur = [None]
        ev_selmm = [None]

        wgu_free = [None, None]
        wd_free = [None] * 2
        cbc_free = [None]
        actp_free = [None, None]
        gu_bank_free = [None] * 4
        ybank_free = [None, None]
        ybank_free[1] = ps5_free
        ev_y_evac = {}

        for e in range(NEXP):
            if half == 0:
                pump.tick((len(pump.jobs) - pump.i + (NEXP - e) - 1)
                          // max(1, (NEXP - e)))
            first_pass = (e == 0)
            ap_i = e % 2
            nq = max(1, NDC // 4)
            ev_wg, ev_wu = [], []
            for ix, qq in enumerate(range(0, NDC, nq)):
                ev_wg.append(mk.dma(Y if ix % 2 == 0 else S,
                                    [wgu_free[0], pump.done[("wg", e)]],
                                    wgu[0][:].rearrange("p (dc f) -> p dc f", f=FEXP)
                                    [:, qq:qq + nq, :],
                                    WG_BF[e, qq:qq + nq].transpose([1, 0, 2])))
            for ix, qq in enumerate(range(0, NDC, nq)):
                ev_wu.append(mk.dma(Y if ix % 2 == 1 else S,
                                    [wgu_free[1], pump.done[("wu", e)]],
                                    wgu[1][:].rearrange("p (dc f) -> p dc f", f=FEXP)
                                    [:, qq:qq + nq, :],
                                    WU_BF[e, qq:qq + nq].transpose([1, 0, 2])))
            ev_selrow = mk.step(V, [ev_sel, ev_selmm[0]],
                                lambda E, e=e: E.tensor_scalar(
                selm[:], iota_p[:, 0:1].to_broadcast([128, 128]),
                float(e), None, ALU.is_equal))
            last_gmm = None
            for fc in range(NFC):
                for tg in range(NTG):
                    bk = (NTG * fc + tg) % 4
                    bank = ps[bk]

                    def gmm(E, fc=fc, tg=tg, bank=bank):
                        out = None
                        for dc in range(NDC):
                            out = E.matmul(
                                bank[:],
                                wgu[0][:, dc * FEXP + fc * 128: dc * FEXP + (fc + 1) * 128],
                                xnh_t[:, dc * HALF + tg * 512: dc * HALF + (tg + 1) * 512],
                                start=(dc == 0), stop=(dc == NDC - 1))
                        return out
                    evb = mk.step(T, ev_wg + [ev_xnh, gu_bank_free[bk]], gmm)
                    last_gmm = evb
                    ev_si = mk.step(S, [evb] +
                                    ([actp_free[ap_i]] if (fc == 0 and tg == 0) else []),
                                    lambda E, fc=fc, tg=tg, bank=bank, ap_i=ap_i:
                                    E.activation(
                        out=actp[ap_i][:, fc * HALF + tg * 512: fc * HALF + (tg + 1) * 512],
                        in_=bank[:], func=AF.Silu))
                    gu_bank_free[bk] = ev_si
            wgu_free[0] = last_gmm
            ev_cb = None
            for tg in range(NTG):
                evb = mk.step(T, [ev_comb_all, ps6b_free, ev_selrow] +
                              ([cbc_free[0]] if tg == 0 else []),
                              lambda E, tg=tg:
                              E.matmul(ps[6][:], selm[0:NEXP, :],
                                       combf[0:NEXP, tg * 512:(tg + 1) * 512],
                                       start=True, stop=True))
                ev_cb = mk.step(V, [evb], lambda E, tg=tg:
                                E.tensor_copy(cbc[0][:, tg * 512:(tg + 1) * 512],
                                              ps[6][:]))
                ps6b_free = ev_cb
            ev_selmm[0] = evb
            last_uv = None
            for fc in range(NFC):
                for tg in range(NTG):
                    bk = (NTG * fc + tg) % 4
                    bank = ps[bk]

                    def umm(E, fc=fc, tg=tg, bank=bank):
                        out = None
                        for dc in range(NDC):
                            out = E.matmul(
                                bank[:],
                                wgu[1][:, dc * FEXP + fc * 128: dc * FEXP + (fc + 1) * 128],
                                xnh_t[:, dc * HALF + tg * 512: dc * HALF + (tg + 1) * 512],
                                start=(dc == 0), stop=(dc == NDC - 1))
                        return out
                    evb = mk.step(T, ev_wu + [ev_xnh, gu_bank_free[bk]], umm)
                    asl = actp[ap_i][:, fc * HALF + tg * 512: fc * HALF + (tg + 1) * 512]
                    ev_uc = mk.chain(V, [evb, ev_cb, gu_bank_free[bk], last_uv], [
                        lambda E, tg=tg, bank=bank:
                            E.tensor_mul(ub[:, 0:512], bank[:],
                                         cbc[0][:, tg * 512:(tg + 1) * 512]),
                        lambda E, asl=asl:
                            E.tensor_mul(asl, asl, ub[:, 0:512]),
                    ])
                    gu_bank_free[bk] = ev_uc
                    last_uv = ev_uc
            wgu_free[1] = evb
            ev_act = last_uv
            cbc_free[0] = last_uv
            lastd = None
            for dg in range(NDG):
                slot = dg % 2
                wd_free[slot] = mk.dma(
                    Y if dg % 2 == 0 else S, [wd_free[slot], pump.done[("wd", e)]],
                    wdsl[slot][:].rearrange("p (fc f) -> p fc f", f=512),
                    WD_BF[e, :, :, dg * 512:(dg + 1) * 512].transpose([1, 0, 2]))
                for tt in range(NTT):
                    ybank = ps[4 + tt % 2]

                    def ymm(E, dg=dg, tt=tt, ybank=ybank, slot=slot, ap_i=ap_i):
                        out = None
                        for fc in range(NFC):
                            out = E.matmul(
                                ybank[:],
                                actp[ap_i][:, fc * HALF + tt * 128: fc * HALF + (tt + 1) * 128],
                                wdsl[slot][:, fc * 512:(fc + 1) * 512],
                                start=(fc == 0), stop=(fc == NFC - 1))
                        return out
                    ev_ym = mk.step(T, [ev_act, wd_free[slot], ybank_free[tt % 2]], ymm)
                    ysl = y_acc[:, tt * D + dg * 512: tt * D + (dg + 1) * 512]
                    if first_pass:
                        ev_ye = mk.step(V, [ev_ym], lambda E, ysl=ysl, ybank=ybank:
                                        E.tensor_copy(ysl, ybank[:]))
                    else:
                        ev_ye = mk.step(V, [ev_ym, ev_y_evac.get((tt, dg))],
                                        lambda E, ysl=ysl, ybank=ybank:
                                        E.tensor_add(ysl, ysl, ybank[:]))
                    ev_y_evac[(tt, dg)] = ev_ye
                    ybank_free[tt % 2] = ev_ye
                    lastd = ev_ym
                wd_free[slot] = lastd
            actp_free[ap_i] = lastd
        pump.finish()
        mk.fence_all()
        b.close()

        bs = ExitStack()

        def sbs(name, shape, dtype):
            return bs.enter_context(nc.sbuf_tensor(name + str(half), shape, dtype))

        NSH = c.NSC // 2
        FSH2 = NSH * 128
        sgh = sbs("sgh", [128, NDC * FSH2], BF16)
        suh = sbs("suh", [128, NDC * FSH2], BF16)
        act_sh = sbs("act_sh", [128, c.NSC * HALF], BF16)
        sdsl = [sbs(f"sdsl{i}", [128, c.NSC * 512], BF16) for i in range(2)]

        sh_bank_free = [None] * 4
        shscr_free = None
        ev_shact = None
        sgh_free = None
        for sh in range(2):
            nq = max(1, NDC // 4)
            ev_sg, ev_su = [], []
            for qq in range(0, NDC, nq):
                ev_sg.append(mk.dma(Y, [sgh_free, pump.done[("sg", sh)]],
                                    sgh[:].rearrange("p (dc f) -> p dc f", f=FSH2)
                                    [:, qq:qq + nq, :],
                                    SG_BF[sh, qq:qq + nq].transpose([1, 0, 2])))
            for qq in range(0, NDC, nq):
                ev_su.append(mk.dma(Y, [sgh_free, pump.done[("su", sh)]],
                                    suh[:].rearrange("p (dc f) -> p dc f", f=FSH2)
                                    [:, qq:qq + nq, :],
                                    SU_BF[sh, qq:qq + nq].transpose([1, 0, 2])))
            last_h_mm = None
            for fc in range(NSH):
                sc = sh * NSH + fc
                bank_evs = []
                for mi in range(2):
                    wsb = sgh if mi == 0 else suh
                    wdep = ev_sg if mi == 0 else ev_su
                    for tg in range(NTG):
                        bank = ps[2 * mi + (tg % 2)]

                        def hmm(E, wsb=wsb, fc=fc, tg=tg, bank=bank):
                            out = None
                            for dc in range(NDC):
                                out = E.matmul(
                                    bank[:],
                                    wsb[:, dc * FSH2 + fc * 128: dc * FSH2 + (fc + 1) * 128],
                                    xnh_t[:, dc * HALF + tg * 512: dc * HALF + (tg + 1) * 512],
                                    start=(dc == 0), stop=(dc == NDC - 1))
                            return out
                        evb = mk.step(T, wdep + [ev_xnh,
                                                  sh_bank_free[2 * mi + (tg % 2)]], hmm)
                        bank_evs.append(evb)
                        last_h_mm = evb
                ev_si = mk.step(S, bank_evs[0:NTG] +
                                ([shscr_free] if shscr_free else []),
                                lambda E, sc=sc: [
                    E.activation(out=act_sh[:, sc * HALF + tg * 512: sc * HALF + (tg + 1) * 512],
                                 in_=ps[tg % 2][:], func=AF.Silu)
                    for tg in range(NTG)])
                ev_ac = mk.step(V, bank_evs[NTG:2 * NTG] + [ev_si],
                                lambda E, sc=sc: [
                    E.tensor_mul(act_sh[:, sc * HALF + tg * 512: sc * HALF + (tg + 1) * 512],
                                 act_sh[:, sc * HALF + tg * 512: sc * HALF + (tg + 1) * 512],
                                 ps[2 + (tg % 2)][:])
                    for tg in range(NTG)])
                for z in range(2):
                    sh_bank_free[z] = ev_si
                    sh_bank_free[2 + z] = ev_ac
                shscr_free = ev_ac
                ev_shact = ev_ac
            sgh_free = last_h_mm
        sd_free = [None, None]
        for dg in range(NDG):
            slot = dg % 2
            nq = max(1, c.NSC // 3)
            ev_sdw = []
            for qq in range(0, c.NSC, nq):
                ev_sdw.append(mk.dma(Y, [sd_free[slot], pump.done[("sd", 0)]],
                                     sdsl[slot][:].rearrange("p (fc f) -> p fc f", f=512)
                                     [:, qq:qq + nq, :],
                                     SD_BF[qq:qq + nq, :, dg * 512:(dg + 1) * 512]
                                     .transpose([1, 0, 2])))
            lastsd = None
            for tt in range(NTT):
                ybank = ps[4 + tt % 2]

                def smm(E, dg=dg, tt=tt, ybank=ybank, slot=slot):
                    out = None
                    for fc in range(c.NSC):
                        out = E.matmul(
                            ybank[:],
                            act_sh[:, fc * HALF + tt * 128: fc * HALF + (tt + 1) * 128],
                            sdsl[slot][:, fc * 512:(fc + 1) * 512],
                            start=(fc == 0), stop=(fc == c.NSC - 1))
                    return out
                ev_ym = mk.step(T, [ev_shact] + ev_sdw + [ybank_free[tt % 2]], smm)
                ysl = y_acc[:, tt * D + dg * 512: tt * D + (dg + 1) * 512]
                ev_ye = mk.step(V, [ev_ym, ev_y_evac.get((tt, dg))],
                                lambda E, ysl=ysl, ybank=ybank:
                                E.tensor_add(ysl, ysl, ybank[:]))
                ev_y_evac[(tt, dg)] = ev_ye
                ybank_free[tt % 2] = ev_ye
                lastsd = ev_ym
            sd_free[slot] = lastsd
        mk.fence_all()
        bs.close()

        cc = ExitStack()

        def sbc(name, shape, dtype):
            return cc.enter_context(nc.sbuf_tensor(name + str(half), shape, dtype))

        x_t = [sbc(f"cx{i}", [128, D], F32) for i in range(2)]
        at_t = [sbc(f"cat{i}", [128, D], F32) for i in range(2)]
        o_t = [sbc(f"cot{i}", [128, D], F32) for i in range(2)]
        xt_free = [None, None]
        at_free = [None, None]
        ot_free = [None, None]
        for tt in range(NTT):
            sl = tt % 2
            row0 = half * HALF + tt * 128
            ev_x = mk.dma(Y, [xt_free[sl]], x_t[sl][:],
                          X[c.LB + row0: c.LB + row0 + 128, :])
            ev_a = mk.dma(Y, [at_free[sl]], at_t[sl][:], ATTN[row0:row0 + 128, :])
            deps = [ev_x, ev_a, ot_free[sl]] + \
                   [ev_y_evac[(tt, dg)] for dg in range(NDG)]
            ev_s = mk.chain(V, deps, [
                lambda E, sl=sl: E.tensor_add(o_t[sl][:], x_t[sl][:], at_t[sl][:]),
                lambda E, sl=sl, tt=tt: E.tensor_add(o_t[sl][:], o_t[sl][:],
                                                     y_acc[:, tt * D:(tt + 1) * D]),
            ])
            xt_free[sl] = ev_s
            at_free[sl] = ev_s
            ev_od = mk.dma(Y, [ev_s], OUT[row0:row0 + 128, :], o_t[sl][:])
            ot_free[sl] = ev_od
            ev_out_final.append(ev_od)
        cc.close()

    for _h in range(2):
        build_half(_h)

    yacc_scope.close()

    for k, eng in mk.dma_eng.items():
        mk._wait(eng, (k, mk.cnt[k]))
    mk.step(Y, ev_out_final, lambda E: E.nop())
    mk.build_blocks()
    return nc


def shard_inputs(cfg, x, wqT, wkT, wvT, woT, gate_r, wg, wu, wd, sg, su, sd):
    in_maps = []
    ncores = cfg.B * cfg.NQ
    for core in range(ncores):
        bidx, q = core // cfg.NQ, core % cfg.NQ
        xp = np.zeros((cfg.SPT, cfg.D), np.float32)
        xp[cfg.LB - q * cfg.OWN:] = x[bidx, :(q + 1) * cfg.OWN]
        pos = np.zeros((1, cfg.SPT), np.float32)
        pos[0, cfg.LB - q * cfg.OWN:] = np.arange((q + 1) * cfg.OWN, dtype=np.float32)
        in_maps.append({
            "x": xp, "pos": pos, "wqT": wqT, "wkT": wkT, "wvT": wvT, "woT": woT,
            "gate_r": gate_r, "wg": wg, "wu": wu, "wd": wd,
            "sg": sg, "su": su, "sd": sd,
        })
    return in_maps


def prep_weights(cfg, wq, wk, wv, wo, gate_w, wg, wu, wd, sg, su, sd):
    f = np.float32
    gate = np.asarray(gate_w, f)
    gate_r = np.ascontiguousarray(
        gate.reshape(cfg.NDC, 128, cfg.NEXP).transpose(1, 0, 2)
        .reshape(128, cfg.NDC * cfg.NEXP))
    return dict(
        wqT=np.ascontiguousarray(np.asarray(wq, f).T),
        wkT=np.ascontiguousarray(np.asarray(wk, f).T),
        wvT=np.ascontiguousarray(np.asarray(wv, f).T),
        woT=np.ascontiguousarray(np.asarray(wo, f).T),
        gate_r=gate_r,
        wg=np.ascontiguousarray(np.asarray(wg, f)),
        wu=np.ascontiguousarray(np.asarray(wu, f)),
        wd=np.ascontiguousarray(np.asarray(wd, f)),
        sg=np.ascontiguousarray(np.asarray(sg, f)),
        su=np.ascontiguousarray(np.asarray(su, f)),
        sd=np.ascontiguousarray(np.asarray(sd, f)),
    )


def kernel(L0_attn_x, L0_attn_wq, L0_attn_wk, L0_attn_wv, L0_attn_wo,
           L0_ffn_gate_w, L0_ffn_wg, L0_ffn_wu, L0_ffn_wd,
           L0_ffn_sg, L0_ffn_su, L0_ffn_sd):
    cfg = CFG()
    x = np.asarray(L0_attn_x, dtype=np.float32)
    w = prep_weights(cfg, L0_attn_wq, L0_attn_wk, L0_attn_wv, L0_attn_wo,
                     L0_ffn_gate_w, L0_ffn_wg, L0_ffn_wu, L0_ffn_wd,
                     L0_ffn_sg, L0_ffn_su, L0_ffn_sd)
    nc = bass.Bass("TRN2", target_bir_lowering=False)
    build_program(nc, cfg)
    in_maps = shard_inputs(cfg, x, **w)
    res = run_bass_kernel_spmd(nc, in_maps, core_ids=list(range(8)))
    out = np.empty((cfg.B, cfg.S, cfg.D), np.float32)
    for core in range(8):
        bidx, q = core // cfg.NQ, core % cfg.NQ
        out[bidx, q * cfg.OWN:(q + 1) * cfg.OWN] = res.results[core]["out"]
    return out


# version 15
# speedup vs baseline: 1.0265x; 1.0130x over previous
import math
import numpy as np
from contextlib import ExitStack

import concourse.bass as bass
import concourse.mybir as mybir
from concourse.bass_utils import run_bass_kernel_spmd

F32 = mybir.dt.float32
BF16 = mybir.dt.bfloat16
I32 = mybir.dt.int32
AF = mybir.ActivationFunctionType
ALU = mybir.AluOpType
AX = mybir.AxisListType

ENGINES = ("sync", "scalar", "vector", "tensor", "gpsimd")
N_DMA_SEMS = 24


class MK:

    def __init__(self, nc):
        self.nc = nc
        self.es = ExitStack()
        self.sems = {e: self.es.enter_context(nc.semaphore(f"mk_{e}")) for e in ENGINES}
        for i in range(N_DMA_SEMS):
            self.sems[i] = self.es.enter_context(nc.semaphore(f"mkd_{i}"))
        self.cnt = {k: 0 for k in self.sems}
        self.dma_rr = 0
        self.dma_eng = {}
        self.waited = {}
        self.phases = []
        self.cur = None
        self.phase()

    def phase(self):
        if self.cur is not None:
            self.fence_dma()
        self.cur = {e: [] for e in ENGINES}
        self.phases.append(self.cur)
        for cns in ENGINES:
            for k in self.sems:
                self.waited[(cns, k)] = self.cnt[k]

    def _wait(self, eng, ev):
        if ev is None:
            return
        k, val = ev
        if self.waited.get((eng, k), 0) >= val:
            return
        self.waited[(eng, k)] = val
        sem = self.sems[k]
        self.cur[eng].append(lambda E, sem=sem, val=val: E.wait_ge(sem, val))

    def step(self, eng, deps, emit, inc=1, key=None):
        for d in deps:
            self._wait(eng, d)
        k = eng if key is None else key
        self.cnt[k] += inc
        sem = self.sems[k]

        def run(E, emit=emit, sem=sem, inc=inc):
            last = emit(E)
            if isinstance(last, (list, tuple)):
                last = last[-1]
            last.then_inc(sem, inc)
        self.cur[eng].append(run)
        return (k, self.cnt[k])

    def chain(self, eng, deps, emits, extra=None):
        ev = None
        for i, em in enumerate(emits):
            d = list(deps if i == 0 else []) + ([ev] if ev else [])
            if extra and extra[i]:
                d += list(extra[i])
            ev = self.step(eng, d, em)
        return ev

    def fence_dma(self):
        for k, eng in self.dma_eng.items():
            self._wait(eng, (k, self.cnt[k]))

    def fence_all(self):
        self.fence_dma()
        for e in ENGINES:
            if self.cnt[e]:
                self._wait("sync", (e, self.cnt[e]))

    def dma(self, eng, deps, out, in_, **kw):
        k = self.dma_rr % N_DMA_SEMS
        self.dma_rr += 1
        self.dma_eng[k] = eng
        self._wait(eng, (k, self.cnt[k]))
        return self.step(eng, deps,
                         lambda E: E.dma_start(out=out, in_=in_, **kw),
                         inc=16, key=k)

    def build_blocks(self):
        for ph in self.phases:
            with self.nc.Block() as blk:
                for e in ENGINES:
                    ops = ph[e]
                    if not ops:
                        continue
                    getattr(blk, e)(lambda E, ops=ops: [f(E) for f in ops])


class CFG:
    def __init__(self, D=2048, NH=16, NKV=4, FEXP=768, NEXP=32, FSH=1536,
                 TOPK=6, OWN=2048, LB=6144, NQ=4, B=2, S=8192, debug=False):
        self.debug = debug
        self.D, self.NH, self.NKV = D, NH, NKV
        self.FEXP, self.NEXP, self.FSH, self.TOPK = FEXP, NEXP, FSH, TOPK
        self.OWN, self.LB, self.NQ, self.B, self.S = OWN, LB, NQ, B, S
        self.DH = 128
        self.NDC = D // 128
        self.NFC = FEXP // 128
        self.NSC = FSH // 128
        self.GR = NH // NKV
        self.NG = NH // 4
        assert self.GR == 4 and self.NG == NKV
        self.SPT = OWN + LB
        self.NT = self.SPT // 128
        self.NT_OWN = OWN // 128
        self.NST = self.SPT // 512
        self.NST_OWN = OWN // 512
        self.HALF = OWN // 2
        self.NTT = self.HALF // 128
        self.NTG = self.HALF // 512
        self.NDG = D // 512
        assert self.NTG >= 1 and LB == (NQ - 1) * OWN
        self.lam = [2.0 ** (-8.0 * (h + 1) / NH) for h in range(NH)]
        self.blk = [math.exp(-l * 128.0) for l in self.lam]


def build_program(nc, c):
    D, NDC, NH, NKV = c.D, c.NDC, c.NH, c.NKV
    NG, NFC, NEXP, FEXP = c.NG, c.NFC, c.NEXP, c.FEXP
    NTT, NDG, NTG, HALF = c.NTT, c.NDG, c.NTG, c.HALF
    KW = NKV * 128
    QW = NH * 128

    X = nc.dram_tensor("x", [c.SPT, D], F32, kind="ExternalInput").ap()
    POS = nc.dram_tensor("pos", [1, c.SPT], F32, kind="ExternalInput").ap()
    WQT = nc.dram_tensor("wqT", [D, QW], F32, kind="ExternalInput").ap()
    WKT = nc.dram_tensor("wkT", [D, KW], F32, kind="ExternalInput").ap()
    WVT = nc.dram_tensor("wvT", [D, KW], F32, kind="ExternalInput").ap()
    WOT = nc.dram_tensor("woT", [QW, D], F32, kind="ExternalInput").ap()
    GATE = nc.dram_tensor("gate_r", [128, NDC * NEXP], F32, kind="ExternalInput").ap()
    WG = nc.dram_tensor("wg", [NEXP, D, FEXP], F32, kind="ExternalInput").ap()
    WU = nc.dram_tensor("wu", [NEXP, D, FEXP], F32, kind="ExternalInput").ap()
    WD = nc.dram_tensor("wd", [NEXP, FEXP, D], F32, kind="ExternalInput").ap()
    SG = nc.dram_tensor("sg", [D, c.FSH], F32, kind="ExternalInput").ap()
    SU = nc.dram_tensor("su", [D, c.FSH], F32, kind="ExternalInput").ap()
    SD = nc.dram_tensor("sd", [c.FSH, D], F32, kind="ExternalInput").ap()
    OUT = nc.dram_tensor("out", [c.OWN, D], F32, kind="ExternalOutput").ap()

    scr_kind = "ExternalOutput" if c.debug else "Internal"
    WG_BF = nc.dram_tensor("wg_bf_scr", [NEXP, NDC, 128, FEXP], BF16, kind="Internal").ap()
    WU_BF = nc.dram_tensor("wu_bf_scr", [NEXP, NDC, 128, FEXP], BF16, kind="Internal").ap()
    WD_BF = nc.dram_tensor("wd_bf_scr", [NEXP, NFC, 128, D], BF16, kind="Internal").ap()
    NSH2 = c.NSC // 2
    FSH2c = NSH2 * 128
    SG_BF = nc.dram_tensor("sg_bf_scr", [2, NDC, 128, FSH2c], BF16, kind="Internal").ap()
    SU_BF = nc.dram_tensor("su_bf_scr", [2, NDC, 128, FSH2c], BF16, kind="Internal").ap()
    SD_BF = nc.dram_tensor("sd_bf_scr", [c.NSC, 128, D], BF16, kind="Internal").ap()
    XN_SCR = nc.dram_tensor("xn_scr", [128, NDC * c.OWN], BF16, kind=scr_kind).ap()
    KVSNAP = nc.dram_tensor("kvsnap", [c.NT_OWN, 128, NH * 128], BF16, kind=scr_kind).ap()
    OFEAT = nc.dram_tensor("ofeat", [c.NT_OWN, 128, NH * 128], BF16, kind=scr_kind).ap()
    ATTN = nc.dram_tensor("attn_scr", [c.OWN, D], F32, kind=scr_kind).ap()

    ident_bf = nc.alloc_sbuf_tensor("ident_bf", [128, 128], BF16)
    ident_f32 = nc.alloc_sbuf_tensor("ident_f32", [128, 128], F32)
    iota_p = nc.alloc_sbuf_tensor("iota_p", [128, 1], F32)
    iota_row = nc.alloc_sbuf_tensor("iota_row", [128, 128], F32)
    ones_row = nc.alloc_sbuf_tensor("ones_row", [1, 128], F32)
    pi_col = nc.alloc_sbuf_tensor("pi_col", [128, 1], F32)
    eps_col = nc.alloc_sbuf_tensor("eps_col", [128, 1], F32)

    NSLOT = 4
    ps = [nc.alloc_psum_tensor(f"psb{i}", [128, 512], F32) for i in range(7)]
    ps7 = nc.alloc_psum_tensor("psb7", [128, 1024], BF16)

    mk = MK(nc)
    V, S, T, G, Y = "vector", "scalar", "tensor", "gpsimd", "sync"

    class Pump:
        def __init__(self):
            self.jobs = []
            self.done = {}
            self.i = 0
            self.stg_free = [None] * NSLOT
            self.bf_free = [None] * NSLOT
            self.ncast = 0

        def add(self, key, blocks):
            for b_ in blocks:
                self.jobs.append((key,) + b_)

        def tick(self, n):
            for _ in range(n):
                if self.i >= len(self.jobs):
                    return
                key, sap, dap, wdt = self.jobs[self.i]
                s = self.i % NSLOT
                led = (Y, G)[self.i % 2]
                ld = mk.dma(led, [self.stg_free[s]], pstg[s][:, 0:wdt], sap)
                if self.i % 2 == 0:
                    ce = mk.step(V, [ld, self.bf_free[s]],
                                 lambda E, s=s, wdt=wdt:
                                 E.tensor_copy(pbf[s][:, 0:wdt], pstg[s][:, 0:wdt]))
                else:
                    ce = mk.step(S, [ld, self.bf_free[s]],
                                 lambda E, s=s, wdt=wdt:
                                 E.copy(pbf[s][:, 0:wdt], pstg[s][:, 0:wdt]))
                self.stg_free[s] = ce
                st = mk.dma(S, [ce], dap, pbf[s][:, 0:wdt])
                self.bf_free[s] = st
                self.done[key] = st
                self.i += 1

        def finish(self):
            self.tick(len(self.jobs))

    pump = Pump()
    GB = max(1, min(NDC, 1024 // FEXP))
    for e in range(NEXP):
        pump.add(("wg", e), [
            (WG[e, dc * 128:(dc + GB) * 128, :].rearrange("(b p) f -> p b f", p=128),
             WG_BF[e, dc:dc + GB].transpose([1, 0, 2]),
             GB * FEXP)
            for dc in range(0, NDC, GB)])
        pump.add(("wu", e), [
            (WU[e, dc * 128:(dc + GB) * 128, :].rearrange("(b p) f -> p b f", p=128),
             WU_BF[e, dc:dc + GB].transpose([1, 0, 2]),
             GB * FEXP)
            for dc in range(0, NDC, GB)])
        WDW = min(1024, D)
        pump.add(("wd", e), [(WD[e, fc * 128:(fc + 1) * 128, hh * WDW:(hh + 1) * WDW],
                              WD_BF[e, fc, :, hh * WDW:(hh + 1) * WDW], WDW)
                             for fc in range(NFC) for hh in range(D // WDW)])
    for sh in range(2):
        pump.add(("sg", sh), [(SG[dc * 128:(dc + 1) * 128, sh * FSH2c:(sh + 1) * FSH2c],
                               SG_BF[sh, dc], FSH2c) for dc in range(NDC)])
        pump.add(("su", sh), [(SU[dc * 128:(dc + 1) * 128, sh * FSH2c:(sh + 1) * FSH2c],
                               SU_BF[sh, dc], FSH2c) for dc in range(NDC)])
    WDW2 = min(1024, D)
    pump.add(("sd", 0), [(SD[fc * 128:(fc + 1) * 128, hh * WDW2:(hh + 1) * WDW2],
                          SD_BF[fc, :, hh * WDW2:(hh + 1) * WDW2], WDW2)
                         for fc in range(c.NSC) for hh in range(D // WDW2)])
    PUMP_PER_TILE = (len(pump.jobs) + c.NT - 1) // c.NT

    a_scope = ExitStack()

    def sba(name, shape, dtype):
        return a_scope.enter_context(nc.sbuf_tensor(name, shape, dtype))

    pstg = [sba(f"pump_f{i}", [128, 1024], F32) for i in range(NSLOT)]
    pbf = [sba(f"pump_b{i}", [128, 1024], BF16) for i in range(NSLOT)]
    cos_own = sba("cos_o", [128, c.OWN], BF16)
    sin_own = sba("sin_o", [128, c.OWN], BF16)
    rotm = sba("rotm", [128, 128], BF16)
    kdec_cols = sba("kdec_cols", [128, NH], F32)
    k_own = sba("k_own", [128, NKV * c.OWN], BF16)
    v_own = sba("v_own", [128, c.NT_OWN * KW], BF16)

    lb_scope = ExitStack()
    cos_lb = lb_scope.enter_context(nc.sbuf_tensor("cos_lb", [128, c.LB], BF16))
    sin_lb = lb_scope.enter_context(nc.sbuf_tensor("sin_lb", [128, c.LB], BF16))

    def tabsl(which, lo, hi):
        if hi <= c.LB:
            return (cos_lb if which == 0 else sin_lb)[:, lo:hi]
        return (cos_own if which == 0 else sin_own)[:, lo - c.LB:hi - c.LB]

    p0 = ExitStack()
    pos_sb = p0.enter_context(nc.sbuf_tensor("pos_sb", [1, c.SPT], F32))
    invfreq = p0.enter_context(nc.sbuf_tensor("invfreq", [1, 128], F32))
    iv_raw = p0.enter_context(nc.sbuf_tensor("iv_raw", [1, 128], F32))
    iv_g = p0.enter_context(nc.sbuf_tensor("iv_g", [1, 128], F32))
    rampb = p0.enter_context(nc.sbuf_tensor("rampb", [128, 128], F32))
    scr0 = p0.enter_context(nc.sbuf_tensor("p0scr0", [128, 512], F32))
    scr1 = p0.enter_context(nc.sbuf_tensor("p0scr1", [128, 512], F32))
    scri = p0.enter_context(nc.sbuf_tensor("p0scri", [128, 512], I32))
    scrH = p0.enter_context(nc.sbuf_tensor("p0scrH", [128, 512], F32))
    scrS = p0.enter_context(nc.sbuf_tensor("p0scrS", [128, 512], F32))

    ev_iota = mk.step(G, [], lambda E: [
        E.iota(iota_p[:], [[1, 1]], base=0, channel_multiplier=1,
               allow_small_or_imprecise_dtypes=True),
        E.iota(iota_row[:], [[1, 128]], base=0, channel_multiplier=0,
               allow_small_or_imprecise_dtypes=True),
        E.iota(iv_raw[:], [[1, 128]], base=0, channel_multiplier=0,
               allow_small_or_imprecise_dtypes=True),
    ])
    ev_pos = mk.dma(Y, [], pos_sb[:], POS)
    ev_const = mk.step(V, [ev_iota], lambda E: [
        E.tensor_single_scalar(ident_bf[:], iota_row[:], iota_p[:, 0:1], ALU.is_equal),
        E.tensor_single_scalar(ident_f32[:], iota_row[:], iota_p[:, 0:1], ALU.is_equal),
        E.memset(ones_row[:], 1.0),
        E.memset(pi_col[:], math.pi),
        E.memset(eps_col[:], 1e-5),
    ])
    ev_ivm = mk.chain(V, [ev_iota, ev_const], [
        lambda E: E.tensor_single_scalar(iv_g[:], iv_raw[:], 64.0, ALU.is_ge),
        lambda E: E.scalar_tensor_tensor(iv_g[:], iv_g[:], -64.0, iv_raw[:],
                                         ALU.mult, ALU.add),
    ])
    ev_if = mk.step(S, [ev_ivm], lambda E:
                    E.activation(out=invfreq[:], in_=iv_g[:], func=AF.Exp,
                                 scale=-math.log(10000.0) / 64.0))
    ev_rotm = mk.chain(V, [ev_iota, ev_const], [
        lambda E: E.tensor_scalar(scr0[:, 0:1], iota_p[:, 0:1], 1.0, 64.0,
                                  ALU.mult, ALU.add),
        lambda E: E.tensor_single_scalar(scr1[:, 0:128], iota_row[:], scr0[:, 0:1],
                                         ALU.is_equal),
        lambda E: E.tensor_scalar(scr0[:, 0:1], iota_p[:, 0:1], 1.0, -64.0,
                                  ALU.mult, ALU.add),
        lambda E: E.tensor_single_scalar(scr1[:, 128:256], iota_row[:], scr0[:, 0:1],
                                         ALU.is_equal),
        lambda E: E.tensor_sub(rotm[:], scr1[:, 0:128], scr1[:, 128:256]),
    ])

    ev_tab_chain = []
    for i in range(c.SPT // 512):
        sl = slice(i * 512, (i + 1) * 512)
        bank = ps[i % 2]
        dep_bank = ev_tab_chain[-2][0] if len(ev_tab_chain) >= 2 else None
        evm = mk.step(T, [ev_if, ev_pos, dep_bank, ev_rotm],
                      lambda E, sl=sl, bank=bank:
                      E.matmul(bank[:], invfreq[0:1, :], pos_sb[0:1, sl],
                               start=True, stop=True))
        prev_s = ev_tab_chain[-1][1] if ev_tab_chain else None
        ev_last = None
        for which in (1, 0):
            shift = 0.0 if which == 1 else 0.5
            evf = mk.chain(V, [evm, ev_const, prev_s if which == 1 else ev_last], [
                lambda E, bank=bank, shift=shift: E.tensor_scalar(
                    scrH[:, 0:512], bank[:], 1.0 / math.pi, shift, ALU.mult, ALU.add),
                lambda E: E.tensor_copy(scri[:, 0:512], scrH[:, 0:512]),
                lambda E: E.tensor_copy(scr1[:, 0:512], scri[:, 0:512]),
                lambda E: E.tensor_sub(scr0[:, 0:512], scrH[:, 0:512], scr1[:, 0:512]),
                lambda E: E.tensor_single_scalar(scrS[:, 0:512], scr0[:, 0:512], 0.5,
                                                 ALU.is_gt),
                lambda E: E.tensor_single_scalar(scrH[:, 0:512], scr0[:, 0:512], -0.5,
                                                 ALU.is_lt),
                lambda E: E.tensor_sub(scrS[:, 0:512], scrS[:, 0:512], scrH[:, 0:512]),
                lambda E: E.tensor_sub(scr0[:, 0:512], scr0[:, 0:512], scrS[:, 0:512]),
                lambda E: E.tensor_add(scr1[:, 0:512], scr1[:, 0:512], scrS[:, 0:512]),
                lambda E: E.tensor_scalar(scrH[:, 0:512], scr1[:, 0:512], 0.5,
                                          None, ALU.mult),
                lambda E: E.tensor_copy(scri[:, 0:512], scrH[:, 0:512]),
                lambda E: E.tensor_copy(scrH[:, 0:512], scri[:, 0:512]),
                lambda E: E.scalar_tensor_tensor(scrH[:, 0:512], scrH[:, 0:512], -2.0,
                                                 scr1[:, 0:512], ALU.mult, ALU.add),
                lambda E: E.tensor_mul(scrH[:, 0:512], scrH[:, 0:512], scrH[:, 0:512]),
                lambda E: E.tensor_scalar(scrH[:, 0:512], scrH[:, 0:512], -2.0, 1.0,
                                          ALU.mult, ALU.add),
            ])
            evs0 = mk.step(S, [evf], lambda E:
                           E.activation(out=scrS[:, 0:512], in_=scr0[:, 0:512],
                                        func=AF.Sin, scale=math.pi))
            tdst = tabsl(which, sl.start, sl.stop)
            ev_last = mk.step(V, [evs0], lambda E, tdst=tdst:
                              E.tensor_mul(tdst, scrS[:, 0:512], scrH[:, 0:512]))
        ev_tab_chain.append((ev_last, ev_last))
    ev_tab_all = ev_tab_chain[-1][1]

    ev_h = ev_tab_chain[-1][0]
    for h in range(NH):
        lam = c.lam[h]
        e6 = mk.step(V, [ev_h], lambda E:
                     E.tensor_scalar(scr1[:, 0:1], iota_p[:, 0:1], -1.0, 127.0,
                                     ALU.mult, ALU.add))
        e7 = mk.step(S, [e6], lambda E, h=h, lam=lam: E.activation(
            out=kdec_cols[:, h:h + 1], in_=scr1[:, 0:1], func=AF.Exp, scale=-lam))
        ev_h = e7
    p0.close()

    mk.phase()
    a1 = ExitStack()

    def sb1(name, shape, dtype):
        return a1.enter_context(nc.sbuf_tensor(name, shape, dtype))

    nbn = 6 * ((D + 511) // 512)
    kv_sb = sb1("kv_sb", [128, NH * 128], F32)
    x_ring = [sb1(f"x_ring{i}", [128, D], F32) for i in range(2)]
    stats2 = [sb1(f"stats{i}", [128, 8], F32) for i in range(2)]
    bnrows2 = [sb1(f"bnrows{i}", [128, nbn], F32) for i in range(2)]
    xn_tok = [sb1(f"xn_tok{i}", [128, D], BF16) for i in range(1)]
    xn_st = [sb1(f"xn_st{i}", [128, NDC * 512], BF16) for i in range(2)]
    kraw = sb1("kraw", [128, NKV * 512], BF16)
    krot_scr = sb1("krot_scr", [128, NKV * 512], BF16)
    rsc = sb1("rsc", [128, 1024], BF16)
    kpos_t = [sb1(f"kpos{i}", [128, KW], BF16) for i in range(2)]
    kprime = sb1("kprime", [128, NH * 128], BF16)
    wk_bf = sb1("wk_bf", [128, NDC * KW], BF16)
    wv_bf = sb1("wv_bf", [128, NDC * KW], BF16)
    wstg = [p[:, 0:KW] for p in pstg]
    snap = [sb1(f"snap{i}", [128, NH * 128], BF16) for i in range(1)]
    v_scr = [sb1(f"vscr{i}", [128, KW], BF16) for i in range(2)]

    ev_kv0 = mk.step(V, [], lambda E: E.memset(kv_sb[:], 0.0))

    wfree = [None] * NSLOT
    last_cast = None
    for i, (W_, dst) in enumerate([(WKT, wk_bf), (WVT, wv_bf)]):
        for dc in range(NDC):
            s = (i * NDC + dc) % NSLOT
            ld = mk.dma(Y, [wfree[s]], wstg[s][:], W_[dc * 128:(dc + 1) * 128, :])
            ce = mk.step(V, [ld], lambda E, s=s, dst=dst, dc=dc:
                         E.tensor_copy(dst[:, dc * KW:(dc + 1) * KW], wstg[s][:]))
            wfree[s] = ce
            last_cast = ce
    ev_wkv = last_cast
    for s in range(NSLOT):
        pump.stg_free[s] = wfree[s]

    x_free = [None] * 2
    xn_free = [None, None]
    xnst_free = [[], []]
    kpos_free = [None, None]
    vbank_free = [None, None]
    vscr_free = [None, None]
    snap_free = [None]
    ps7_free = None
    ps6_free = None
    rot_bank_free = [None, None]
    kprime_free = None
    rsc_free = None
    ev_kv_upd = ev_kv0
    ev_k_prev_st = None
    stats_free = [None, None]
    kraw_free = None
    krot_free = None

    for st in range(c.NST):
        own_st = st >= c.NST - c.NST_OWN
        so = st - (c.NST - c.NST_OWN)
        xs = xn_st[st % 2]
        ev_tp_tiles = []
        for tl in range(4):
            t = st * 4 + tl
            pump.tick(PUMP_PER_TILE)
            xslot = t % 2
            stats = stats2[xslot]
            bnrows = bnrows2[xslot]
            ev_x = mk.dma(Y, [x_free[xslot]], x_ring[xslot][:],
                          X[t * 128:(t + 1) * 128, :])
            ngrp = (D + 511) // 512
            ev_bn = mk.chain(V, [ev_x, stats_free[xslot]], [
                lambda E, xslot=xslot, ngrp=ngrp, bnrows=bnrows: [
                    E.bn_stats(bnrows[:, g * 6:(g + 1) * 6],
                               x_ring[xslot][:, g * 512:min((g + 1) * 512, D)])
                    for g in range(ngrp)],
                lambda E, ngrp=ngrp, stats=stats, bnrows=bnrows:
                    E.bn_aggr(stats[:, 0:2], bnrows[:, 0:6 * ngrp]),
            ])
            ev_sq = mk.step(S, [ev_bn], lambda E, stats=stats: E.activation(
                out=stats[:, 5:6], in_=stats[:, 1:2], func=AF.Sqrt,
                bias=eps_col[:, 0:1]))
            ev_nmr = mk.chain(V, [ev_sq], [
                lambda E, stats=stats: E.reciprocal(stats[:, 2:3], stats[:, 5:6]),
                lambda E, stats=stats: E.tensor_scalar(stats[:, 3:4], stats[:, 0:1],
                                                       -1.0, None, ALU.mult),
                lambda E, stats=stats: E.tensor_mul(stats[:, 4:5], stats[:, 3:4],
                                                    stats[:, 2:3]),
            ])
            xnslot = 0
            ev_xn = mk.step(S, [ev_nmr, xn_free[xnslot]],
                            lambda E, xslot=xslot, xnslot=xnslot, stats=stats:
                            E.activation(
                out=xn_tok[xnslot][:], in_=x_ring[xslot][:], func=AF.Identity,
                scale=stats[:, 2:3], bias=stats[:, 4:5]))
            x_free[xslot] = ev_xn
            stats_free[xslot] = ev_xn
            ev_ev2 = None
            for hf in range(2):
                ndc_h = min(8, NDC - hf * 8)
                if ndc_h <= 0:
                    continue
                ev_tp = mk.step(T, [ev_xn, ps7_free] +
                                (xnst_free[st % 2] if (hf == 0 and tl == 0) else []),
                                lambda E, xnslot=xnslot, hf=hf, ndc_h=ndc_h: [
                    E.transpose(ps7[:, j * 128:(j + 1) * 128],
                                xn_tok[xnslot][:, (hf * 8 + j) * 128:(hf * 8 + j + 1) * 128],
                                ident_bf[:])
                    for j in range(ndc_h)])
                eng = V if hf == 0 else S
                if eng == V:
                    ev_ev2 = mk.step(V, [ev_tp],
                                     lambda E, xs=xs, tl=tl, hf=hf, ndc_h=ndc_h:
                                     E.tensor_copy(
                        xs[:].rearrange("p (dc t) -> p dc t", t=512)
                          [:, hf * 8:hf * 8 + ndc_h, tl * 128:(tl + 1) * 128],
                        ps7[:].rearrange("p (dc t) -> p dc t", t=128)[:, 0:ndc_h, :]))
                else:
                    ev_ev2 = mk.step(S, [ev_tp],
                                     lambda E, xs=xs, tl=tl, hf=hf, ndc_h=ndc_h:
                                     E.copy(
                        xs[:].rearrange("p (dc t) -> p dc t", t=512)
                          [:, hf * 8:hf * 8 + ndc_h, tl * 128:(tl + 1) * 128],
                        ps7[:].rearrange("p (dc t) -> p dc t", t=128)[:, 0:ndc_h, :]))
                ps7_free = ev_ev2
            xn_free[xnslot] = ev_ev2
            ev_tp_tiles.append(ev_ev2)
        ev_xn_dma = None
        if own_st:
            ev_xn_dma = mk.dma(Y, ev_tp_tiles,
                               XN_SCR[:].rearrange("p (dc t) -> p dc t", t=c.OWN)
                               [:, :, so * 512:(so + 1) * 512],
                               xs[:].rearrange("p (dc t) -> p dc t", t=512))

        ev_k_oc = []
        for oc in range(NKV):
            bank = ps[oc % 2]
            prev = ev_k_oc[oc - 2] if oc >= 2 else ev_k_prev_st

            def kmm(E, oc=oc, bank=bank, xs=xs):
                out = None
                for dc in range(NDC):
                    out = E.matmul(bank[:],
                                   wk_bf[:, dc * KW + oc * 128: dc * KW + (oc + 1) * 128],
                                   xs[:, dc * 512:(dc + 1) * 512],
                                   start=(dc == 0), stop=(dc == NDC - 1))
                return out
            evm = mk.step(T, ev_tp_tiles + [ev_wkv, prev], kmm)
            eve = mk.step(V, [evm, kraw_free if oc == 0 else None],
                          lambda E, oc=oc, bank=bank:
                          E.tensor_copy(kraw[:, oc * 512:(oc + 1) * 512], bank[:]))
            ev_k_oc.append(eve)
        if own_st:
            kout = k_own[:].rearrange("p (kv t) -> p kv t", t=c.OWN)[:, :, so * 512:(so + 1) * 512]
        else:
            kout = krot_scr[:].rearrange("p (kv t) -> p kv t", t=512)
        ev_krot = None
        for kv in range(NKV):
            cosl = tabsl(0, st * 512, (st + 1) * 512)
            sinl = tabsl(1, st * 512, (st + 1) * 512)
            p7s = ps[4 + kv % 2][:]
            ev_rt = mk.step(T, [ev_k_oc[kv], ev_rotm, rot_bank_free[kv % 2]],
                            lambda E, kv=kv, p7s=p7s:
                            E.matmul(p7s, rotm[:], kraw[:, kv * 512:(kv + 1) * 512],
                                     start=True, stop=True))
            ev_krot = mk.chain(V, [ev_rt, ev_tab_all, rsc_free, krot_free], [
                lambda E, kv=kv, cosl=cosl:
                    E.tensor_tensor(rsc[:, 0:512], kraw[:, kv * 512:(kv + 1) * 512],
                                    cosl, ALU.mult),
                lambda E, p7s=p7s, sinl=sinl:
                    E.tensor_tensor(rsc[:, 512:1024], p7s, sinl, ALU.mult),
                lambda E, kv=kv, kout=kout:
                    E.tensor_add(kout[:, kv], rsc[:, 0:512], rsc[:, 512:1024]),
            ])
            rot_bank_free[kv % 2] = ev_krot
            rsc_free = ev_krot
        kraw_free = ev_krot
        ev_k_prev_st = ev_k_oc[-1]

        ev_last_delta_tile = None
        for tl in range(4):
            t = st * 4 + tl
            vb = tl % 2
            vbank = ps[2 + vb]

            def vmm(E, xs=xs, tl=tl, vbank=vbank):
                out = None
                for dc in range(NDC):
                    out = E.matmul(vbank[:, 0:KW],
                                   xs[:, dc * 512 + tl * 128: dc * 512 + (tl + 1) * 128],
                                   wv_bf[:, dc * KW:(dc + 1) * KW],
                                   start=(dc == 0), stop=(dc == NDC - 1))
                return out
            evv = mk.step(T, [ev_tp_tiles[tl], ev_wkv, vbank_free[vb]], vmm)
            ev_last_vmm = evv
            if own_st:
                vcur = v_own[:, (so * 4 + tl) * KW:(so * 4 + tl + 1) * KW]
                evve = mk.step(V, [evv], lambda E, vcur=vcur, vbank=vbank:
                               E.tensor_copy(vcur, vbank[:, 0:KW]))
            else:
                vs_ = t % 2
                vcur = v_scr[vs_][:, 0:KW]
                evve = mk.step(V, [evv, vscr_free[vs_]],
                               lambda E, vcur=vcur, vbank=vbank:
                               E.tensor_copy(vcur, vbank[:, 0:KW]))
            vbank_free[vb] = evve

            kslot = t % 2
            ev_kt = mk.step(T, [ev_krot, kpos_free[kslot], ps7_free],
                            lambda E, kout=kout, tl=tl: [
                E.transpose(ps7[:, j * 128:(j + 1) * 128],
                            kout[:, j, tl * 128:(tl + 1) * 128], ident_bf[:])
                for j in range(NKV)])
            ev_kte = mk.step(S, [ev_kt], lambda E, kslot=kslot:
                             E.copy(kpos_t[kslot][:, 0:KW], ps7[:, 0:KW]))
            ps7_free = ev_kte
            ev_kp = mk.step(V, [ev_kte, kprime_free], lambda E, kslot=kslot: [
                E.tensor_single_scalar(kprime[:, h * 128:(h + 1) * 128],
                                       kpos_t[kslot][:, (h // 4) * 128:(h // 4 + 1) * 128],
                                       kdec_cols[:, h:h + 1], ALU.mult)
                for h in range(NH)])
            kpos_free[kslot] = ev_kp
            last_kt = ev_kt

            ev_sn = None
            if own_st:
                co = so * 4 + tl
                sslot = 0
                ev_sn = mk.step(V, [ev_kv_upd, snap_free[sslot]],
                                lambda E, sslot=sslot:
                                E.tensor_copy(snap[sslot][:], kv_sb[:]))
                ev_snd = mk.dma(Y, [ev_sn], KVSNAP[co], snap[sslot][:])
                snap_free[sslot] = ev_snd

            for g in range(NG):
                def dmm(E, g=g, vcur=vcur):
                    out = None
                    for hp in range(4):
                        out = E.matmul(ps[6][:, hp * 128:(hp + 1) * 128],
                                       kprime[:, (g * 4 + hp) * 128:(g * 4 + hp + 1) * 128],
                                       vcur[:, g * 128:(g + 1) * 128],
                                       start=True, stop=True)
                    return out
                evd = mk.step(T, [ev_kp, evve, ps6_free], dmm)
                ev_up = mk.step(V, [evd] + ([ev_sn] if ev_sn else []),
                                lambda E, g=g: [
                    E.scalar_tensor_tensor(
                        kv_sb[:, (g * 4 + hp) * 128:(g * 4 + hp + 1) * 128],
                        kv_sb[:, (g * 4 + hp) * 128:(g * 4 + hp + 1) * 128],
                        c.blk[g * 4 + hp],
                        ps[6][:, hp * 128:(hp + 1) * 128],
                        ALU.mult, ALU.add)
                    for hp in range(4)])
                ps6_free = ev_up
                ev_kv_upd = ev_up
                ev_last_delta_tile = ev_up
            kprime_free = ev_last_delta_tile
            if not own_st:
                vscr_free[t % 2] = ev_last_delta_tile
        xnst_free[st % 2] = [ev_last_vmm] + ([ev_xn_dma] if ev_xn_dma else [])
        krot_free = last_kt

    pump.finish()
    a1.close()
    lb_scope.close()

    mk.phase()
    a2 = ExitStack()

    def sb2(name, shape, dtype):
        return a2.enter_context(nc.sbuf_tensor(name, shape, dtype))

    mask_pack = sb2("mask_pack", [128, NG * 512], BF16)
    qdec_pack = sb2("qdec_pack", [128, NG * 512], BF16)

    mp0 = ExitStack()
    rampb = mp0.enter_context(nc.sbuf_tensor("rampb2", [128, 128], F32))
    mscr = mp0.enter_context(nc.sbuf_tensor("mscr", [128, 512], F32))
    ev_rb = mk.step(T, [], lambda E:
                    E.matmul(ps[2][:, 0:128], ones_row[0:1, :], iota_row[0:1, :],
                             start=True, stop=True))
    ev_rbs = mk.step(V, [ev_rb], lambda E:
                     E.tensor_copy(rampb[:], ps[2][:, 0:128]))
    ev_diff = mk.step(V, [ev_rbs], lambda E:
                      E.tensor_single_scalar(mscr[:, 0:128], rampb[:], iota_p[:, 0:1],
                                             ALU.subtract))
    ev_h2 = ev_diff
    for h in range(NH):
        lam = c.lam[h]
        g, hp = h // 4, h % 4
        msl = slice(g * 512 + hp * 128, g * 512 + (hp + 1) * 128)
        e1 = mk.step(V, [ev_h2], lambda E: [
            E.tensor_single_scalar(mscr[:, 128:256], mscr[:, 0:128], 0.0, ALU.is_ge),
            E.tensor_single_scalar(mscr[:, 256:384], mscr[:, 0:128], 0.0, ALU.max),
        ])
        e2 = mk.step(S, [e1], lambda E, lam=lam: E.activation(
            out=mscr[:, 384:512], in_=mscr[:, 256:384], func=AF.Exp, scale=-lam))
        e3 = mk.step(V, [e2], lambda E, msl=msl:
                     E.tensor_mul(mask_pack[:, msl], mscr[:, 384:512], mscr[:, 128:256]))
        e4 = mk.step(S, [ev_rbs, e3], lambda E, lam=lam: E.activation(
            out=mscr[:, 384:512], in_=rampb[:], func=AF.Exp, scale=-lam))
        e5 = mk.step(V, [e4], lambda E, msl=msl, lam=lam:
                     E.tensor_scalar(qdec_pack[:, msl], mscr[:, 384:512],
                                     math.exp(-lam), None, ALU.mult))
        ev_h2 = e5
    mp0.close()

    wq_bf = sb2("wq_bf", [128, NDC * QW], BF16)
    wstg2 = [pstg[0][:, 0:QW // 2], pstg[1][:, 0:QW // 2]]
    xn_q = [sb2(f"xn_q{i}", [128, NDC * 512], BF16) for i in range(1)]
    q_st = sb2("q_st", [128, NH * 512], BF16)
    q_rot = sb2("q_rot", [128, NH * 512], BF16)
    rsc2 = sb2("rsc2", [128, 1024], BF16)
    qpk = [sb2(f"qpk{i}", [128, 512], BF16) for i in range(2)]
    scm = [sb2(f"scm{i}", [128, 512], BF16) for i in range(2)]
    kvsn = [sb2(f"kvsn{i}", [128, NH * 128], BF16) for i in range(2)]
    o_stage = [sb2(f"ostage{i}", [128, NH * 128], BF16) for i in range(1)]

    wfree2 = [None, None]
    last2 = None
    for dc2 in range(2 * NDC):
        dc, hh = dc2 // 2, dc2 % 2
        s = dc2 % 2
        HW = QW // 2
        ld = mk.dma(Y, [wfree2[s]], wstg2[s][:],
                    WQT[dc * 128:(dc + 1) * 128, hh * HW:(hh + 1) * HW])
        ce = mk.step(V, [ld], lambda E, s=s, dc=dc, hh=hh, HW=HW:
                     E.tensor_copy(wq_bf[:, dc * QW + hh * HW: dc * QW + (hh + 1) * HW],
                                   wstg2[s][:]))
        wfree2[s] = ce
        last2 = ce
    ev_wq = last2

    xnq_free = [None]
    kvsn_free = [None, None]
    ost_free = [None]
    qpk_free = [None, None]
    scm_free = [None, None]
    opack_free = [None, None]
    ev_q_last = None
    ev_qrot_free = None
    ps6a_free = None
    rot2_free = [None, None]

    for so in range(c.NST_OWN):
        xq = xn_q[0]
        ev_xq = mk.dma(Y, [xnq_free[0]],
                       xq[:].rearrange("p (dc t) -> p dc t", t=512),
                       XN_SCR[:].rearrange("p (dc t) -> p dc t", t=c.OWN)
                       [:, :, so * 512:(so + 1) * 512])
        ev_q = []
        for oc in range(NH):
            bank = ps[oc % 2]
            prev = ev_q[oc - 2][1] if oc >= 2 else ev_q_last

            def qmm(E, oc=oc, bank=bank, xq=xq):
                out = None
                for dc in range(NDC):
                    out = E.matmul(
                        bank[:],
                        wq_bf[:, dc * QW + oc * 128: dc * QW + (oc + 1) * 128],
                        xq[:, dc * 512:(dc + 1) * 512],
                        start=(dc == 0), stop=(dc == NDC - 1))
                return out
            evm = mk.step(T, [ev_xq, ev_wq, prev], qmm)
            eve = mk.step(S, [evm, ev_qrot_free if oc == 0 else None],
                          lambda E, oc=oc, bank=bank:
                          E.copy(q_st[:, oc * 512:(oc + 1) * 512], bank[:]))
            ev_q.append((evm, eve))
        xnq_free[0] = ev_q[-1][0]
        ev_q_last = ev_q[-1][1]

        cosl = cos_own[:, so * 512:(so + 1) * 512]
        sinl = sin_own[:, so * 512:(so + 1) * 512]
        ev_qrot = None
        for h in range(NH):
            p7s = ps[4 + h % 2][:]
            ev_rt = mk.step(T, [ev_q[h][1], ev_rotm, rot2_free[h % 2]],
                            lambda E, h=h, p7s=p7s:
                            E.matmul(p7s, rotm[:], q_st[:, h * 512:(h + 1) * 512],
                                     start=True, stop=True))
            ev_qrot = mk.chain(V, [ev_rt, ev_qrot, ev_qrot_free if h < 2 else None], [
                lambda E, h=h, cosl=cosl:
                    E.tensor_tensor(rsc2[:, 0:512], q_st[:, h * 512:(h + 1) * 512],
                                    cosl, ALU.mult),
                lambda E, p7s=p7s, sinl=sinl:
                    E.tensor_tensor(rsc2[:, 512:1024], p7s, sinl, ALU.mult),
                lambda E, h=h:
                    E.tensor_add(q_rot[:, h * 512:(h + 1) * 512],
                                 rsc2[:, 0:512], rsc2[:, 512:1024]),
            ])
            rot2_free[h % 2] = ev_qrot
        ev_qrot_free = None

        last_mix = None
        for tl in range(4):
            co = so * 4 + tl
            kslot = co % 2
            ev_ksd = mk.dma(Y, [kvsn_free[kslot]], kvsn[kslot][:], KVSNAP[co])
            oslot = 0
            ev_o_evs = []
            for g in range(NG):
                qslot = g % 2
                ev_qp = mk.step(V, [ev_qrot, qpk_free[qslot]],
                                lambda E, qslot=qslot, g=g, tl=tl:
                                E.tensor_tensor(
                    qpk[qslot][:].rearrange("p (a t) -> p a t", t=128),
                    q_rot[:].rearrange("p (h t) -> p h t", t=512)
                    [:, 4 * g:4 * g + 4, tl * 128:(tl + 1) * 128],
                    qdec_pack[:, g * 512:(g + 1) * 512].rearrange("p (a t) -> p a t", t=128),
                    ALU.mult))
                ev_sc = mk.step(T, [ev_qrot, ps6a_free],
                                lambda E, g=g, co=co, tl=tl:
                                E.matmul(ps[6][:],
                                         k_own[:].rearrange("p (kv t) -> p kv t", t=c.OWN)
                                         [:, g, co * 128:(co + 1) * 128],
                                         q_rot[:].rearrange("p (h t) -> p h t", t=512)
                                         [:, 4 * g:4 * g + 4, tl * 128:(tl + 1) * 128],
                                         start=True, stop=True))
                sslot = g % 2
                ev_ms = mk.step(V, [ev_sc, scm_free[sslot]],
                                lambda E, sslot=sslot, g=g:
                                E.tensor_mul(scm[sslot][:], ps[6][:],
                                             mask_pack[:, g * 512:(g + 1) * 512]))
                ps6a_free = ev_ms
                obank = ps[2 + g % 2]

                def omm(E, g=g, co=co, sslot=sslot, qslot=qslot, kslot=kslot, obank=obank):
                    E.matmul(obank[:],
                             v_own[:, co * KW + g * 128: co * KW + (g + 1) * 128],
                             scm[sslot][:], start=True, stop=False)
                    out = None
                    for hp in range(4):
                        h = 4 * g + hp
                        out = E.matmul(obank[:, hp * 128:(hp + 1) * 128],
                                       kvsn[kslot][:, h * 128:(h + 1) * 128],
                                       qpk[qslot][:, hp * 128:(hp + 1) * 128],
                                       start=False, stop=(hp == 3))
                    return out
                ev_om = mk.step(T, [ev_ms, ev_qp, ev_ksd, opack_free[g % 2]], omm)
                scm_free[sslot] = ev_om
                qpk_free[qslot] = ev_om
                ev_oe = mk.step(S, [ev_om] + ([ost_free[oslot]] if g == 0 else []),
                                lambda E, g=g, oslot=oslot, obank=obank:
                                E.copy(o_stage[oslot][:, g * 512:(g + 1) * 512],
                                       obank[:]))
                opack_free[g % 2] = ev_oe
                ev_o_evs.append(ev_oe)
                last_mix = ev_om
            kvsn_free[kslot] = ev_o_evs[-1]
            ev_od = mk.dma(Y, ev_o_evs, OFEAT[co], o_stage[oslot][:])
            ost_free[oslot] = ev_od
        ev_qrot_free = last_mix
    a2.close()
    a_scope.close()

    mk.phase()
    a2b = ExitStack()

    def sb3(name, shape, dtype):
        return a2b.enter_context(nc.sbuf_tensor(name, shape, dtype))

    wo_bf = sb3("wo_bf", [128, NH * D], BF16)
    wstg3 = [sb3(f"wstg3_{i}", [128, D], F32) for i in range(2)]
    o_buf = [sb3(f"obuf{i}", [128, 4 * NH * 128], BF16) for i in range(2)]
    att_f = sb3("att_f", [128, 512], F32)
    att_stage = [sb3(f"attst{i}", [128, D], F32) for i in range(4)]

    wfree3 = [None, None]
    last3 = None
    for ec in range(NH):
        s = ec % 2
        ld = mk.dma(Y, [wfree3[s]], wstg3[s][:], WOT[ec * 128:(ec + 1) * 128, :])
        ce = mk.step(V, [ld], lambda E, s=s, ec=ec:
                     E.tensor_copy(wo_bf[:, ec * D:(ec + 1) * D], wstg3[s][:]))
        wfree3[s] = ce
        last3 = ce
    ev_wo = last3

    obuf_free = [None, None]
    attst_free = [None] * 4
    attf_free = None
    pstp_free = None
    abank_free = [None, None]
    for so in range(c.NST_OWN):
        oslot = so % 2
        ev_ob = mk.dma(Y, [obuf_free[oslot]],
                       o_buf[oslot][:].rearrange("p (cc f) -> p cc f", f=NH * 128),
                       OFEAT[so * 4:(so + 1) * 4].transpose([1, 0, 2]))
        last_am = None
        for oc in range(NDC):
            bank = ps[oc % 2]

            def amm(E, oc=oc, oslot=oslot, bank=bank):
                out = None
                for ec in range(NH):
                    out = E.matmul(bank[:],
                                   wo_bf[:, ec * D + oc * 128: ec * D + (oc + 1) * 128],
                                   o_buf[oslot][:].rearrange("p (cc f) -> p cc f", f=NH * 128)
                                   [:, :, ec * 128:(ec + 1) * 128],
                                   start=(ec == 0), stop=(ec == NH - 1))
                return out
            ev_am = mk.step(T, [ev_ob, ev_wo, abank_free[oc % 2]], amm)
            last_am = ev_am
            ev_ae = mk.step(V, [ev_am, attf_free], lambda E, bank=bank:
                            E.tensor_copy(att_f[:], bank[:]))
            abank_free[oc % 2] = ev_ae
            ev_tp = mk.step(T, [ev_ae, pstp_free], lambda E: [
                E.transpose(ps[4][:, j * 128:(j + 1) * 128],
                            att_f[:, j * 128:(j + 1) * 128], ident_f32[:])
                for j in range(4)])
            attf_free = ev_tp
            ev_te = mk.step(S, [ev_tp] + (attst_free if oc == 0 else []),
                            lambda E, oc=oc: [
                E.copy(att_stage[j][:, oc * 128:(oc + 1) * 128],
                       ps[4][:, j * 128:(j + 1) * 128])
                for j in range(4)])
            pstp_free = ev_te
        for j in range(4):
            t = so * 4 + j
            evd = mk.dma(Y, [pstp_free], ATTN[t * 128:(t + 1) * 128, :],
                         att_stage[j][:])
            attst_free[j] = evd
        obuf_free[oslot] = last_am
    a2b.close()

    TOPK = c.TOPK
    yacc_scope = ExitStack()
    y_acc = yacc_scope.enter_context(nc.sbuf_tensor("y_acc", [128, NTT * D], F32))
    xnh_t = yacc_scope.enter_context(nc.sbuf_tensor("xnh", [128, NDC * HALF], BF16))

    ev_out_final = []

    def build_half(half):
        mk.phase()
        b = ExitStack()

        def sbb(name, shape, dtype):
            return b.enter_context(nc.sbuf_tensor(name + str(half), shape, dtype))

        combf = sbb("combf", [max(NEXP, 2), HALF], BF16)
        selm = sbb("selm", [128, 128], BF16)
        gsc_scope = ExitStack()

        def sbg(name, shape, dtype):
            return gsc_scope.enter_context(nc.sbuf_tensor(name + str(half), shape, dtype))

        gstg = sbg("gstg", [128, NDC * NEXP], F32)
        gate_bf = sbg("gate_bf", [128, NDC * NEXP], BF16)
        sgt = sbg("sgt", [128, NEXP], F32)
        gsc = sbg("gsc", [128, NEXP], F32)
        gm = sbg("gm", [128, 4], F32)
        comb_t = sbg("comb_t", [128, NEXP], F32)

        ev_xnh = mk.dma(Y, [],
                        xnh_t[:].rearrange("p (dc t) -> p dc t", t=HALF),
                        XN_SCR[:].rearrange("p (dc t) -> p dc t", t=c.OWN)
                        [:, :, half * HALF:(half + 1) * HALF])
        ev_gld = mk.dma(Y, [], gstg[:], GATE)
        ev_gc = mk.step(V, [ev_gld], lambda E: E.tensor_copy(gate_bf[:], gstg[:]))

        ps6b_free = None
        ps5_free = None
        ev_tk_prev = None
        ev_tk_prev2 = None
        ev_comb_all = None
        for tt in range(NTT):
            def gmm(E, tt=tt):
                out = None
                for dc in range(NDC):
                    out = E.matmul(ps[6][:, 0:NEXP],
                                   xnh_t[:, dc * HALF + tt * 128: dc * HALF + (tt + 1) * 128],
                                   gate_bf[:, dc * NEXP:(dc + 1) * NEXP],
                                   start=(dc == 0), stop=(dc == NDC - 1))
                return out
            ev_gm = mk.step(T, [ev_xnh, ev_gc, ps6b_free], gmm)
            ev_sig = mk.step(S, [ev_gm, ev_tk_prev], lambda E: E.activation(
                out=sgt[:], in_=ps[6][:, 0:NEXP], func=AF.Sigmoid))
            ps6b_free = ev_sig

            emits = [lambda E: E.tensor_copy(gsc[:], sgt[:])]
            for r in range(TOPK):
                emits.append(lambda E: E.reduce_max(gm[:, 0:1], gsc[:], axis=AX.X))
                if r < TOPK - 1:
                    emits.append(lambda E: E.tensor_single_scalar(
                        comb_t[:], gsc[:], gm[:, 0:1], ALU.is_ge))
                    emits.append(lambda E: E.scalar_tensor_tensor(
                        gsc[:], comb_t[:], -1e30, gsc[:], ALU.mult, ALU.add))
            emits += [
                lambda E: E.tensor_single_scalar(comb_t[:], sgt[:], gm[:, 0:1], ALU.is_ge),
                lambda E: E.tensor_mul(comb_t[:], comb_t[:], sgt[:]),
                lambda E: E.reduce_sum(gm[:, 1:2], comb_t[:], axis=AX.X),
                lambda E: E.tensor_scalar(gm[:, 2:3], gm[:, 1:2], 1e-20, None, ALU.add),
                lambda E: E.reciprocal(gm[:, 3:4], gm[:, 2:3]),
                lambda E: E.tensor_single_scalar(comb_t[:], comb_t[:], gm[:, 3:4], ALU.mult),
            ]
            ev_tk = mk.chain(V, [ev_sig, ev_tk_prev2], emits)
            ev_tk_prev = ev_tk
            ev_tk_prev2 = ev_ct if False else None
            ev_ct = mk.step(T, [ev_tk, ps5_free],
                            lambda E: E.transpose(ps[5][0:NEXP, 0:128], comb_t[:],
                                                  ident_f32[:]))
            ev_tk_prev2 = ev_ct
            ev_ce = mk.step(V, [ev_ct], lambda E, tt=tt:
                            E.tensor_copy(combf[0:NEXP, tt * 128:(tt + 1) * 128],
                                          ps[5][0:NEXP, 0:128]))
            ps5_free = ev_ce
            ev_comb_all = ev_ce

        if half == 0:
            pass
        mk.fence_all()
        gsc_scope.close()
        wgu = [sbb(f"wgu{i}", [128, NDC * FEXP], BF16) for i in range(2)]
        wdsl = [sbb(f"wdsl{i}", [128, NFC * 512], BF16) for i in range(2)]
        actp = [sbb(f"actp{i}", [128, NFC * HALF], BF16) for i in range(2)]
        sgb = sbb("sgb", [128, NTG * 512], BF16)
        ub = sbb("ub", [128, 512], BF16)
        cbc = [sbb(f"cbc{i}", [128, HALF], BF16) for i in range(1)]
        ev_sel = mk.step(V, [], lambda E: E.memset(selm[:], 0.0))
        sel_cur = [None]
        ev_selmm = [None]

        wgu_free = [None, None]
        wd_free = [None] * 2
        cbc_free = [None]
        actp_free = [None, None]
        gu_bank_free = [None] * 4
        ybank_free = [None, None, None]
        ybank_free[1] = ps5_free
        ev_y_evac = {}

        ybank_free[1] = ps5_free
        ybank_free[2] = ps6b_free
        for e in range(NEXP):
            first_pass = (e == 0)
            ap_i = e % 2
            nq = max(1, NDC // 4)
            ev_wg, ev_wu = [], []
            for ix, qq in enumerate(range(0, NDC, nq)):
                ev_wg.append(mk.dma(Y if ix % 2 == 0 else S,
                                    [wgu_free[0], pump.done[("wg", e)]],
                                    wgu[0][:].rearrange("p (dc f) -> p dc f", f=FEXP)
                                    [:, qq:qq + nq, :],
                                    WG_BF[e, qq:qq + nq].transpose([1, 0, 2])))
            for ix, qq in enumerate(range(0, NDC, nq)):
                ev_wu.append(mk.dma(Y if ix % 2 == 1 else S,
                                    [wgu_free[1], pump.done[("wu", e)]],
                                    wgu[1][:].rearrange("p (dc f) -> p dc f", f=FEXP)
                                    [:, qq:qq + nq, :],
                                    WU_BF[e, qq:qq + nq].transpose([1, 0, 2])))
            ev_selrow = mk.step(V, [ev_sel, ev_selmm[0]],
                                lambda E, e=e: E.tensor_scalar(
                selm[:], iota_p[:, 0:1].to_broadcast([128, 128]),
                float(e), None, ALU.is_equal))
            last_gmm = None
            for fc in range(NFC):
                for tg in range(NTG):
                    bk = (NTG * fc + tg) % 4
                    bank = ps[bk]

                    def gmm(E, fc=fc, tg=tg, bank=bank):
                        out = None
                        for dc in range(NDC):
                            out = E.matmul(
                                bank[:],
                                wgu[0][:, dc * FEXP + fc * 128: dc * FEXP + (fc + 1) * 128],
                                xnh_t[:, dc * HALF + tg * 512: dc * HALF + (tg + 1) * 512],
                                start=(dc == 0), stop=(dc == NDC - 1))
                        return out
                    evb = mk.step(T, ev_wg + [ev_xnh, gu_bank_free[bk]], gmm)
                    last_gmm = evb
                    ev_si = mk.step(S, [evb] +
                                    ([actp_free[ap_i]] if (fc == 0 and tg == 0) else []),
                                    lambda E, fc=fc, tg=tg, bank=bank, ap_i=ap_i:
                                    E.activation(
                        out=actp[ap_i][:, fc * HALF + tg * 512: fc * HALF + (tg + 1) * 512],
                        in_=bank[:], func=AF.Silu))
                    gu_bank_free[bk] = ev_si
            wgu_free[0] = last_gmm
            ev_cb = None
            for tg in range(NTG):
                bkc = tg % 2
                evb = mk.step(T, [ev_comb_all, gu_bank_free[bkc], ev_selrow] +
                              ([cbc_free[0]] if tg == 0 else []),
                              lambda E, tg=tg, bkc=bkc:
                              E.matmul(ps[bkc][:], selm[0:NEXP, :],
                                       combf[0:NEXP, tg * 512:(tg + 1) * 512],
                                       start=True, stop=True))
                ev_cb = mk.step(V, [evb], lambda E, tg=tg, bkc=bkc:
                                E.tensor_copy(cbc[0][:, tg * 512:(tg + 1) * 512],
                                              ps[bkc][:]))
                gu_bank_free[bkc] = ev_cb
            ev_selmm[0] = evb
            last_uv = None
            for fc in range(NFC):
                for tg in range(NTG):
                    bk = (NTG * fc + tg) % 4
                    bank = ps[bk]

                    def umm(E, fc=fc, tg=tg, bank=bank):
                        out = None
                        for dc in range(NDC):
                            out = E.matmul(
                                bank[:],
                                wgu[1][:, dc * FEXP + fc * 128: dc * FEXP + (fc + 1) * 128],
                                xnh_t[:, dc * HALF + tg * 512: dc * HALF + (tg + 1) * 512],
                                start=(dc == 0), stop=(dc == NDC - 1))
                        return out
                    evb = mk.step(T, ev_wu + [ev_xnh, gu_bank_free[bk]], umm)
                    asl = actp[ap_i][:, fc * HALF + tg * 512: fc * HALF + (tg + 1) * 512]
                    ev_uc = mk.chain(V, [evb, ev_cb, gu_bank_free[bk], last_uv], [
                        lambda E, tg=tg, bank=bank:
                            E.tensor_mul(ub[:, 0:512], bank[:],
                                         cbc[0][:, tg * 512:(tg + 1) * 512]),
                        lambda E, asl=asl:
                            E.tensor_mul(asl, asl, ub[:, 0:512]),
                    ])
                    gu_bank_free[bk] = ev_uc
                    last_uv = ev_uc
            wgu_free[1] = evb
            ev_act = last_uv
            cbc_free[0] = last_uv
            lastd = None
            for dg in range(NDG):
                slot = dg % 2
                wd_free[slot] = mk.dma(
                    Y if dg % 2 == 0 else S, [wd_free[slot], pump.done[("wd", e)]],
                    wdsl[slot][:].rearrange("p (fc f) -> p fc f", f=512),
                    WD_BF[e, :, :, dg * 512:(dg + 1) * 512].transpose([1, 0, 2]))
                for tt in range(NTT):
                    ybk = (dg * NTT + tt) % 3
                    ybank = ps[4 + ybk]

                    def ymm(E, dg=dg, tt=tt, ybank=ybank, slot=slot, ap_i=ap_i):
                        out = None
                        for fc in range(NFC):
                            out = E.matmul(
                                ybank[:],
                                actp[ap_i][:, fc * HALF + tt * 128: fc * HALF + (tt + 1) * 128],
                                wdsl[slot][:, fc * 512:(fc + 1) * 512],
                                start=(fc == 0), stop=(fc == NFC - 1))
                        return out
                    ev_ym = mk.step(T, [ev_act, wd_free[slot], ybank_free[ybk]], ymm)
                    ysl = y_acc[:, tt * D + dg * 512: tt * D + (dg + 1) * 512]
                    if first_pass:
                        ev_ye = mk.step(V, [ev_ym], lambda E, ysl=ysl, ybank=ybank:
                                        E.tensor_copy(ysl, ybank[:]))
                    else:
                        ev_ye = mk.step(V, [ev_ym, ev_y_evac.get((tt, dg))],
                                        lambda E, ysl=ysl, ybank=ybank:
                                        E.tensor_add(ysl, ysl, ybank[:]))
                    ev_y_evac[(tt, dg)] = ev_ye
                    ybank_free[ybk] = ev_ye
                    lastd = ev_ym
                wd_free[slot] = lastd
            actp_free[ap_i] = lastd
        mk.fence_all()
        b.close()

        bs = ExitStack()

        def sbs(name, shape, dtype):
            return bs.enter_context(nc.sbuf_tensor(name + str(half), shape, dtype))

        NSH = c.NSC // 2
        FSH2 = NSH * 128
        sgh = sbs("sgh", [128, NDC * FSH2], BF16)
        suh = sbs("suh", [128, NDC * FSH2], BF16)
        act_sh = sbs("act_sh", [128, c.NSC * HALF], BF16)
        sdsl = [sbs(f"sdsl{i}", [128, c.NSC * 512], BF16) for i in range(2)]

        sh_bank_free = [None] * 4
        shscr_free = None
        ev_shact = None
        sgh_free = None
        for sh in range(2):
            nq = max(1, NDC // 4)
            ev_sg, ev_su = [], []
            for qq in range(0, NDC, nq):
                ev_sg.append(mk.dma(Y, [sgh_free, pump.done[("sg", sh)]],
                                    sgh[:].rearrange("p (dc f) -> p dc f", f=FSH2)
                                    [:, qq:qq + nq, :],
                                    SG_BF[sh, qq:qq + nq].transpose([1, 0, 2])))
            for qq in range(0, NDC, nq):
                ev_su.append(mk.dma(Y, [sgh_free, pump.done[("su", sh)]],
                                    suh[:].rearrange("p (dc f) -> p dc f", f=FSH2)
                                    [:, qq:qq + nq, :],
                                    SU_BF[sh, qq:qq + nq].transpose([1, 0, 2])))
            last_h_mm = None
            for fc in range(NSH):
                sc = sh * NSH + fc
                bank_evs = []
                for mi in range(2):
                    wsb = sgh if mi == 0 else suh
                    wdep = ev_sg if mi == 0 else ev_su
                    for tg in range(NTG):
                        bank = ps[2 * mi + (tg % 2)]

                        def hmm(E, wsb=wsb, fc=fc, tg=tg, bank=bank):
                            out = None
                            for dc in range(NDC):
                                out = E.matmul(
                                    bank[:],
                                    wsb[:, dc * FSH2 + fc * 128: dc * FSH2 + (fc + 1) * 128],
                                    xnh_t[:, dc * HALF + tg * 512: dc * HALF + (tg + 1) * 512],
                                    start=(dc == 0), stop=(dc == NDC - 1))
                            return out
                        evb = mk.step(T, wdep + [ev_xnh,
                                                  sh_bank_free[2 * mi + (tg % 2)]], hmm)
                        bank_evs.append(evb)
                        last_h_mm = evb
                ev_si = mk.step(S, bank_evs[0:NTG] +
                                ([shscr_free] if shscr_free else []),
                                lambda E, sc=sc: [
                    E.activation(out=act_sh[:, sc * HALF + tg * 512: sc * HALF + (tg + 1) * 512],
                                 in_=ps[tg % 2][:], func=AF.Silu)
                    for tg in range(NTG)])
                ev_ac = mk.step(V, bank_evs[NTG:2 * NTG] + [ev_si],
                                lambda E, sc=sc: [
                    E.tensor_mul(act_sh[:, sc * HALF + tg * 512: sc * HALF + (tg + 1) * 512],
                                 act_sh[:, sc * HALF + tg * 512: sc * HALF + (tg + 1) * 512],
                                 ps[2 + (tg % 2)][:])
                    for tg in range(NTG)])
                for z in range(2):
                    sh_bank_free[z] = ev_si
                    sh_bank_free[2 + z] = ev_ac
                shscr_free = ev_ac
                ev_shact = ev_ac
            sgh_free = last_h_mm
        sd_free = [None, None]
        for dg in range(NDG):
            slot = dg % 2
            nq = max(1, c.NSC // 3)
            ev_sdw = []
            for qq in range(0, c.NSC, nq):
                ev_sdw.append(mk.dma(Y, [sd_free[slot], pump.done[("sd", 0)]],
                                     sdsl[slot][:].rearrange("p (fc f) -> p fc f", f=512)
                                     [:, qq:qq + nq, :],
                                     SD_BF[qq:qq + nq, :, dg * 512:(dg + 1) * 512]
                                     .transpose([1, 0, 2])))
            lastsd = None
            for tt in range(NTT):
                ybk = (dg * NTT + tt) % 3
                ybank = ps[4 + ybk]

                def smm(E, dg=dg, tt=tt, ybank=ybank, slot=slot):
                    out = None
                    for fc in range(c.NSC):
                        out = E.matmul(
                            ybank[:],
                            act_sh[:, fc * HALF + tt * 128: fc * HALF + (tt + 1) * 128],
                            sdsl[slot][:, fc * 512:(fc + 1) * 512],
                            start=(fc == 0), stop=(fc == c.NSC - 1))
                    return out
                ev_ym = mk.step(T, [ev_shact] + ev_sdw + [ybank_free[ybk]], smm)
                ysl = y_acc[:, tt * D + dg * 512: tt * D + (dg + 1) * 512]
                ev_ye = mk.step(V, [ev_ym, ev_y_evac.get((tt, dg))],
                                lambda E, ysl=ysl, ybank=ybank:
                                E.tensor_add(ysl, ysl, ybank[:]))
                ev_y_evac[(tt, dg)] = ev_ye
                ybank_free[ybk] = ev_ye
                lastsd = ev_ym
            sd_free[slot] = lastsd
        mk.fence_all()
        bs.close()

        cc = ExitStack()

        def sbc(name, shape, dtype):
            return cc.enter_context(nc.sbuf_tensor(name + str(half), shape, dtype))

        x_t = [sbc(f"cx{i}", [128, D], F32) for i in range(2)]
        at_t = [sbc(f"cat{i}", [128, D], F32) for i in range(2)]
        o_t = [sbc(f"cot{i}", [128, D], F32) for i in range(2)]
        xt_free = [None, None]
        at_free = [None, None]
        ot_free = [None, None]
        for tt in range(NTT):
            sl = tt % 2
            row0 = half * HALF + tt * 128
            ev_x = mk.dma(Y, [xt_free[sl]], x_t[sl][:],
                          X[c.LB + row0: c.LB + row0 + 128, :])
            ev_a = mk.dma(Y, [at_free[sl]], at_t[sl][:], ATTN[row0:row0 + 128, :])
            deps = [ev_x, ev_a, ot_free[sl]] + \
                   [ev_y_evac[(tt, dg)] for dg in range(NDG)]
            ev_s = mk.chain(V, deps, [
                lambda E, sl=sl: E.tensor_add(o_t[sl][:], x_t[sl][:], at_t[sl][:]),
                lambda E, sl=sl, tt=tt: E.tensor_add(o_t[sl][:], o_t[sl][:],
                                                     y_acc[:, tt * D:(tt + 1) * D]),
            ])
            xt_free[sl] = ev_s
            at_free[sl] = ev_s
            ev_od = mk.dma(Y, [ev_s], OUT[row0:row0 + 128, :], o_t[sl][:])
            ot_free[sl] = ev_od
            ev_out_final.append(ev_od)
        cc.close()

    for _h in range(2):
        build_half(_h)

    yacc_scope.close()

    for k, eng in mk.dma_eng.items():
        mk._wait(eng, (k, mk.cnt[k]))
    mk.step(Y, ev_out_final, lambda E: E.nop())
    mk.build_blocks()
    return nc


def shard_inputs(cfg, x, wqT, wkT, wvT, woT, gate_r, wg, wu, wd, sg, su, sd):
    in_maps = []
    ncores = cfg.B * cfg.NQ
    for core in range(ncores):
        bidx, q = core // cfg.NQ, core % cfg.NQ
        xp = np.zeros((cfg.SPT, cfg.D), np.float32)
        xp[cfg.LB - q * cfg.OWN:] = x[bidx, :(q + 1) * cfg.OWN]
        pos = np.zeros((1, cfg.SPT), np.float32)
        pos[0, cfg.LB - q * cfg.OWN:] = np.arange((q + 1) * cfg.OWN, dtype=np.float32)
        in_maps.append({
            "x": xp, "pos": pos, "wqT": wqT, "wkT": wkT, "wvT": wvT, "woT": woT,
            "gate_r": gate_r, "wg": wg, "wu": wu, "wd": wd,
            "sg": sg, "su": su, "sd": sd,
        })
    return in_maps


def prep_weights(cfg, wq, wk, wv, wo, gate_w, wg, wu, wd, sg, su, sd):
    f = np.float32
    gate = np.asarray(gate_w, f)
    gate_r = np.ascontiguousarray(
        gate.reshape(cfg.NDC, 128, cfg.NEXP).transpose(1, 0, 2)
        .reshape(128, cfg.NDC * cfg.NEXP))
    return dict(
        wqT=np.ascontiguousarray(np.asarray(wq, f).T),
        wkT=np.ascontiguousarray(np.asarray(wk, f).T),
        wvT=np.ascontiguousarray(np.asarray(wv, f).T),
        woT=np.ascontiguousarray(np.asarray(wo, f).T),
        gate_r=gate_r,
        wg=np.ascontiguousarray(np.asarray(wg, f)),
        wu=np.ascontiguousarray(np.asarray(wu, f)),
        wd=np.ascontiguousarray(np.asarray(wd, f)),
        sg=np.ascontiguousarray(np.asarray(sg, f)),
        su=np.ascontiguousarray(np.asarray(su, f)),
        sd=np.ascontiguousarray(np.asarray(sd, f)),
    )


def kernel(L0_attn_x, L0_attn_wq, L0_attn_wk, L0_attn_wv, L0_attn_wo,
           L0_ffn_gate_w, L0_ffn_wg, L0_ffn_wu, L0_ffn_wd,
           L0_ffn_sg, L0_ffn_su, L0_ffn_sd):
    cfg = CFG()
    x = np.asarray(L0_attn_x, dtype=np.float32)
    w = prep_weights(cfg, L0_attn_wq, L0_attn_wk, L0_attn_wv, L0_attn_wo,
                     L0_ffn_gate_w, L0_ffn_wg, L0_ffn_wu, L0_ffn_wd,
                     L0_ffn_sg, L0_ffn_su, L0_ffn_sd)
    nc = bass.Bass("TRN2", target_bir_lowering=False)
    build_program(nc, cfg)
    in_maps = shard_inputs(cfg, x, **w)
    res = run_bass_kernel_spmd(nc, in_maps, core_ids=list(range(8)))
    out = np.empty((cfg.B, cfg.S, cfg.D), np.float32)
    for core in range(8):
        bidx, q = core // cfg.NQ, core % cfg.NQ
        out[bidx, q * cfg.OWN:(q + 1) * cfg.OWN] = res.results[core]["out"]
    return out


# version 16 (final)
# speedup vs baseline: 1.1627x; 1.1327x over previous
import math
import numpy as np
from contextlib import ExitStack

import concourse.bass as bass
import concourse.mybir as mybir
from concourse.bass_utils import run_bass_kernel_spmd

F32 = mybir.dt.float32
BF16 = mybir.dt.bfloat16
I32 = mybir.dt.int32
AF = mybir.ActivationFunctionType
ALU = mybir.AluOpType
AX = mybir.AxisListType

ENGINES = ("sync", "scalar", "vector", "tensor", "gpsimd")
N_DMA_SEMS = 24


class MK:

    def __init__(self, nc):
        self.nc = nc
        self.es = ExitStack()
        self.sems = {e: self.es.enter_context(nc.semaphore(f"mk_{e}")) for e in ENGINES}
        for i in range(N_DMA_SEMS):
            self.sems[i] = self.es.enter_context(nc.semaphore(f"mkd_{i}"))
        self.cnt = {k: 0 for k in self.sems}
        self.dma_rr = 0
        self.dma_eng = {}
        self.waited = {}
        self.phases = []
        self.cur = None
        self.phase()

    def phase(self):
        if self.cur is not None:
            self.fence_dma()
        self.cur = {e: [] for e in ENGINES}
        self.phases.append(self.cur)
        for cns in ENGINES:
            for k in self.sems:
                self.waited[(cns, k)] = self.cnt[k]

    def _wait(self, eng, ev):
        if ev is None:
            return
        k, val = ev
        if self.waited.get((eng, k), 0) >= val:
            return
        self.waited[(eng, k)] = val
        sem = self.sems[k]
        self.cur[eng].append(lambda E, sem=sem, val=val: E.wait_ge(sem, val))

    def step(self, eng, deps, emit, inc=1, key=None):
        for d in deps:
            self._wait(eng, d)
        k = eng if key is None else key
        self.cnt[k] += inc
        sem = self.sems[k]

        def run(E, emit=emit, sem=sem, inc=inc):
            last = emit(E)
            if isinstance(last, (list, tuple)):
                last = last[-1]
            last.then_inc(sem, inc)
        self.cur[eng].append(run)
        return (k, self.cnt[k])

    def chain(self, eng, deps, emits, extra=None):
        ev = None
        for i, em in enumerate(emits):
            d = list(deps if i == 0 else []) + ([ev] if ev else [])
            if extra and extra[i]:
                d += list(extra[i])
            ev = self.step(eng, d, em)
        return ev

    def fence_dma(self):
        for k, eng in self.dma_eng.items():
            self._wait(eng, (k, self.cnt[k]))

    def fence_all(self):
        self.fence_dma()
        for e in ENGINES:
            if self.cnt[e]:
                self._wait("sync", (e, self.cnt[e]))

    def dma(self, eng, deps, out, in_, **kw):
        k = self.dma_rr % N_DMA_SEMS
        self.dma_rr += 1
        self.dma_eng[k] = eng
        self._wait(eng, (k, self.cnt[k]))
        return self.step(eng, deps,
                         lambda E: E.dma_start(out=out, in_=in_, **kw),
                         inc=16, key=k)

    def build_blocks(self):
        for ph in self.phases:
            with self.nc.Block() as blk:
                for e in ENGINES:
                    ops = ph[e]
                    if not ops:
                        continue
                    getattr(blk, e)(lambda E, ops=ops: [f(E) for f in ops])


class CFG:
    def __init__(self, D=2048, NH=16, NKV=4, FEXP=768, NEXP=32, FSH=1536,
                 TOPK=6, OWN=2048, LB=6144, NQ=4, B=2, S=8192, debug=False):
        self.debug = debug
        self.D, self.NH, self.NKV = D, NH, NKV
        self.FEXP, self.NEXP, self.FSH, self.TOPK = FEXP, NEXP, FSH, TOPK
        self.OWN, self.LB, self.NQ, self.B, self.S = OWN, LB, NQ, B, S
        self.DH = 128
        self.NDC = D // 128
        self.NFC = FEXP // 128
        self.NSC = FSH // 128
        self.GR = NH // NKV
        self.NG = NH // 4
        assert self.GR == 4 and self.NG == NKV
        self.SPT = OWN + LB
        self.NT = self.SPT // 128
        self.NT_OWN = OWN // 128
        self.NST = self.SPT // 512
        self.NST_OWN = OWN // 512
        self.HALF = OWN // 2
        self.NTT = self.HALF // 128
        self.NTG = self.HALF // 512
        self.NDG = D // 512
        assert self.NTG >= 1 and LB == (NQ - 1) * OWN
        self.lam = [2.0 ** (-8.0 * (h + 1) / NH) for h in range(NH)]
        self.blk = [math.exp(-l * 128.0) for l in self.lam]


def build_program(nc, c):
    D, NDC, NH, NKV = c.D, c.NDC, c.NH, c.NKV
    NG, NFC, NEXP, FEXP = c.NG, c.NFC, c.NEXP, c.FEXP
    NTT, NDG, NTG, HALF = c.NTT, c.NDG, c.NTG, c.HALF
    KW = NKV * 128
    QW = NH * 128

    X = nc.dram_tensor("x", [c.SPT, D], F32, kind="ExternalInput").ap()
    POS = nc.dram_tensor("pos", [1, c.SPT], F32, kind="ExternalInput").ap()
    WQT = nc.dram_tensor("wqT", [D, QW], F32, kind="ExternalInput").ap()
    WKT = nc.dram_tensor("wkT", [D, KW], F32, kind="ExternalInput").ap()
    WVT = nc.dram_tensor("wvT", [D, KW], F32, kind="ExternalInput").ap()
    WOT = nc.dram_tensor("woT", [QW, D], F32, kind="ExternalInput").ap()
    GATE = nc.dram_tensor("gate_r", [128, NDC * NEXP], F32, kind="ExternalInput").ap()
    WG = nc.dram_tensor("wg", [NEXP, D, FEXP], F32, kind="ExternalInput").ap()
    WU = nc.dram_tensor("wu", [NEXP, D, FEXP], F32, kind="ExternalInput").ap()
    WD = nc.dram_tensor("wd", [NEXP, FEXP, D], F32, kind="ExternalInput").ap()
    SG = nc.dram_tensor("sg", [D, c.FSH], F32, kind="ExternalInput").ap()
    SU = nc.dram_tensor("su", [D, c.FSH], F32, kind="ExternalInput").ap()
    SD = nc.dram_tensor("sd", [c.FSH, D], F32, kind="ExternalInput").ap()
    OUT = nc.dram_tensor("out", [c.OWN, D], F32, kind="ExternalOutput").ap()

    scr_kind = "ExternalOutput" if c.debug else "Internal"
    WG_BF = nc.dram_tensor("wg_bf_scr", [NEXP, NDC, 128, FEXP], BF16, kind="Internal").ap()
    WU_BF = nc.dram_tensor("wu_bf_scr", [NEXP, NDC, 128, FEXP], BF16, kind="Internal").ap()
    WD_BF = nc.dram_tensor("wd_bf_scr", [NEXP, NFC, 128, D], BF16, kind="Internal").ap()
    NSH2 = c.NSC // 2
    FSH2c = NSH2 * 128
    SG_BF = nc.dram_tensor("sg_bf_scr", [2, NDC, 128, FSH2c], BF16, kind="Internal").ap()
    SU_BF = nc.dram_tensor("su_bf_scr", [2, NDC, 128, FSH2c], BF16, kind="Internal").ap()
    SD_BF = nc.dram_tensor("sd_bf_scr", [c.NSC, 128, D], BF16, kind="Internal").ap()
    XN_SCR = nc.dram_tensor("xn_scr", [128, NDC * c.OWN], BF16, kind=scr_kind).ap()
    KVSNAP = nc.dram_tensor("kvsnap", [c.NT_OWN, 128, NH * 128], BF16, kind=scr_kind).ap()
    OFEAT = nc.dram_tensor("ofeat", [c.NT_OWN, 128, NH * 128], BF16, kind=scr_kind).ap()
    ATTN = nc.dram_tensor("attn_scr", [c.OWN, D], F32, kind=scr_kind).ap()

    ident_bf = nc.alloc_sbuf_tensor("ident_bf", [128, 128], BF16)
    ident_f32 = nc.alloc_sbuf_tensor("ident_f32", [128, 128], F32)
    iota_p = nc.alloc_sbuf_tensor("iota_p", [128, 1], F32)
    iota_row = nc.alloc_sbuf_tensor("iota_row", [128, 128], F32)
    ones_row = nc.alloc_sbuf_tensor("ones_row", [1, 128], F32)
    pi_col = nc.alloc_sbuf_tensor("pi_col", [128, 1], F32)
    eps_col = nc.alloc_sbuf_tensor("eps_col", [128, 1], F32)

    NSLOT = 4
    ps = [nc.alloc_psum_tensor(f"psb{i}", [128, 512], F32) for i in range(7)]
    ps7 = nc.alloc_psum_tensor("psb7", [128, 1024], BF16)

    mk = MK(nc)
    V, S, T, G, Y = "vector", "scalar", "tensor", "gpsimd", "sync"

    class Pump:
        def __init__(self):
            self.jobs = []
            self.done = {}
            self.i = 0
            self.stg_free = [None] * NSLOT
            self.bf_free = [None] * NSLOT
            self.ncast = 0

        def add(self, key, blocks):
            for b_ in blocks:
                self.jobs.append((key,) + b_)

        def tick(self, n):
            for _ in range(n):
                if self.i >= len(self.jobs):
                    return
                key, sap, dap, wdt = self.jobs[self.i]
                s = self.i % NSLOT
                ld = mk.dma(G, [self.stg_free[s]], pstg[s][:, 0:wdt], sap)
                if self.i % 2 == 0:
                    ce = mk.step(V, [ld, self.bf_free[s]],
                                 lambda E, s=s, wdt=wdt:
                                 E.tensor_copy(pbf[s][:, 0:wdt], pstg[s][:, 0:wdt]))
                else:
                    ce = mk.step(S, [ld, self.bf_free[s]],
                                 lambda E, s=s, wdt=wdt:
                                 E.copy(pbf[s][:, 0:wdt], pstg[s][:, 0:wdt]))
                self.stg_free[s] = ce
                st = mk.dma((S, Y)[self.i % 2], [ce], dap, pbf[s][:, 0:wdt])
                self.bf_free[s] = st
                self.done[key] = st
                self.i += 1

        def finish(self):
            self.tick(len(self.jobs))

    pump = Pump()
    GB = max(1, min(NDC, 1024 // FEXP))
    for e in range(NEXP):
        pump.add(("wg", e), [
            (WG[e, dc * 128:(dc + GB) * 128, :].rearrange("(b p) f -> p b f", p=128),
             WG_BF[e, dc:dc + GB].transpose([1, 0, 2]),
             GB * FEXP)
            for dc in range(0, NDC, GB)])
        pump.add(("wu", e), [
            (WU[e, dc * 128:(dc + GB) * 128, :].rearrange("(b p) f -> p b f", p=128),
             WU_BF[e, dc:dc + GB].transpose([1, 0, 2]),
             GB * FEXP)
            for dc in range(0, NDC, GB)])
        WDW = min(1024, D)
        pump.add(("wd", e), [(WD[e, fc * 128:(fc + 1) * 128, hh * WDW:(hh + 1) * WDW],
                              WD_BF[e, fc, :, hh * WDW:(hh + 1) * WDW], WDW)
                             for fc in range(NFC) for hh in range(D // WDW)])
    for sh in range(2):
        pump.add(("sg", sh), [(SG[dc * 128:(dc + 1) * 128, sh * FSH2c:(sh + 1) * FSH2c],
                               SG_BF[sh, dc], FSH2c) for dc in range(NDC)])
        pump.add(("su", sh), [(SU[dc * 128:(dc + 1) * 128, sh * FSH2c:(sh + 1) * FSH2c],
                               SU_BF[sh, dc], FSH2c) for dc in range(NDC)])
    WDW2 = min(1024, D)
    pump.add(("sd", 0), [(SD[fc * 128:(fc + 1) * 128, hh * WDW2:(hh + 1) * WDW2],
                          SD_BF[fc, :, hh * WDW2:(hh + 1) * WDW2], WDW2)
                         for fc in range(c.NSC) for hh in range(D // WDW2)])
    PUMP_PER_TILE = (len(pump.jobs) + c.NT - 1) // c.NT

    a_scope = ExitStack()

    def sba(name, shape, dtype):
        return a_scope.enter_context(nc.sbuf_tensor(name, shape, dtype))

    pstg = [sba(f"pump_f{i}", [128, 1024], F32) for i in range(NSLOT)]
    pbf = [sba(f"pump_b{i}", [128, 1024], BF16) for i in range(NSLOT)]
    cos_own = sba("cos_o", [128, c.OWN], BF16)
    sin_own = sba("sin_o", [128, c.OWN], BF16)
    rotm = sba("rotm", [128, 128], BF16)
    kdec_cols = sba("kdec_cols", [128, NH], F32)
    k_own = sba("k_own", [128, NKV * c.OWN], BF16)
    v_own = sba("v_own", [128, c.NT_OWN * KW], BF16)

    lb_scope = ExitStack()
    cos_lb = lb_scope.enter_context(nc.sbuf_tensor("cos_lb", [128, c.LB], BF16))
    sin_lb = lb_scope.enter_context(nc.sbuf_tensor("sin_lb", [128, c.LB], BF16))

    def tabsl(which, lo, hi):
        if hi <= c.LB:
            return (cos_lb if which == 0 else sin_lb)[:, lo:hi]
        return (cos_own if which == 0 else sin_own)[:, lo - c.LB:hi - c.LB]

    p0 = ExitStack()
    pos_sb = p0.enter_context(nc.sbuf_tensor("pos_sb", [1, c.SPT], F32))
    invfreq = p0.enter_context(nc.sbuf_tensor("invfreq", [1, 128], F32))
    iv_raw = p0.enter_context(nc.sbuf_tensor("iv_raw", [1, 128], F32))
    iv_g = p0.enter_context(nc.sbuf_tensor("iv_g", [1, 128], F32))
    rampb = p0.enter_context(nc.sbuf_tensor("rampb", [128, 128], F32))
    scr0 = p0.enter_context(nc.sbuf_tensor("p0scr0", [128, 512], F32))
    scr1 = p0.enter_context(nc.sbuf_tensor("p0scr1", [128, 512], F32))
    scri = p0.enter_context(nc.sbuf_tensor("p0scri", [128, 512], I32))
    scrH = p0.enter_context(nc.sbuf_tensor("p0scrH", [128, 512], F32))
    scrS = p0.enter_context(nc.sbuf_tensor("p0scrS", [128, 512], F32))

    ev_iota = mk.step(G, [], lambda E: [
        E.iota(iota_p[:], [[1, 1]], base=0, channel_multiplier=1,
               allow_small_or_imprecise_dtypes=True),
        E.iota(iota_row[:], [[1, 128]], base=0, channel_multiplier=0,
               allow_small_or_imprecise_dtypes=True),
        E.iota(iv_raw[:], [[1, 128]], base=0, channel_multiplier=0,
               allow_small_or_imprecise_dtypes=True),
    ])
    ev_pos = mk.dma(Y, [], pos_sb[:], POS)
    ev_const = mk.step(V, [ev_iota], lambda E: [
        E.tensor_single_scalar(ident_bf[:], iota_row[:], iota_p[:, 0:1], ALU.is_equal),
        E.tensor_single_scalar(ident_f32[:], iota_row[:], iota_p[:, 0:1], ALU.is_equal),
        E.memset(ones_row[:], 1.0),
        E.memset(pi_col[:], math.pi),
        E.memset(eps_col[:], 1e-5),
    ])
    ev_ivm = mk.chain(V, [ev_iota, ev_const], [
        lambda E: E.tensor_single_scalar(iv_g[:], iv_raw[:], 64.0, ALU.is_ge),
        lambda E: E.scalar_tensor_tensor(iv_g[:], iv_g[:], -64.0, iv_raw[:],
                                         ALU.mult, ALU.add),
    ])
    ev_if = mk.step(S, [ev_ivm], lambda E:
                    E.activation(out=invfreq[:], in_=iv_g[:], func=AF.Exp,
                                 scale=-math.log(10000.0) / 64.0))
    ev_rotm = mk.chain(V, [ev_iota, ev_const], [
        lambda E: E.tensor_scalar(scr0[:, 0:1], iota_p[:, 0:1], 1.0, 64.0,
                                  ALU.mult, ALU.add),
        lambda E: E.tensor_single_scalar(scr1[:, 0:128], iota_row[:], scr0[:, 0:1],
                                         ALU.is_equal),
        lambda E: E.tensor_scalar(scr0[:, 0:1], iota_p[:, 0:1], 1.0, -64.0,
                                  ALU.mult, ALU.add),
        lambda E: E.tensor_single_scalar(scr1[:, 128:256], iota_row[:], scr0[:, 0:1],
                                         ALU.is_equal),
        lambda E: E.tensor_sub(rotm[:], scr1[:, 0:128], scr1[:, 128:256]),
    ])

    ev_tab_chain = []
    for i in range(c.SPT // 512):
        sl = slice(i * 512, (i + 1) * 512)
        bank = ps[i % 2]
        dep_bank = ev_tab_chain[-2][0] if len(ev_tab_chain) >= 2 else None
        evm = mk.step(T, [ev_if, ev_pos, dep_bank, ev_rotm],
                      lambda E, sl=sl, bank=bank:
                      E.matmul(bank[:], invfreq[0:1, :], pos_sb[0:1, sl],
                               start=True, stop=True))
        prev_s = ev_tab_chain[-1][1] if ev_tab_chain else None
        ev_last = None
        for which in (1, 0):
            shift = 0.0 if which == 1 else 0.5
            evf = mk.chain(V, [evm, ev_const, prev_s if which == 1 else ev_last], [
                lambda E, bank=bank, shift=shift: E.tensor_scalar(
                    scrH[:, 0:512], bank[:], 1.0 / math.pi, shift, ALU.mult, ALU.add),
                lambda E: E.tensor_copy(scri[:, 0:512], scrH[:, 0:512]),
                lambda E: E.tensor_copy(scr1[:, 0:512], scri[:, 0:512]),
                lambda E: E.tensor_sub(scr0[:, 0:512], scrH[:, 0:512], scr1[:, 0:512]),
                lambda E: E.tensor_single_scalar(scrS[:, 0:512], scr0[:, 0:512], 0.5,
                                                 ALU.is_gt),
                lambda E: E.tensor_single_scalar(scrH[:, 0:512], scr0[:, 0:512], -0.5,
                                                 ALU.is_lt),
                lambda E: E.tensor_sub(scrS[:, 0:512], scrS[:, 0:512], scrH[:, 0:512]),
                lambda E: E.tensor_sub(scr0[:, 0:512], scr0[:, 0:512], scrS[:, 0:512]),
                lambda E: E.tensor_add(scr1[:, 0:512], scr1[:, 0:512], scrS[:, 0:512]),
                lambda E: E.tensor_scalar(scrH[:, 0:512], scr1[:, 0:512], 0.5,
                                          None, ALU.mult),
                lambda E: E.tensor_copy(scri[:, 0:512], scrH[:, 0:512]),
                lambda E: E.tensor_copy(scrH[:, 0:512], scri[:, 0:512]),
                lambda E: E.scalar_tensor_tensor(scrH[:, 0:512], scrH[:, 0:512], -2.0,
                                                 scr1[:, 0:512], ALU.mult, ALU.add),
                lambda E: E.tensor_mul(scrH[:, 0:512], scrH[:, 0:512], scrH[:, 0:512]),
                lambda E: E.tensor_scalar(scrH[:, 0:512], scrH[:, 0:512], -2.0, 1.0,
                                          ALU.mult, ALU.add),
            ])
            evs0 = mk.step(S, [evf], lambda E:
                           E.activation(out=scrS[:, 0:512], in_=scr0[:, 0:512],
                                        func=AF.Sin, scale=math.pi))
            tdst = tabsl(which, sl.start, sl.stop)
            ev_last = mk.step(V, [evs0], lambda E, tdst=tdst:
                              E.tensor_mul(tdst, scrS[:, 0:512], scrH[:, 0:512]))
        ev_tab_chain.append((ev_last, ev_last))
    ev_tab_all = ev_tab_chain[-1][1]

    ev_h = ev_tab_chain[-1][0]
    for h in range(NH):
        lam = c.lam[h]
        e6 = mk.step(V, [ev_h], lambda E:
                     E.tensor_scalar(scr1[:, 0:1], iota_p[:, 0:1], -1.0, 127.0,
                                     ALU.mult, ALU.add))
        e7 = mk.step(S, [e6], lambda E, h=h, lam=lam: E.activation(
            out=kdec_cols[:, h:h + 1], in_=scr1[:, 0:1], func=AF.Exp, scale=-lam))
        ev_h = e7
    p0.close()

    mk.phase()
    a1 = ExitStack()

    def sb1(name, shape, dtype):
        return a1.enter_context(nc.sbuf_tensor(name, shape, dtype))

    nbn = 6 * ((D + 511) // 512)
    kv_sb = sb1("kv_sb", [128, NH * 128], F32)
    x_ring = [sb1(f"x_ring{i}", [128, D], F32) for i in range(2)]
    stats2 = [sb1(f"stats{i}", [128, 8], F32) for i in range(2)]
    bnrows2 = [sb1(f"bnrows{i}", [128, nbn], F32) for i in range(2)]
    xn_tok = [sb1(f"xn_tok{i}", [128, D], BF16) for i in range(1)]
    xn_st = [sb1(f"xn_st{i}", [128, NDC * 512], BF16) for i in range(2)]
    kraw = sb1("kraw", [128, NKV * 512], BF16)
    krot_scr = sb1("krot_scr", [128, NKV * 512], BF16)
    rsc = sb1("rsc", [128, 1024], BF16)
    kpos_t = [sb1(f"kpos{i}", [128, KW], BF16) for i in range(2)]
    kprime = sb1("kprime", [128, NH * 128], BF16)
    wk_bf = sb1("wk_bf", [128, NDC * KW], BF16)
    wv_bf = sb1("wv_bf", [128, NDC * KW], BF16)
    wstg = [p[:, 0:KW] for p in pstg]
    snap = [sb1(f"snap{i}", [128, NH * 128], BF16) for i in range(1)]
    v_scr = [sb1(f"vscr{i}", [128, KW], BF16) for i in range(2)]

    ev_kv0 = mk.step(V, [], lambda E: E.memset(kv_sb[:], 0.0))

    wfree = [None] * NSLOT
    last_cast = None
    for i, (W_, dst) in enumerate([(WKT, wk_bf), (WVT, wv_bf)]):
        for dc in range(NDC):
            s = (i * NDC + dc) % NSLOT
            ld = mk.dma(Y, [wfree[s]], wstg[s][:], W_[dc * 128:(dc + 1) * 128, :])
            ce = mk.step(V, [ld], lambda E, s=s, dst=dst, dc=dc:
                         E.tensor_copy(dst[:, dc * KW:(dc + 1) * KW], wstg[s][:]))
            wfree[s] = ce
            last_cast = ce
    ev_wkv = last_cast
    for s in range(NSLOT):
        pump.stg_free[s] = wfree[s]

    x_free = [None] * 2
    xn_free = [None, None]
    xnst_free = [[], []]
    kpos_free = [None, None]
    vbank_free = [None, None]
    vscr_free = [None, None]
    snap_free = [None]
    ps7_free = None
    ps6_free = None
    rot_bank_free = [None, None]
    kprime_free = None
    rsc_free = None
    ev_kv_upd = ev_kv0
    ev_k_prev_st = None
    stats_free = [None, None]
    kraw_free = None
    krot_free = None

    for st in range(c.NST):
        own_st = st >= c.NST - c.NST_OWN
        so = st - (c.NST - c.NST_OWN)
        xs = xn_st[st % 2]
        ev_tp_tiles = []
        for tl in range(4):
            t = st * 4 + tl
            xslot = t % 2
            stats = stats2[xslot]
            bnrows = bnrows2[xslot]
            ev_x = mk.dma(Y, [x_free[xslot]], x_ring[xslot][:],
                          X[t * 128:(t + 1) * 128, :])
            pump.tick(PUMP_PER_TILE)
            ngrp = (D + 511) // 512
            ev_bn = mk.chain(V, [ev_x, stats_free[xslot]], [
                lambda E, xslot=xslot, ngrp=ngrp, bnrows=bnrows: [
                    E.bn_stats(bnrows[:, g * 6:(g + 1) * 6],
                               x_ring[xslot][:, g * 512:min((g + 1) * 512, D)])
                    for g in range(ngrp)],
                lambda E, ngrp=ngrp, stats=stats, bnrows=bnrows:
                    E.bn_aggr(stats[:, 0:2], bnrows[:, 0:6 * ngrp]),
            ])
            ev_sq = mk.step(S, [ev_bn], lambda E, stats=stats: E.activation(
                out=stats[:, 5:6], in_=stats[:, 1:2], func=AF.Sqrt,
                bias=eps_col[:, 0:1]))
            ev_nmr = mk.chain(V, [ev_sq], [
                lambda E, stats=stats: E.reciprocal(stats[:, 2:3], stats[:, 5:6]),
                lambda E, stats=stats: E.tensor_scalar(stats[:, 3:4], stats[:, 0:1],
                                                       -1.0, None, ALU.mult),
                lambda E, stats=stats: E.tensor_mul(stats[:, 4:5], stats[:, 3:4],
                                                    stats[:, 2:3]),
            ])
            xnslot = 0
            ev_xn = mk.step(S, [ev_nmr, xn_free[xnslot]],
                            lambda E, xslot=xslot, xnslot=xnslot, stats=stats:
                            E.activation(
                out=xn_tok[xnslot][:], in_=x_ring[xslot][:], func=AF.Identity,
                scale=stats[:, 2:3], bias=stats[:, 4:5]))
            x_free[xslot] = ev_xn
            stats_free[xslot] = ev_xn
            ev_ev2 = None
            for hf in range(2):
                ndc_h = min(8, NDC - hf * 8)
                if ndc_h <= 0:
                    continue
                ev_tp = mk.step(T, [ev_xn, ps7_free] +
                                (xnst_free[st % 2] if (hf == 0 and tl == 0) else []),
                                lambda E, xnslot=xnslot, hf=hf, ndc_h=ndc_h: [
                    E.transpose(ps7[:, j * 128:(j + 1) * 128],
                                xn_tok[xnslot][:, (hf * 8 + j) * 128:(hf * 8 + j + 1) * 128],
                                ident_bf[:])
                    for j in range(ndc_h)])
                eng = V if hf == 0 else S
                if eng == V:
                    ev_ev2 = mk.step(V, [ev_tp],
                                     lambda E, xs=xs, tl=tl, hf=hf, ndc_h=ndc_h:
                                     E.tensor_copy(
                        xs[:].rearrange("p (dc t) -> p dc t", t=512)
                          [:, hf * 8:hf * 8 + ndc_h, tl * 128:(tl + 1) * 128],
                        ps7[:].rearrange("p (dc t) -> p dc t", t=128)[:, 0:ndc_h, :]))
                else:
                    ev_ev2 = mk.step(S, [ev_tp],
                                     lambda E, xs=xs, tl=tl, hf=hf, ndc_h=ndc_h:
                                     E.copy(
                        xs[:].rearrange("p (dc t) -> p dc t", t=512)
                          [:, hf * 8:hf * 8 + ndc_h, tl * 128:(tl + 1) * 128],
                        ps7[:].rearrange("p (dc t) -> p dc t", t=128)[:, 0:ndc_h, :]))
                ps7_free = ev_ev2
            xn_free[xnslot] = ev_ev2
            ev_tp_tiles.append(ev_ev2)
        ev_xn_dma = None
        if own_st:
            ev_xn_dma = mk.dma(Y, ev_tp_tiles,
                               XN_SCR[:].rearrange("p (dc t) -> p dc t", t=c.OWN)
                               [:, :, so * 512:(so + 1) * 512],
                               xs[:].rearrange("p (dc t) -> p dc t", t=512))

        ev_k_oc = []
        for oc in range(NKV):
            bank = ps[oc % 2]
            prev = ev_k_oc[oc - 2] if oc >= 2 else ev_k_prev_st

            def kmm(E, oc=oc, bank=bank, xs=xs):
                out = None
                for dc in range(NDC):
                    out = E.matmul(bank[:],
                                   wk_bf[:, dc * KW + oc * 128: dc * KW + (oc + 1) * 128],
                                   xs[:, dc * 512:(dc + 1) * 512],
                                   start=(dc == 0), stop=(dc == NDC - 1))
                return out
            evm = mk.step(T, ev_tp_tiles + [ev_wkv, prev], kmm)
            eve = mk.step(V, [evm, kraw_free if oc == 0 else None],
                          lambda E, oc=oc, bank=bank:
                          E.tensor_copy(kraw[:, oc * 512:(oc + 1) * 512], bank[:]))
            ev_k_oc.append(eve)
        if own_st:
            kout = k_own[:].rearrange("p (kv t) -> p kv t", t=c.OWN)[:, :, so * 512:(so + 1) * 512]
        else:
            kout = krot_scr[:].rearrange("p (kv t) -> p kv t", t=512)
        ev_krot = None
        for kv in range(NKV):
            cosl = tabsl(0, st * 512, (st + 1) * 512)
            sinl = tabsl(1, st * 512, (st + 1) * 512)
            p7s = ps[4 + kv % 2][:]
            ev_rt = mk.step(T, [ev_k_oc[kv], ev_rotm, rot_bank_free[kv % 2]],
                            lambda E, kv=kv, p7s=p7s:
                            E.matmul(p7s, rotm[:], kraw[:, kv * 512:(kv + 1) * 512],
                                     start=True, stop=True))
            ev_krot = mk.chain(V, [ev_rt, ev_tab_all, rsc_free, krot_free], [
                lambda E, kv=kv, cosl=cosl:
                    E.tensor_tensor(rsc[:, 0:512], kraw[:, kv * 512:(kv + 1) * 512],
                                    cosl, ALU.mult),
                lambda E, p7s=p7s, sinl=sinl:
                    E.tensor_tensor(rsc[:, 512:1024], p7s, sinl, ALU.mult),
                lambda E, kv=kv, kout=kout:
                    E.tensor_add(kout[:, kv], rsc[:, 0:512], rsc[:, 512:1024]),
            ])
            rot_bank_free[kv % 2] = ev_krot
            rsc_free = ev_krot
        kraw_free = ev_krot
        ev_k_prev_st = ev_k_oc[-1]

        ev_last_delta_tile = None
        for tl in range(4):
            t = st * 4 + tl
            vb = tl % 2
            vbank = ps[2 + vb]

            def vmm(E, xs=xs, tl=tl, vbank=vbank):
                out = None
                for dc in range(NDC):
                    out = E.matmul(vbank[:, 0:KW],
                                   xs[:, dc * 512 + tl * 128: dc * 512 + (tl + 1) * 128],
                                   wv_bf[:, dc * KW:(dc + 1) * KW],
                                   start=(dc == 0), stop=(dc == NDC - 1))
                return out
            evv = mk.step(T, [ev_tp_tiles[tl], ev_wkv, vbank_free[vb]], vmm)
            ev_last_vmm = evv
            if own_st:
                vcur = v_own[:, (so * 4 + tl) * KW:(so * 4 + tl + 1) * KW]
                evve = mk.step(V, [evv], lambda E, vcur=vcur, vbank=vbank:
                               E.tensor_copy(vcur, vbank[:, 0:KW]))
            else:
                vs_ = t % 2
                vcur = v_scr[vs_][:, 0:KW]
                evve = mk.step(V, [evv, vscr_free[vs_]],
                               lambda E, vcur=vcur, vbank=vbank:
                               E.tensor_copy(vcur, vbank[:, 0:KW]))
            vbank_free[vb] = evve

            kslot = t % 2
            ev_kt = mk.step(T, [ev_krot, kpos_free[kslot], ps7_free],
                            lambda E, kout=kout, tl=tl: [
                E.transpose(ps7[:, j * 128:(j + 1) * 128],
                            kout[:, j, tl * 128:(tl + 1) * 128], ident_bf[:])
                for j in range(NKV)])
            ev_kte = mk.step(S, [ev_kt], lambda E, kslot=kslot:
                             E.copy(kpos_t[kslot][:, 0:KW], ps7[:, 0:KW]))
            ps7_free = ev_kte
            ev_kp = mk.step(V, [ev_kte, kprime_free], lambda E, kslot=kslot: [
                E.tensor_single_scalar(kprime[:, h * 128:(h + 1) * 128],
                                       kpos_t[kslot][:, (h // 4) * 128:(h // 4 + 1) * 128],
                                       kdec_cols[:, h:h + 1], ALU.mult)
                for h in range(NH)])
            kpos_free[kslot] = ev_kp
            last_kt = ev_kt

            ev_sn = None
            if own_st:
                co = so * 4 + tl
                sslot = 0
                ev_sn = mk.step(V, [ev_kv_upd, snap_free[sslot]],
                                lambda E, sslot=sslot:
                                E.tensor_copy(snap[sslot][:], kv_sb[:]))
                ev_snd = mk.dma(Y, [ev_sn], KVSNAP[co], snap[sslot][:])
                snap_free[sslot] = ev_snd

            for g in range(NG):
                def dmm(E, g=g, vcur=vcur):
                    out = None
                    for hp in range(4):
                        out = E.matmul(ps[6][:, hp * 128:(hp + 1) * 128],
                                       kprime[:, (g * 4 + hp) * 128:(g * 4 + hp + 1) * 128],
                                       vcur[:, g * 128:(g + 1) * 128],
                                       start=True, stop=True)
                    return out
                evd = mk.step(T, [ev_kp, evve, ps6_free], dmm)
                ev_up = mk.step(V, [evd] + ([ev_sn] if ev_sn else []),
                                lambda E, g=g: [
                    E.scalar_tensor_tensor(
                        kv_sb[:, (g * 4 + hp) * 128:(g * 4 + hp + 1) * 128],
                        kv_sb[:, (g * 4 + hp) * 128:(g * 4 + hp + 1) * 128],
                        c.blk[g * 4 + hp],
                        ps[6][:, hp * 128:(hp + 1) * 128],
                        ALU.mult, ALU.add)
                    for hp in range(4)])
                ps6_free = ev_up
                ev_kv_upd = ev_up
                ev_last_delta_tile = ev_up
            kprime_free = ev_last_delta_tile
            if not own_st:
                vscr_free[t % 2] = ev_last_delta_tile
        xnst_free[st % 2] = [ev_last_vmm] + ([ev_xn_dma] if ev_xn_dma else [])
        krot_free = last_kt

    pump.finish()
    a1.close()
    lb_scope.close()

    mk.phase()
    a2 = ExitStack()

    def sb2(name, shape, dtype):
        return a2.enter_context(nc.sbuf_tensor(name, shape, dtype))

    mask_pack = sb2("mask_pack", [128, NG * 512], BF16)
    qdec_pack = sb2("qdec_pack", [128, NG * 512], BF16)

    mp0 = ExitStack()
    rampb = mp0.enter_context(nc.sbuf_tensor("rampb2", [128, 128], F32))
    mscr = mp0.enter_context(nc.sbuf_tensor("mscr", [128, 512], F32))
    ev_rb = mk.step(T, [], lambda E:
                    E.matmul(ps[2][:, 0:128], ones_row[0:1, :], iota_row[0:1, :],
                             start=True, stop=True))
    ev_rbs = mk.step(V, [ev_rb], lambda E:
                     E.tensor_copy(rampb[:], ps[2][:, 0:128]))
    ev_diff = mk.step(V, [ev_rbs], lambda E:
                      E.tensor_single_scalar(mscr[:, 0:128], rampb[:], iota_p[:, 0:1],
                                             ALU.subtract))
    ev_h2 = ev_diff
    for h in range(NH):
        lam = c.lam[h]
        g, hp = h // 4, h % 4
        msl = slice(g * 512 + hp * 128, g * 512 + (hp + 1) * 128)
        e1 = mk.step(V, [ev_h2], lambda E: [
            E.tensor_single_scalar(mscr[:, 128:256], mscr[:, 0:128], 0.0, ALU.is_ge),
            E.tensor_single_scalar(mscr[:, 256:384], mscr[:, 0:128], 0.0, ALU.max),
        ])
        e2 = mk.step(S, [e1], lambda E, lam=lam: E.activation(
            out=mscr[:, 384:512], in_=mscr[:, 256:384], func=AF.Exp, scale=-lam))
        e3 = mk.step(V, [e2], lambda E, msl=msl:
                     E.tensor_mul(mask_pack[:, msl], mscr[:, 384:512], mscr[:, 128:256]))
        e4 = mk.step(S, [ev_rbs, e3], lambda E, lam=lam: E.activation(
            out=mscr[:, 384:512], in_=rampb[:], func=AF.Exp, scale=-lam))
        e5 = mk.step(V, [e4], lambda E, msl=msl, lam=lam:
                     E.tensor_scalar(qdec_pack[:, msl], mscr[:, 384:512],
                                     math.exp(-lam), None, ALU.mult))
        ev_h2 = e5
    mp0.close()

    wq_bf = sb2("wq_bf", [128, NDC * QW], BF16)
    wstg2 = [pstg[0][:, 0:QW // 2], pstg[1][:, 0:QW // 2]]
    xn_q = [sb2(f"xn_q{i}", [128, NDC * 512], BF16) for i in range(1)]
    q_st = sb2("q_st", [128, NH * 512], BF16)
    q_rot = sb2("q_rot", [128, NH * 512], BF16)
    rsc2 = sb2("rsc2", [128, 1024], BF16)
    qpk = [sb2(f"qpk{i}", [128, 512], BF16) for i in range(2)]
    scm = [sb2(f"scm{i}", [128, 512], BF16) for i in range(2)]
    kvsn = [sb2(f"kvsn{i}", [128, NH * 128], BF16) for i in range(2)]
    o_stage = [sb2(f"ostage{i}", [128, NH * 128], BF16) for i in range(1)]

    wfree2 = [None, None]
    last2 = None
    for dc2 in range(2 * NDC):
        dc, hh = dc2 // 2, dc2 % 2
        s = dc2 % 2
        HW = QW // 2
        ld = mk.dma(Y, [wfree2[s]], wstg2[s][:],
                    WQT[dc * 128:(dc + 1) * 128, hh * HW:(hh + 1) * HW])
        ce = mk.step(V, [ld], lambda E, s=s, dc=dc, hh=hh, HW=HW:
                     E.tensor_copy(wq_bf[:, dc * QW + hh * HW: dc * QW + (hh + 1) * HW],
                                   wstg2[s][:]))
        wfree2[s] = ce
        last2 = ce
    ev_wq = last2

    xnq_free = [None]
    kvsn_free = [None, None]
    ost_free = [None]
    qpk_free = [None, None]
    scm_free = [None, None]
    opack_free = [None, None]
    ev_q_last = None
    ev_qrot_free = None
    ps6a_free = None
    rot2_free = [None, None]

    for so in range(c.NST_OWN):
        xq = xn_q[0]
        ev_xq = mk.dma(Y, [xnq_free[0]],
                       xq[:].rearrange("p (dc t) -> p dc t", t=512),
                       XN_SCR[:].rearrange("p (dc t) -> p dc t", t=c.OWN)
                       [:, :, so * 512:(so + 1) * 512])
        ev_q = []
        for oc in range(NH):
            bank = ps[oc % 2]
            prev = ev_q[oc - 2][1] if oc >= 2 else ev_q_last

            def qmm(E, oc=oc, bank=bank, xq=xq):
                out = None
                for dc in range(NDC):
                    out = E.matmul(
                        bank[:],
                        wq_bf[:, dc * QW + oc * 128: dc * QW + (oc + 1) * 128],
                        xq[:, dc * 512:(dc + 1) * 512],
                        start=(dc == 0), stop=(dc == NDC - 1))
                return out
            evm = mk.step(T, [ev_xq, ev_wq, prev], qmm)
            eve = mk.step(S, [evm, ev_qrot_free if oc == 0 else None],
                          lambda E, oc=oc, bank=bank:
                          E.copy(q_st[:, oc * 512:(oc + 1) * 512], bank[:]))
            ev_q.append((evm, eve))
        xnq_free[0] = ev_q[-1][0]
        ev_q_last = ev_q[-1][1]

        cosl = cos_own[:, so * 512:(so + 1) * 512]
        sinl = sin_own[:, so * 512:(so + 1) * 512]
        ev_qrot = None
        for h in range(NH):
            p7s = ps[4 + h % 2][:]
            ev_rt = mk.step(T, [ev_q[h][1], ev_rotm, rot2_free[h % 2]],
                            lambda E, h=h, p7s=p7s:
                            E.matmul(p7s, rotm[:], q_st[:, h * 512:(h + 1) * 512],
                                     start=True, stop=True))
            ev_qrot = mk.chain(V, [ev_rt, ev_qrot, ev_qrot_free if h < 2 else None], [
                lambda E, h=h, cosl=cosl:
                    E.tensor_tensor(rsc2[:, 0:512], q_st[:, h * 512:(h + 1) * 512],
                                    cosl, ALU.mult),
                lambda E, p7s=p7s, sinl=sinl:
                    E.tensor_tensor(rsc2[:, 512:1024], p7s, sinl, ALU.mult),
                lambda E, h=h:
                    E.tensor_add(q_rot[:, h * 512:(h + 1) * 512],
                                 rsc2[:, 0:512], rsc2[:, 512:1024]),
            ])
            rot2_free[h % 2] = ev_qrot
        ev_qrot_free = None

        last_mix = None
        for tl in range(4):
            co = so * 4 + tl
            kslot = co % 2
            ev_ksd = mk.dma(Y, [kvsn_free[kslot]], kvsn[kslot][:], KVSNAP[co])
            oslot = 0
            ev_o_evs = []
            for g in range(NG):
                qslot = g % 2
                ev_qp = mk.step(V, [ev_qrot, qpk_free[qslot]],
                                lambda E, qslot=qslot, g=g, tl=tl:
                                E.tensor_tensor(
                    qpk[qslot][:].rearrange("p (a t) -> p a t", t=128),
                    q_rot[:].rearrange("p (h t) -> p h t", t=512)
                    [:, 4 * g:4 * g + 4, tl * 128:(tl + 1) * 128],
                    qdec_pack[:, g * 512:(g + 1) * 512].rearrange("p (a t) -> p a t", t=128),
                    ALU.mult))
                ev_sc = mk.step(T, [ev_qrot, ps6a_free],
                                lambda E, g=g, co=co, tl=tl:
                                E.matmul(ps[6][:],
                                         k_own[:].rearrange("p (kv t) -> p kv t", t=c.OWN)
                                         [:, g, co * 128:(co + 1) * 128],
                                         q_rot[:].rearrange("p (h t) -> p h t", t=512)
                                         [:, 4 * g:4 * g + 4, tl * 128:(tl + 1) * 128],
                                         start=True, stop=True))
                sslot = g % 2
                ev_ms = mk.step(V, [ev_sc, scm_free[sslot]],
                                lambda E, sslot=sslot, g=g:
                                E.tensor_mul(scm[sslot][:], ps[6][:],
                                             mask_pack[:, g * 512:(g + 1) * 512]))
                ps6a_free = ev_ms
                obank = ps[2 + g % 2]

                def omm(E, g=g, co=co, sslot=sslot, qslot=qslot, kslot=kslot, obank=obank):
                    E.matmul(obank[:],
                             v_own[:, co * KW + g * 128: co * KW + (g + 1) * 128],
                             scm[sslot][:], start=True, stop=False)
                    out = None
                    for hp in range(4):
                        h = 4 * g + hp
                        out = E.matmul(obank[:, hp * 128:(hp + 1) * 128],
                                       kvsn[kslot][:, h * 128:(h + 1) * 128],
                                       qpk[qslot][:, hp * 128:(hp + 1) * 128],
                                       start=False, stop=(hp == 3))
                    return out
                ev_om = mk.step(T, [ev_ms, ev_qp, ev_ksd, opack_free[g % 2]], omm)
                scm_free[sslot] = ev_om
                qpk_free[qslot] = ev_om
                ev_oe = mk.step(S, [ev_om] + ([ost_free[oslot]] if g == 0 else []),
                                lambda E, g=g, oslot=oslot, obank=obank:
                                E.copy(o_stage[oslot][:, g * 512:(g + 1) * 512],
                                       obank[:]))
                opack_free[g % 2] = ev_oe
                ev_o_evs.append(ev_oe)
                last_mix = ev_om
            kvsn_free[kslot] = ev_o_evs[-1]
            ev_od = mk.dma(Y, ev_o_evs, OFEAT[co], o_stage[oslot][:])
            ost_free[oslot] = ev_od
        ev_qrot_free = last_mix
    a2.close()
    a_scope.close()

    mk.phase()
    a2b = ExitStack()

    def sb3(name, shape, dtype):
        return a2b.enter_context(nc.sbuf_tensor(name, shape, dtype))

    wo_bf = sb3("wo_bf", [128, NH * D], BF16)
    wstg3 = [sb3(f"wstg3_{i}", [128, D], F32) for i in range(2)]
    o_buf = [sb3(f"obuf{i}", [128, 4 * NH * 128], BF16) for i in range(2)]
    att_f = sb3("att_f", [128, 512], F32)
    att_stage = [sb3(f"attst{i}", [128, D], F32) for i in range(4)]

    wfree3 = [None, None]
    last3 = None
    for ec in range(NH):
        s = ec % 2
        ld = mk.dma(Y, [wfree3[s]], wstg3[s][:], WOT[ec * 128:(ec + 1) * 128, :])
        ce = mk.step(V, [ld], lambda E, s=s, ec=ec:
                     E.tensor_copy(wo_bf[:, ec * D:(ec + 1) * D], wstg3[s][:]))
        wfree3[s] = ce
        last3 = ce
    ev_wo = last3

    obuf_free = [None, None]
    attst_free = [None] * 4
    attf_free = None
    pstp_free = None
    abank_free = [None, None]
    for so in range(c.NST_OWN):
        oslot = so % 2
        ev_ob = mk.dma(Y, [obuf_free[oslot]],
                       o_buf[oslot][:].rearrange("p (cc f) -> p cc f", f=NH * 128),
                       OFEAT[so * 4:(so + 1) * 4].transpose([1, 0, 2]))
        last_am = None
        for oc in range(NDC):
            bank = ps[oc % 2]

            def amm(E, oc=oc, oslot=oslot, bank=bank):
                out = None
                for ec in range(NH):
                    out = E.matmul(bank[:],
                                   wo_bf[:, ec * D + oc * 128: ec * D + (oc + 1) * 128],
                                   o_buf[oslot][:].rearrange("p (cc f) -> p cc f", f=NH * 128)
                                   [:, :, ec * 128:(ec + 1) * 128],
                                   start=(ec == 0), stop=(ec == NH - 1))
                return out
            ev_am = mk.step(T, [ev_ob, ev_wo, abank_free[oc % 2]], amm)
            last_am = ev_am
            ev_ae = mk.step(V, [ev_am, attf_free], lambda E, bank=bank:
                            E.tensor_copy(att_f[:], bank[:]))
            abank_free[oc % 2] = ev_ae
            ev_tp = mk.step(T, [ev_ae, pstp_free], lambda E: [
                E.transpose(ps[4][:, j * 128:(j + 1) * 128],
                            att_f[:, j * 128:(j + 1) * 128], ident_f32[:])
                for j in range(4)])
            attf_free = ev_tp
            ev_te = mk.step(S, [ev_tp] + (attst_free if oc == 0 else []),
                            lambda E, oc=oc: [
                E.copy(att_stage[j][:, oc * 128:(oc + 1) * 128],
                       ps[4][:, j * 128:(j + 1) * 128])
                for j in range(4)])
            pstp_free = ev_te
        for j in range(4):
            t = so * 4 + j
            evd = mk.dma(Y, [pstp_free], ATTN[t * 128:(t + 1) * 128, :],
                         att_stage[j][:])
            attst_free[j] = evd
        obuf_free[oslot] = last_am
    a2b.close()

    TOPK = c.TOPK
    yacc_scope = ExitStack()
    y_acc = yacc_scope.enter_context(nc.sbuf_tensor("y_acc", [128, NTT * D], F32))
    xnh_t = yacc_scope.enter_context(nc.sbuf_tensor("xnh", [128, NDC * HALF], BF16))

    ev_out_final = []

    def build_half(half):
        mk.phase()
        b = ExitStack()

        def sbb(name, shape, dtype):
            return b.enter_context(nc.sbuf_tensor(name + str(half), shape, dtype))

        combf = sbb("combf", [max(NEXP, 2), HALF], BF16)
        selm = sbb("selm", [128, 128], BF16)
        gsc_scope = ExitStack()

        def sbg(name, shape, dtype):
            return gsc_scope.enter_context(nc.sbuf_tensor(name + str(half), shape, dtype))

        gstg = sbg("gstg", [128, NDC * NEXP], F32)
        gate_bf = sbg("gate_bf", [128, NDC * NEXP], BF16)
        sgt = sbg("sgt", [128, NEXP], F32)
        gsc = sbg("gsc", [128, NEXP], F32)
        gm = sbg("gm", [128, 4], F32)
        comb_t = sbg("comb_t", [128, NEXP], F32)

        ev_xnh = mk.dma(Y, [],
                        xnh_t[:].rearrange("p (dc t) -> p dc t", t=HALF),
                        XN_SCR[:].rearrange("p (dc t) -> p dc t", t=c.OWN)
                        [:, :, half * HALF:(half + 1) * HALF])
        ev_gld = mk.dma(Y, [], gstg[:], GATE)
        ev_gc = mk.step(V, [ev_gld], lambda E: E.tensor_copy(gate_bf[:], gstg[:]))

        ps6b_free = None
        ps5_free = None
        ev_tk_prev = None
        ev_tk_prev2 = None
        ev_comb_all = None
        for tt in range(NTT):
            def gmm(E, tt=tt):
                out = None
                for dc in range(NDC):
                    out = E.matmul(ps[6][:, 0:NEXP],
                                   xnh_t[:, dc * HALF + tt * 128: dc * HALF + (tt + 1) * 128],
                                   gate_bf[:, dc * NEXP:(dc + 1) * NEXP],
                                   start=(dc == 0), stop=(dc == NDC - 1))
                return out
            ev_gm = mk.step(T, [ev_xnh, ev_gc, ps6b_free], gmm)
            ev_sig = mk.step(S, [ev_gm, ev_tk_prev], lambda E: E.activation(
                out=sgt[:], in_=ps[6][:, 0:NEXP], func=AF.Sigmoid))
            ps6b_free = ev_sig

            emits = [lambda E: E.tensor_copy(gsc[:], sgt[:])]
            for r in range(TOPK):
                emits.append(lambda E: E.reduce_max(gm[:, 0:1], gsc[:], axis=AX.X))
                if r < TOPK - 1:
                    emits.append(lambda E: E.tensor_single_scalar(
                        comb_t[:], gsc[:], gm[:, 0:1], ALU.is_ge))
                    emits.append(lambda E: E.scalar_tensor_tensor(
                        gsc[:], comb_t[:], -1e30, gsc[:], ALU.mult, ALU.add))
            emits += [
                lambda E: E.tensor_single_scalar(comb_t[:], sgt[:], gm[:, 0:1], ALU.is_ge),
                lambda E: E.tensor_mul(comb_t[:], comb_t[:], sgt[:]),
                lambda E: E.reduce_sum(gm[:, 1:2], comb_t[:], axis=AX.X),
                lambda E: E.tensor_scalar(gm[:, 2:3], gm[:, 1:2], 1e-20, None, ALU.add),
                lambda E: E.reciprocal(gm[:, 3:4], gm[:, 2:3]),
                lambda E: E.tensor_single_scalar(comb_t[:], comb_t[:], gm[:, 3:4], ALU.mult),
            ]
            ev_tk = mk.chain(V, [ev_sig, ev_tk_prev2], emits)
            ev_tk_prev = ev_tk
            ev_tk_prev2 = ev_ct if False else None
            ev_ct = mk.step(T, [ev_tk, ps5_free],
                            lambda E: E.transpose(ps[5][0:NEXP, 0:128], comb_t[:],
                                                  ident_f32[:]))
            ev_tk_prev2 = ev_ct
            ev_ce = mk.step(V, [ev_ct], lambda E, tt=tt:
                            E.tensor_copy(combf[0:NEXP, tt * 128:(tt + 1) * 128],
                                          ps[5][0:NEXP, 0:128]))
            ps5_free = ev_ce
            ev_comb_all = ev_ce

        if half == 0:
            pass
        mk.fence_all()
        gsc_scope.close()
        wgu = [sbb(f"wgu{i}", [128, NDC * FEXP], BF16) for i in range(2)]
        wdsl = [sbb(f"wdsl{i}", [128, NFC * 512], BF16) for i in range(2)]
        actp = [sbb(f"actp{i}", [128, NFC * HALF], BF16) for i in range(2)]
        sgb = sbb("sgb", [128, NTG * 512], BF16)
        ub = sbb("ub", [128, 512], BF16)
        cbc = [sbb(f"cbc{i}", [128, HALF], BF16) for i in range(1)]
        ev_sel = mk.step(V, [], lambda E: E.memset(selm[:], 0.0))
        sel_cur = [None]
        ev_selmm = [None]

        wgu_free = [None, None]
        wd_free = [None] * 2
        cbc_free = [None]
        actp_free = [None, None]
        gu_bank_free = [None] * 4
        ybank_free = [None, None, None]
        ybank_free[1] = ps5_free
        ev_y_evac = {}

        ybank_free[1] = ps5_free
        ybank_free[2] = ps6b_free
        for e in range(NEXP):
            first_pass = (e == 0)
            ap_i = e % 2
            nq = max(1, NDC // 4)
            ev_wg, ev_wu = [], []
            for ix, qq in enumerate(range(0, NDC, nq)):
                ev_wg.append(mk.dma(Y if ix % 2 == 0 else S,
                                    [wgu_free[0], pump.done[("wg", e)]],
                                    wgu[0][:].rearrange("p (dc f) -> p dc f", f=FEXP)
                                    [:, qq:qq + nq, :],
                                    WG_BF[e, qq:qq + nq].transpose([1, 0, 2])))
            for ix, qq in enumerate(range(0, NDC, nq)):
                ev_wu.append(mk.dma(Y if ix % 2 == 1 else S,
                                    [wgu_free[1], pump.done[("wu", e)]],
                                    wgu[1][:].rearrange("p (dc f) -> p dc f", f=FEXP)
                                    [:, qq:qq + nq, :],
                                    WU_BF[e, qq:qq + nq].transpose([1, 0, 2])))
            ev_selrow = mk.step(V, [ev_sel, ev_selmm[0]],
                                lambda E, e=e: E.tensor_scalar(
                selm[:], iota_p[:, 0:1].to_broadcast([128, 128]),
                float(e), None, ALU.is_equal))
            last_gmm = None
            for fc in range(NFC):
                for tg in range(NTG):
                    bk = (NTG * fc + tg) % 4
                    bank = ps[bk]

                    def gmm(E, fc=fc, tg=tg, bank=bank):
                        out = None
                        for dc in range(NDC):
                            out = E.matmul(
                                bank[:],
                                wgu[0][:, dc * FEXP + fc * 128: dc * FEXP + (fc + 1) * 128],
                                xnh_t[:, dc * HALF + tg * 512: dc * HALF + (tg + 1) * 512],
                                start=(dc == 0), stop=(dc == NDC - 1))
                        return out
                    evb = mk.step(T, ev_wg + [ev_xnh, gu_bank_free[bk]], gmm)
                    last_gmm = evb
                    ev_si = mk.step(S, [evb] +
                                    ([actp_free[ap_i]] if (fc == 0 and tg == 0) else []),
                                    lambda E, fc=fc, tg=tg, bank=bank, ap_i=ap_i:
                                    E.activation(
                        out=actp[ap_i][:, fc * HALF + tg * 512: fc * HALF + (tg + 1) * 512],
                        in_=bank[:], func=AF.Silu))
                    gu_bank_free[bk] = ev_si
            wgu_free[0] = last_gmm
            ev_cb = None
            for tg in range(NTG):
                bkc = tg % 2
                evb = mk.step(T, [ev_comb_all, gu_bank_free[bkc], ev_selrow] +
                              ([cbc_free[0]] if tg == 0 else []),
                              lambda E, tg=tg, bkc=bkc:
                              E.matmul(ps[bkc][:], selm[0:NEXP, :],
                                       combf[0:NEXP, tg * 512:(tg + 1) * 512],
                                       start=True, stop=True))
                ev_cb = mk.step(V, [evb], lambda E, tg=tg, bkc=bkc:
                                E.tensor_copy(cbc[0][:, tg * 512:(tg + 1) * 512],
                                              ps[bkc][:]))
                gu_bank_free[bkc] = ev_cb
            ev_selmm[0] = evb
            last_uv = None
            for fc in range(NFC):
                for tg in range(NTG):
                    bk = (NTG * fc + tg) % 4
                    bank = ps[bk]

                    def umm(E, fc=fc, tg=tg, bank=bank):
                        out = None
                        for dc in range(NDC):
                            out = E.matmul(
                                bank[:],
                                wgu[1][:, dc * FEXP + fc * 128: dc * FEXP + (fc + 1) * 128],
                                xnh_t[:, dc * HALF + tg * 512: dc * HALF + (tg + 1) * 512],
                                start=(dc == 0), stop=(dc == NDC - 1))
                        return out
                    evb = mk.step(T, ev_wu + [ev_xnh, gu_bank_free[bk]], umm)
                    asl = actp[ap_i][:, fc * HALF + tg * 512: fc * HALF + (tg + 1) * 512]
                    ev_uc = mk.chain(V, [evb, ev_cb, gu_bank_free[bk], last_uv], [
                        lambda E, tg=tg, bank=bank:
                            E.tensor_mul(ub[:, 0:512], bank[:],
                                         cbc[0][:, tg * 512:(tg + 1) * 512]),
                        lambda E, asl=asl:
                            E.tensor_mul(asl, asl, ub[:, 0:512]),
                    ])
                    gu_bank_free[bk] = ev_uc
                    last_uv = ev_uc
            wgu_free[1] = evb
            ev_act = last_uv
            cbc_free[0] = last_uv
            lastd = None
            for dg in range(NDG):
                slot = dg % 2
                wd_free[slot] = mk.dma(
                    Y if dg % 2 == 0 else S, [wd_free[slot], pump.done[("wd", e)]],
                    wdsl[slot][:].rearrange("p (fc f) -> p fc f", f=512),
                    WD_BF[e, :, :, dg * 512:(dg + 1) * 512].transpose([1, 0, 2]))
                for tt in range(NTT):
                    ybk = (dg * NTT + tt) % 3
                    ybank = ps[4 + ybk]

                    def ymm(E, dg=dg, tt=tt, ybank=ybank, slot=slot, ap_i=ap_i):
                        out = None
                        for fc in range(NFC):
                            out = E.matmul(
                                ybank[:],
                                actp[ap_i][:, fc * HALF + tt * 128: fc * HALF + (tt + 1) * 128],
                                wdsl[slot][:, fc * 512:(fc + 1) * 512],
                                start=(fc == 0), stop=(fc == NFC - 1))
                        return out
                    ev_ym = mk.step(T, [ev_act, wd_free[slot], ybank_free[ybk]], ymm)
                    ysl = y_acc[:, tt * D + dg * 512: tt * D + (dg + 1) * 512]
                    if first_pass:
                        ev_ye = mk.step(V, [ev_ym], lambda E, ysl=ysl, ybank=ybank:
                                        E.tensor_copy(ysl, ybank[:]))
                    else:
                        ev_ye = mk.step(V, [ev_ym, ev_y_evac.get((tt, dg))],
                                        lambda E, ysl=ysl, ybank=ybank:
                                        E.tensor_add(ysl, ysl, ybank[:]))
                    ev_y_evac[(tt, dg)] = ev_ye
                    ybank_free[ybk] = ev_ye
                    lastd = ev_ym
                wd_free[slot] = lastd
            actp_free[ap_i] = lastd
        mk.fence_all()
        b.close()

        bs = ExitStack()

        def sbs(name, shape, dtype):
            return bs.enter_context(nc.sbuf_tensor(name + str(half), shape, dtype))

        NSH = c.NSC // 2
        FSH2 = NSH * 128
        sgh = sbs("sgh", [128, NDC * FSH2], BF16)
        suh = sbs("suh", [128, NDC * FSH2], BF16)
        act_sh = sbs("act_sh", [128, c.NSC * HALF], BF16)
        sdsl = [sbs(f"sdsl{i}", [128, c.NSC * 512], BF16) for i in range(2)]

        sh_bank_free = [None] * 4
        shscr_free = None
        ev_shact = None
        sgh_free = None
        for sh in range(2):
            nq = max(1, NDC // 4)
            ev_sg, ev_su = [], []
            for qq in range(0, NDC, nq):
                ev_sg.append(mk.dma(Y, [sgh_free, pump.done[("sg", sh)]],
                                    sgh[:].rearrange("p (dc f) -> p dc f", f=FSH2)
                                    [:, qq:qq + nq, :],
                                    SG_BF[sh, qq:qq + nq].transpose([1, 0, 2])))
            for qq in range(0, NDC, nq):
                ev_su.append(mk.dma(Y, [sgh_free, pump.done[("su", sh)]],
                                    suh[:].rearrange("p (dc f) -> p dc f", f=FSH2)
                                    [:, qq:qq + nq, :],
                                    SU_BF[sh, qq:qq + nq].transpose([1, 0, 2])))
            last_h_mm = None
            for fc in range(NSH):
                sc = sh * NSH + fc
                bank_evs = []
                for mi in range(2):
                    wsb = sgh if mi == 0 else suh
                    wdep = ev_sg if mi == 0 else ev_su
                    for tg in range(NTG):
                        bank = ps[2 * mi + (tg % 2)]

                        def hmm(E, wsb=wsb, fc=fc, tg=tg, bank=bank):
                            out = None
                            for dc in range(NDC):
                                out = E.matmul(
                                    bank[:],
                                    wsb[:, dc * FSH2 + fc * 128: dc * FSH2 + (fc + 1) * 128],
                                    xnh_t[:, dc * HALF + tg * 512: dc * HALF + (tg + 1) * 512],
                                    start=(dc == 0), stop=(dc == NDC - 1))
                            return out
                        evb = mk.step(T, wdep + [ev_xnh,
                                                  sh_bank_free[2 * mi + (tg % 2)]], hmm)
                        bank_evs.append(evb)
                        last_h_mm = evb
                ev_si = mk.step(S, bank_evs[0:NTG] +
                                ([shscr_free] if shscr_free else []),
                                lambda E, sc=sc: [
                    E.activation(out=act_sh[:, sc * HALF + tg * 512: sc * HALF + (tg + 1) * 512],
                                 in_=ps[tg % 2][:], func=AF.Silu)
                    for tg in range(NTG)])
                ev_ac = mk.step(V, bank_evs[NTG:2 * NTG] + [ev_si],
                                lambda E, sc=sc: [
                    E.tensor_mul(act_sh[:, sc * HALF + tg * 512: sc * HALF + (tg + 1) * 512],
                                 act_sh[:, sc * HALF + tg * 512: sc * HALF + (tg + 1) * 512],
                                 ps[2 + (tg % 2)][:])
                    for tg in range(NTG)])
                for z in range(2):
                    sh_bank_free[z] = ev_si
                    sh_bank_free[2 + z] = ev_ac
                shscr_free = ev_ac
                ev_shact = ev_ac
            sgh_free = last_h_mm
        sd_free = [None, None]
        for dg in range(NDG):
            slot = dg % 2
            nq = max(1, c.NSC // 3)
            ev_sdw = []
            for qq in range(0, c.NSC, nq):
                ev_sdw.append(mk.dma(Y, [sd_free[slot], pump.done[("sd", 0)]],
                                     sdsl[slot][:].rearrange("p (fc f) -> p fc f", f=512)
                                     [:, qq:qq + nq, :],
                                     SD_BF[qq:qq + nq, :, dg * 512:(dg + 1) * 512]
                                     .transpose([1, 0, 2])))
            lastsd = None
            for tt in range(NTT):
                ybk = (dg * NTT + tt) % 3
                ybank = ps[4 + ybk]

                def smm(E, dg=dg, tt=tt, ybank=ybank, slot=slot):
                    out = None
                    for fc in range(c.NSC):
                        out = E.matmul(
                            ybank[:],
                            act_sh[:, fc * HALF + tt * 128: fc * HALF + (tt + 1) * 128],
                            sdsl[slot][:, fc * 512:(fc + 1) * 512],
                            start=(fc == 0), stop=(fc == c.NSC - 1))
                    return out
                ev_ym = mk.step(T, [ev_shact] + ev_sdw + [ybank_free[ybk]], smm)
                ysl = y_acc[:, tt * D + dg * 512: tt * D + (dg + 1) * 512]
                ev_ye = mk.step(V, [ev_ym, ev_y_evac.get((tt, dg))],
                                lambda E, ysl=ysl, ybank=ybank:
                                E.tensor_add(ysl, ysl, ybank[:]))
                ev_y_evac[(tt, dg)] = ev_ye
                ybank_free[ybk] = ev_ye
                lastsd = ev_ym
            sd_free[slot] = lastsd
        mk.fence_all()
        bs.close()

        cc = ExitStack()

        def sbc(name, shape, dtype):
            return cc.enter_context(nc.sbuf_tensor(name + str(half), shape, dtype))

        x_t = [sbc(f"cx{i}", [128, D], F32) for i in range(2)]
        at_t = [sbc(f"cat{i}", [128, D], F32) for i in range(2)]
        o_t = [sbc(f"cot{i}", [128, D], F32) for i in range(2)]
        xt_free = [None, None]
        at_free = [None, None]
        ot_free = [None, None]
        for tt in range(NTT):
            sl = tt % 2
            row0 = half * HALF + tt * 128
            ev_x = mk.dma(Y, [xt_free[sl]], x_t[sl][:],
                          X[c.LB + row0: c.LB + row0 + 128, :])
            ev_a = mk.dma(Y, [at_free[sl]], at_t[sl][:], ATTN[row0:row0 + 128, :])
            deps = [ev_x, ev_a, ot_free[sl]] + \
                   [ev_y_evac[(tt, dg)] for dg in range(NDG)]
            ev_s = mk.chain(V, deps, [
                lambda E, sl=sl: E.tensor_add(o_t[sl][:], x_t[sl][:], at_t[sl][:]),
                lambda E, sl=sl, tt=tt: E.tensor_add(o_t[sl][:], o_t[sl][:],
                                                     y_acc[:, tt * D:(tt + 1) * D]),
            ])
            xt_free[sl] = ev_s
            at_free[sl] = ev_s
            ev_od = mk.dma(Y, [ev_s], OUT[row0:row0 + 128, :], o_t[sl][:])
            ot_free[sl] = ev_od
            ev_out_final.append(ev_od)
        cc.close()

    for _h in range(2):
        build_half(_h)

    yacc_scope.close()

    for k, eng in mk.dma_eng.items():
        mk._wait(eng, (k, mk.cnt[k]))
    mk.step(Y, ev_out_final, lambda E: E.nop())
    mk.build_blocks()
    return nc


def shard_inputs(cfg, x, wqT, wkT, wvT, woT, gate_r, wg, wu, wd, sg, su, sd):
    in_maps = []
    ncores = cfg.B * cfg.NQ
    for core in range(ncores):
        bidx, q = core // cfg.NQ, core % cfg.NQ
        xp = np.zeros((cfg.SPT, cfg.D), np.float32)
        xp[cfg.LB - q * cfg.OWN:] = x[bidx, :(q + 1) * cfg.OWN]
        pos = np.zeros((1, cfg.SPT), np.float32)
        pos[0, cfg.LB - q * cfg.OWN:] = np.arange((q + 1) * cfg.OWN, dtype=np.float32)
        in_maps.append({
            "x": xp, "pos": pos, "wqT": wqT, "wkT": wkT, "wvT": wvT, "woT": woT,
            "gate_r": gate_r, "wg": wg, "wu": wu, "wd": wd,
            "sg": sg, "su": su, "sd": sd,
        })
    return in_maps


def prep_weights(cfg, wq, wk, wv, wo, gate_w, wg, wu, wd, sg, su, sd):
    f = np.float32
    gate = np.asarray(gate_w, f)
    gate_r = np.ascontiguousarray(
        gate.reshape(cfg.NDC, 128, cfg.NEXP).transpose(1, 0, 2)
        .reshape(128, cfg.NDC * cfg.NEXP))
    return dict(
        wqT=np.ascontiguousarray(np.asarray(wq, f).T),
        wkT=np.ascontiguousarray(np.asarray(wk, f).T),
        wvT=np.ascontiguousarray(np.asarray(wv, f).T),
        woT=np.ascontiguousarray(np.asarray(wo, f).T),
        gate_r=gate_r,
        wg=np.ascontiguousarray(np.asarray(wg, f)),
        wu=np.ascontiguousarray(np.asarray(wu, f)),
        wd=np.ascontiguousarray(np.asarray(wd, f)),
        sg=np.ascontiguousarray(np.asarray(sg, f)),
        su=np.ascontiguousarray(np.asarray(su, f)),
        sd=np.ascontiguousarray(np.asarray(sd, f)),
    )


def kernel(L0_attn_x, L0_attn_wq, L0_attn_wk, L0_attn_wv, L0_attn_wo,
           L0_ffn_gate_w, L0_ffn_wg, L0_ffn_wu, L0_ffn_wd,
           L0_ffn_sg, L0_ffn_su, L0_ffn_sd):
    cfg = CFG()
    x = np.asarray(L0_attn_x, dtype=np.float32)
    w = prep_weights(cfg, L0_attn_wq, L0_attn_wk, L0_attn_wv, L0_attn_wo,
                     L0_ffn_gate_w, L0_ffn_wg, L0_ffn_wu, L0_ffn_wd,
                     L0_ffn_sg, L0_ffn_su, L0_ffn_sd)
    nc = bass.Bass("TRN2", target_bir_lowering=False)
    build_program(nc, cfg)
    in_maps = shard_inputs(cfg, x, **w)
    res = run_bass_kernel_spmd(nc, in_maps, core_ids=list(range(8)))
    out = np.empty((cfg.B, cfg.S, cfg.D), np.float32)
    for core in range(8):
        bidx, q = core // cfg.NQ, core % cfg.NQ
        out[bidx, q * cfg.OWN:(q + 1) * cfg.OWN] = res.results[core]["out"]
    return out


# version 17
# speedup vs baseline: 1.1753x; 1.0108x over previous
import math
import numpy as np
from contextlib import ExitStack

import concourse.bass as bass
import concourse.mybir as mybir
from concourse.bass_utils import run_bass_kernel_spmd

F32 = mybir.dt.float32
BF16 = mybir.dt.bfloat16
I32 = mybir.dt.int32
AF = mybir.ActivationFunctionType
ALU = mybir.AluOpType
AX = mybir.AxisListType

ENGINES = ("sync", "scalar", "vector", "tensor", "gpsimd")
N_DMA_SEMS = 24


class MK:

    def __init__(self, nc):
        self.nc = nc
        self.es = ExitStack()
        self.sems = {e: self.es.enter_context(nc.semaphore(f"mk_{e}")) for e in ENGINES}
        for i in range(N_DMA_SEMS):
            self.sems[i] = self.es.enter_context(nc.semaphore(f"mkd_{i}"))
        self.cnt = {k: 0 for k in self.sems}
        self.dma_rr = 0
        self.dma_eng = {}
        self.waited = {}
        self.phases = []
        self.cur = None
        self.phase()

    def phase(self):
        if self.cur is not None:
            self.fence_dma()
        self.cur = {e: [] for e in ENGINES}
        self.phases.append(self.cur)
        for cns in ENGINES:
            for k in self.sems:
                self.waited[(cns, k)] = self.cnt[k]

    def _wait(self, eng, ev):
        if ev is None:
            return
        k, val = ev
        if self.waited.get((eng, k), 0) >= val:
            return
        self.waited[(eng, k)] = val
        sem = self.sems[k]
        self.cur[eng].append(lambda E, sem=sem, val=val: E.wait_ge(sem, val))

    def step(self, eng, deps, emit, inc=1, key=None):
        for d in deps:
            self._wait(eng, d)
        k = eng if key is None else key
        self.cnt[k] += inc
        sem = self.sems[k]

        def run(E, emit=emit, sem=sem, inc=inc):
            last = emit(E)
            if isinstance(last, (list, tuple)):
                last = last[-1]
            last.then_inc(sem, inc)
        self.cur[eng].append(run)
        return (k, self.cnt[k])

    def chain(self, eng, deps, emits, extra=None):
        ev = None
        for i, em in enumerate(emits):
            d = list(deps if i == 0 else []) + ([ev] if ev else [])
            if extra and extra[i]:
                d += list(extra[i])
            ev = self.step(eng, d, em)
        return ev

    def fence_dma(self):
        for k, eng in self.dma_eng.items():
            self._wait(eng, (k, self.cnt[k]))

    def fence_all(self):
        self.fence_dma()
        for e in ENGINES:
            if self.cnt[e]:
                self._wait("sync", (e, self.cnt[e]))

    def dma(self, eng, deps, out, in_, **kw):
        k = self.dma_rr % N_DMA_SEMS
        self.dma_rr += 1
        self.dma_eng[k] = eng
        self._wait(eng, (k, self.cnt[k]))
        return self.step(eng, deps,
                         lambda E: E.dma_start(out=out, in_=in_, **kw),
                         inc=16, key=k)

    def build_blocks(self):
        for ph in self.phases:
            with self.nc.Block() as blk:
                for e in ENGINES:
                    ops = ph[e]
                    if not ops:
                        continue
                    getattr(blk, e)(lambda E, ops=ops: [f(E) for f in ops])


class CFG:
    def __init__(self, D=2048, NH=16, NKV=4, FEXP=768, NEXP=32, FSH=1536,
                 TOPK=6, OWN=2048, LB=6144, NQ=4, B=2, S=8192, debug=False):
        self.debug = debug
        self.D, self.NH, self.NKV = D, NH, NKV
        self.FEXP, self.NEXP, self.FSH, self.TOPK = FEXP, NEXP, FSH, TOPK
        self.OWN, self.LB, self.NQ, self.B, self.S = OWN, LB, NQ, B, S
        self.DH = 128
        self.NDC = D // 128
        self.NFC = FEXP // 128
        self.NSC = FSH // 128
        self.GR = NH // NKV
        self.NG = NH // 4
        assert self.GR == 4 and self.NG == NKV
        self.SPT = OWN + LB
        self.NT = self.SPT // 128
        self.NT_OWN = OWN // 128
        self.NST = self.SPT // 512
        self.NST_OWN = OWN // 512
        self.HALF = OWN // 2
        self.NTT = self.HALF // 128
        self.NTG = self.HALF // 512
        self.NDG = D // 512
        assert self.NTG >= 1 and LB == (NQ - 1) * OWN
        self.lam = [2.0 ** (-8.0 * (h + 1) / NH) for h in range(NH)]
        self.blk = [math.exp(-l * 128.0) for l in self.lam]


def build_program(nc, c):
    D, NDC, NH, NKV = c.D, c.NDC, c.NH, c.NKV
    NG, NFC, NEXP, FEXP = c.NG, c.NFC, c.NEXP, c.FEXP
    NTT, NDG, NTG, HALF = c.NTT, c.NDG, c.NTG, c.HALF
    KW = NKV * 128
    QW = NH * 128

    X = nc.dram_tensor("x", [c.SPT, D], F32, kind="ExternalInput").ap()
    POS = nc.dram_tensor("pos", [1, c.SPT], F32, kind="ExternalInput").ap()
    WQT = nc.dram_tensor("wqT", [D, QW], F32, kind="ExternalInput").ap()
    WKT = nc.dram_tensor("wkT", [D, KW], F32, kind="ExternalInput").ap()
    WVT = nc.dram_tensor("wvT", [D, KW], F32, kind="ExternalInput").ap()
    WOT = nc.dram_tensor("woT", [QW, D], F32, kind="ExternalInput").ap()
    GATE = nc.dram_tensor("gate_r", [128, NDC * NEXP], F32, kind="ExternalInput").ap()
    WG = nc.dram_tensor("wg", [NEXP, D, FEXP], F32, kind="ExternalInput").ap()
    WU = nc.dram_tensor("wu", [NEXP, D, FEXP], F32, kind="ExternalInput").ap()
    WD = nc.dram_tensor("wd", [NEXP, FEXP, D], F32, kind="ExternalInput").ap()
    SG = nc.dram_tensor("sg", [D, c.FSH], F32, kind="ExternalInput").ap()
    SU = nc.dram_tensor("su", [D, c.FSH], F32, kind="ExternalInput").ap()
    SD = nc.dram_tensor("sd", [c.FSH, D], F32, kind="ExternalInput").ap()
    OUT = nc.dram_tensor("out", [c.OWN, D], F32, kind="ExternalOutput").ap()

    scr_kind = "ExternalOutput" if c.debug else "Internal"
    WG_BF = nc.dram_tensor("wg_bf_scr", [NEXP, NDC, 128, FEXP], BF16, kind="Internal").ap()
    WU_BF = nc.dram_tensor("wu_bf_scr", [NEXP, NDC, 128, FEXP], BF16, kind="Internal").ap()
    WD_BF = nc.dram_tensor("wd_bf_scr", [NEXP, NFC, 128, D], BF16, kind="Internal").ap()
    NSH2 = c.NSC // 2
    FSH2c = NSH2 * 128
    SG_BF = nc.dram_tensor("sg_bf_scr", [2, NDC, 128, FSH2c], BF16, kind="Internal").ap()
    SU_BF = nc.dram_tensor("su_bf_scr", [2, NDC, 128, FSH2c], BF16, kind="Internal").ap()
    SD_BF = nc.dram_tensor("sd_bf_scr", [c.NSC, 128, D], BF16, kind="Internal").ap()
    XN_SCR = nc.dram_tensor("xn_scr", [128, NDC * c.OWN], BF16, kind=scr_kind).ap()
    KVSNAP = nc.dram_tensor("kvsnap", [c.NT_OWN, 128, NH * 128], BF16, kind=scr_kind).ap()
    OFEAT = nc.dram_tensor("ofeat", [c.NT_OWN, 128, NH * 128], BF16, kind=scr_kind).ap()
    ATTN = nc.dram_tensor("attn_scr", [c.OWN, D], F32, kind=scr_kind).ap()

    ident_bf = nc.alloc_sbuf_tensor("ident_bf", [128, 128], BF16)
    ident_f32 = nc.alloc_sbuf_tensor("ident_f32", [128, 128], F32)
    iota_p = nc.alloc_sbuf_tensor("iota_p", [128, 1], F32)
    iota_row = nc.alloc_sbuf_tensor("iota_row", [128, 128], F32)
    ones_row = nc.alloc_sbuf_tensor("ones_row", [1, 128], F32)
    pi_col = nc.alloc_sbuf_tensor("pi_col", [128, 1], F32)
    eps_col = nc.alloc_sbuf_tensor("eps_col", [128, 1], F32)

    NSLOT = 4
    ps = [nc.alloc_psum_tensor(f"psb{i}", [128, 512], F32) for i in range(7)]
    ps7 = nc.alloc_psum_tensor("psb7", [128, 1024], BF16)

    mk = MK(nc)
    V, S, T, G, Y = "vector", "scalar", "tensor", "gpsimd", "sync"

    class Pump:
        def __init__(self):
            self.jobs = []
            self.done = {}
            self.i = 0
            self.stg_free = [None] * NSLOT
            self.bf_free = [None] * NSLOT
            self.ncast = 0

        def add(self, key, blocks):
            for b_ in blocks:
                self.jobs.append((key,) + b_)

        def tick(self, n):
            for _ in range(n):
                if self.i >= len(self.jobs):
                    return
                key, sap, dap, wdt = self.jobs[self.i]
                s = self.i % NSLOT
                ld = mk.dma(G, [self.stg_free[s]], pstg[s][:, 0:wdt], sap)
                if self.i % 2 == 0:
                    ce = mk.step(V, [ld, self.bf_free[s]],
                                 lambda E, s=s, wdt=wdt:
                                 E.tensor_copy(pbf[s][:, 0:wdt], pstg[s][:, 0:wdt]))
                else:
                    ce = mk.step(S, [ld, self.bf_free[s]],
                                 lambda E, s=s, wdt=wdt:
                                 E.copy(pbf[s][:, 0:wdt], pstg[s][:, 0:wdt]))
                self.stg_free[s] = ce
                st = mk.dma((S, Y)[self.i % 2], [ce], dap, pbf[s][:, 0:wdt])
                self.bf_free[s] = st
                self.done[key] = st
                self.i += 1

        def finish(self):
            self.tick(len(self.jobs))

    pump = Pump()
    GB = max(1, min(NDC, 1024 // FEXP))
    for e in range(NEXP):
        pump.add(("wg", e), [
            (WG[e, dc * 128:(dc + GB) * 128, :].rearrange("(b p) f -> p b f", p=128),
             WG_BF[e, dc:dc + GB].transpose([1, 0, 2]),
             GB * FEXP)
            for dc in range(0, NDC, GB)])
        pump.add(("wu", e), [
            (WU[e, dc * 128:(dc + GB) * 128, :].rearrange("(b p) f -> p b f", p=128),
             WU_BF[e, dc:dc + GB].transpose([1, 0, 2]),
             GB * FEXP)
            for dc in range(0, NDC, GB)])
        WDW = min(1024, D)
        pump.add(("wd", e), [(WD[e, fc * 128:(fc + 1) * 128, hh * WDW:(hh + 1) * WDW],
                              WD_BF[e, fc, :, hh * WDW:(hh + 1) * WDW], WDW)
                             for fc in range(NFC) for hh in range(D // WDW)])
    for sh in range(2):
        pump.add(("sg", sh), [(SG[dc * 128:(dc + 1) * 128, sh * FSH2c:(sh + 1) * FSH2c],
                               SG_BF[sh, dc], FSH2c) for dc in range(NDC)])
        pump.add(("su", sh), [(SU[dc * 128:(dc + 1) * 128, sh * FSH2c:(sh + 1) * FSH2c],
                               SU_BF[sh, dc], FSH2c) for dc in range(NDC)])
    WDW2 = min(1024, D)
    pump.add(("sd", 0), [(SD[fc * 128:(fc + 1) * 128, hh * WDW2:(hh + 1) * WDW2],
                          SD_BF[fc, :, hh * WDW2:(hh + 1) * WDW2], WDW2)
                         for fc in range(c.NSC) for hh in range(D // WDW2)])
    PUMP_PER_TILE = (len(pump.jobs) + c.NT - 1) // c.NT

    a_scope = ExitStack()

    def sba(name, shape, dtype):
        return a_scope.enter_context(nc.sbuf_tensor(name, shape, dtype))

    pstg = [sba(f"pump_f{i}", [128, 1024], F32) for i in range(NSLOT)]
    pbf = [sba(f"pump_b{i}", [128, 1024], BF16) for i in range(NSLOT)]
    cos_own = sba("cos_o", [128, c.OWN], BF16)
    sin_own = sba("sin_o", [128, c.OWN], BF16)
    rotm = sba("rotm", [128, 128], BF16)
    kdec_cols = sba("kdec_cols", [128, NH], F32)
    k_own = sba("k_own", [128, NKV * c.OWN], BF16)
    v_own = sba("v_own", [128, c.NT_OWN * KW], BF16)

    lb_scope = ExitStack()
    cos_lb = lb_scope.enter_context(nc.sbuf_tensor("cos_lb", [128, c.LB], BF16))
    sin_lb = lb_scope.enter_context(nc.sbuf_tensor("sin_lb", [128, c.LB], BF16))

    def tabsl(which, lo, hi):
        if hi <= c.LB:
            return (cos_lb if which == 0 else sin_lb)[:, lo:hi]
        return (cos_own if which == 0 else sin_own)[:, lo - c.LB:hi - c.LB]

    p0 = ExitStack()
    pos_sb = p0.enter_context(nc.sbuf_tensor("pos_sb", [1, c.SPT], F32))
    invfreq = p0.enter_context(nc.sbuf_tensor("invfreq", [1, 128], F32))
    iv_raw = p0.enter_context(nc.sbuf_tensor("iv_raw", [1, 128], F32))
    iv_g = p0.enter_context(nc.sbuf_tensor("iv_g", [1, 128], F32))
    rampb = p0.enter_context(nc.sbuf_tensor("rampb", [128, 128], F32))
    scr0 = p0.enter_context(nc.sbuf_tensor("p0scr0", [128, 512], F32))
    scr1 = p0.enter_context(nc.sbuf_tensor("p0scr1", [128, 512], F32))
    scri = p0.enter_context(nc.sbuf_tensor("p0scri", [128, 512], I32))
    scrH = p0.enter_context(nc.sbuf_tensor("p0scrH", [128, 512], F32))
    scrS = p0.enter_context(nc.sbuf_tensor("p0scrS", [128, 512], F32))

    ev_iota = mk.step(G, [], lambda E: [
        E.iota(iota_p[:], [[1, 1]], base=0, channel_multiplier=1,
               allow_small_or_imprecise_dtypes=True),
        E.iota(iota_row[:], [[1, 128]], base=0, channel_multiplier=0,
               allow_small_or_imprecise_dtypes=True),
        E.iota(iv_raw[:], [[1, 128]], base=0, channel_multiplier=0,
               allow_small_or_imprecise_dtypes=True),
    ])
    ev_pos = mk.dma(Y, [], pos_sb[:], POS)
    ev_const = mk.step(V, [ev_iota], lambda E: [
        E.tensor_single_scalar(ident_bf[:], iota_row[:], iota_p[:, 0:1], ALU.is_equal),
        E.tensor_single_scalar(ident_f32[:], iota_row[:], iota_p[:, 0:1], ALU.is_equal),
        E.memset(ones_row[:], 1.0),
        E.memset(pi_col[:], math.pi),
        E.memset(eps_col[:], 1e-5),
    ])
    ev_ivm = mk.chain(V, [ev_iota, ev_const], [
        lambda E: E.tensor_single_scalar(iv_g[:], iv_raw[:], 64.0, ALU.is_ge),
        lambda E: E.scalar_tensor_tensor(iv_g[:], iv_g[:], -64.0, iv_raw[:],
                                         ALU.mult, ALU.add),
    ])
    ev_if = mk.step(S, [ev_ivm], lambda E:
                    E.activation(out=invfreq[:], in_=iv_g[:], func=AF.Exp,
                                 scale=-math.log(10000.0) / 64.0))
    ev_rotm = mk.chain(V, [ev_iota, ev_const], [
        lambda E: E.tensor_scalar(scr0[:, 0:1], iota_p[:, 0:1], 1.0, 64.0,
                                  ALU.mult, ALU.add),
        lambda E: E.tensor_single_scalar(scr1[:, 0:128], iota_row[:], scr0[:, 0:1],
                                         ALU.is_equal),
        lambda E: E.tensor_scalar(scr0[:, 0:1], iota_p[:, 0:1], 1.0, -64.0,
                                  ALU.mult, ALU.add),
        lambda E: E.tensor_single_scalar(scr1[:, 128:256], iota_row[:], scr0[:, 0:1],
                                         ALU.is_equal),
        lambda E: E.tensor_sub(rotm[:], scr1[:, 0:128], scr1[:, 128:256]),
    ])

    ev_tab_chain = []
    for i in range(c.SPT // 512):
        sl = slice(i * 512, (i + 1) * 512)
        bank = ps[i % 2]
        dep_bank = ev_tab_chain[-2][0] if len(ev_tab_chain) >= 2 else None
        evm = mk.step(T, [ev_if, ev_pos, dep_bank, ev_rotm],
                      lambda E, sl=sl, bank=bank:
                      E.matmul(bank[:], invfreq[0:1, :], pos_sb[0:1, sl],
                               start=True, stop=True))
        prev_s = ev_tab_chain[-1][1] if ev_tab_chain else None
        ev_last = None
        for which in (1, 0):
            shift = 0.0 if which == 1 else 0.5
            evf = mk.chain(V, [evm, ev_const, prev_s if which == 1 else ev_last], [
                lambda E, bank=bank, shift=shift: E.tensor_scalar(
                    scrH[:, 0:512], bank[:], 1.0 / math.pi, shift, ALU.mult, ALU.add),
                lambda E: E.tensor_copy(scri[:, 0:512], scrH[:, 0:512]),
                lambda E: E.tensor_copy(scr1[:, 0:512], scri[:, 0:512]),
                lambda E: E.tensor_sub(scr0[:, 0:512], scrH[:, 0:512], scr1[:, 0:512]),
                lambda E: E.tensor_single_scalar(scrS[:, 0:512], scr0[:, 0:512], 0.5,
                                                 ALU.is_gt),
                lambda E: E.tensor_single_scalar(scrH[:, 0:512], scr0[:, 0:512], -0.5,
                                                 ALU.is_lt),
                lambda E: E.tensor_sub(scrS[:, 0:512], scrS[:, 0:512], scrH[:, 0:512]),
                lambda E: E.tensor_sub(scr0[:, 0:512], scr0[:, 0:512], scrS[:, 0:512]),
                lambda E: E.tensor_add(scr1[:, 0:512], scr1[:, 0:512], scrS[:, 0:512]),
                lambda E: E.tensor_scalar(scrH[:, 0:512], scr1[:, 0:512], 0.5,
                                          None, ALU.mult),
                lambda E: E.tensor_copy(scri[:, 0:512], scrH[:, 0:512]),
                lambda E: E.tensor_copy(scrH[:, 0:512], scri[:, 0:512]),
                lambda E: E.scalar_tensor_tensor(scrH[:, 0:512], scrH[:, 0:512], -2.0,
                                                 scr1[:, 0:512], ALU.mult, ALU.add),
                lambda E: E.tensor_mul(scrH[:, 0:512], scrH[:, 0:512], scrH[:, 0:512]),
                lambda E: E.tensor_scalar(scrH[:, 0:512], scrH[:, 0:512], -2.0, 1.0,
                                          ALU.mult, ALU.add),
            ])
            evs0 = mk.step(S, [evf], lambda E:
                           E.activation(out=scrS[:, 0:512], in_=scr0[:, 0:512],
                                        func=AF.Sin, scale=math.pi))
            tdst = tabsl(which, sl.start, sl.stop)
            ev_last = mk.step(V, [evs0], lambda E, tdst=tdst:
                              E.tensor_mul(tdst, scrS[:, 0:512], scrH[:, 0:512]))
        ev_tab_chain.append((ev_last, ev_last))
    ev_tab_all = ev_tab_chain[-1][1]

    ev_h = ev_tab_chain[-1][0]
    for h in range(NH):
        lam = c.lam[h]
        e6 = mk.step(V, [ev_h], lambda E:
                     E.tensor_scalar(scr1[:, 0:1], iota_p[:, 0:1], -1.0, 127.0,
                                     ALU.mult, ALU.add))
        e7 = mk.step(S, [e6], lambda E, h=h, lam=lam: E.activation(
            out=kdec_cols[:, h:h + 1], in_=scr1[:, 0:1], func=AF.Exp, scale=-lam))
        ev_h = e7
    p0.close()

    mk.phase()
    a1 = ExitStack()

    def sb1(name, shape, dtype):
        return a1.enter_context(nc.sbuf_tensor(name, shape, dtype))

    nbn = 6 * ((D + 511) // 512)
    kv_sb = sb1("kv_sb", [128, NH * 128], F32)
    x_ring = [sb1(f"x_ring{i}", [128, D], F32) for i in range(2)]
    stats2 = [sb1(f"stats{i}", [128, 8], F32) for i in range(2)]
    bnrows2 = [sb1(f"bnrows{i}", [128, nbn], F32) for i in range(2)]
    xn_tok = [sb1(f"xn_tok{i}", [128, D], BF16) for i in range(1)]
    xn_st = [sb1(f"xn_st{i}", [128, NDC * 512], BF16) for i in range(2)]
    kraw = sb1("kraw", [128, NKV * 512], BF16)
    krot_scr = sb1("krot_scr", [128, NKV * 512], BF16)
    rsc = sb1("rsc", [128, 1024], BF16)
    kpos_t = [sb1(f"kpos{i}", [128, KW], BF16) for i in range(2)]
    kprime = sb1("kprime", [128, NH * 128], BF16)
    wk_bf = sb1("wk_bf", [128, NDC * KW], BF16)
    wv_bf = sb1("wv_bf", [128, NDC * KW], BF16)
    wstg = [p[:, 0:KW] for p in pstg]
    snap = [sb1(f"snap{i}", [128, NH * 128], BF16) for i in range(1)]
    v_scr = [sb1(f"vscr{i}", [128, KW], BF16) for i in range(2)]

    ev_kv0 = mk.step(V, [], lambda E: E.memset(kv_sb[:], 0.0))

    wfree = [None] * NSLOT
    last_cast = None
    for i, (W_, dst) in enumerate([(WKT, wk_bf), (WVT, wv_bf)]):
        for dc in range(NDC):
            s = (i * NDC + dc) % NSLOT
            ld = mk.dma(Y, [wfree[s]], wstg[s][:], W_[dc * 128:(dc + 1) * 128, :])
            ce = mk.step(V, [ld], lambda E, s=s, dst=dst, dc=dc:
                         E.tensor_copy(dst[:, dc * KW:(dc + 1) * KW], wstg[s][:]))
            wfree[s] = ce
            last_cast = ce
    ev_wkv = last_cast
    for s in range(NSLOT):
        pump.stg_free[s] = wfree[s]

    x_free = [None] * 2
    xn_free = [None, None]
    xnst_free = [[], []]
    kpos_free = [None, None]
    vbank_free = [None, None]
    vscr_free = [None, None]
    snap_free = [None]
    ps7_free = None
    ps6_free = None
    rot_bank_free = [None, None]
    kprime_free = None
    rsc_free = None
    ev_kv_upd = ev_kv0
    ev_k_prev_st = None
    ev_delta_prev_st = None
    stats_free = [None, None]
    kraw_free = None
    krot_free = None

    for st in range(c.NST):
        own_st = st >= c.NST - c.NST_OWN
        so = st - (c.NST - c.NST_OWN)
        xs = xn_st[st % 2]
        ev_tp_tiles = []
        for tl in range(4):
            t = st * 4 + tl
            xslot = t % 2
            stats = stats2[xslot]
            bnrows = bnrows2[xslot]
            ev_x = mk.dma(Y, [x_free[xslot]], x_ring[xslot][:],
                          X[t * 128:(t + 1) * 128, :])
            pump.tick(PUMP_PER_TILE)
            ngrp = (D + 511) // 512
            ev_bn = mk.chain(V, [ev_x, stats_free[xslot]], [
                lambda E, xslot=xslot, ngrp=ngrp, bnrows=bnrows: [
                    E.bn_stats(bnrows[:, g * 6:(g + 1) * 6],
                               x_ring[xslot][:, g * 512:min((g + 1) * 512, D)])
                    for g in range(ngrp)],
                lambda E, ngrp=ngrp, stats=stats, bnrows=bnrows:
                    E.bn_aggr(stats[:, 0:2], bnrows[:, 0:6 * ngrp]),
            ])
            ev_sq = mk.step(S, [ev_bn], lambda E, stats=stats: E.activation(
                out=stats[:, 5:6], in_=stats[:, 1:2], func=AF.Sqrt,
                bias=eps_col[:, 0:1]))
            ev_nmr = mk.chain(V, [ev_sq], [
                lambda E, stats=stats: E.reciprocal(stats[:, 2:3], stats[:, 5:6]),
                lambda E, stats=stats: E.tensor_scalar(stats[:, 3:4], stats[:, 0:1],
                                                       -1.0, None, ALU.mult),
                lambda E, stats=stats: E.tensor_mul(stats[:, 4:5], stats[:, 3:4],
                                                    stats[:, 2:3]),
            ])
            xnslot = 0
            ev_xn = mk.step(S, [ev_nmr, xn_free[xnslot]],
                            lambda E, xslot=xslot, xnslot=xnslot, stats=stats:
                            E.activation(
                out=xn_tok[xnslot][:], in_=x_ring[xslot][:], func=AF.Identity,
                scale=stats[:, 2:3], bias=stats[:, 4:5]))
            x_free[xslot] = ev_xn
            stats_free[xslot] = ev_xn
            ev_ev2 = None
            for hf in range(2):
                ndc_h = min(8, NDC - hf * 8)
                if ndc_h <= 0:
                    continue
                ev_tp = mk.step(T, [ev_xn, ps7_free] +
                                (xnst_free[st % 2] if (hf == 0 and tl == 0) else []),
                                lambda E, xnslot=xnslot, hf=hf, ndc_h=ndc_h: [
                    E.transpose(ps7[:, j * 128:(j + 1) * 128],
                                xn_tok[xnslot][:, (hf * 8 + j) * 128:(hf * 8 + j + 1) * 128],
                                ident_bf[:])
                    for j in range(ndc_h)])
                eng = V if hf == 0 else S
                if eng == V:
                    ev_ev2 = mk.step(V, [ev_tp],
                                     lambda E, xs=xs, tl=tl, hf=hf, ndc_h=ndc_h:
                                     E.tensor_copy(
                        xs[:].rearrange("p (dc t) -> p dc t", t=512)
                          [:, hf * 8:hf * 8 + ndc_h, tl * 128:(tl + 1) * 128],
                        ps7[:].rearrange("p (dc t) -> p dc t", t=128)[:, 0:ndc_h, :]))
                else:
                    ev_ev2 = mk.step(S, [ev_tp],
                                     lambda E, xs=xs, tl=tl, hf=hf, ndc_h=ndc_h:
                                     E.copy(
                        xs[:].rearrange("p (dc t) -> p dc t", t=512)
                          [:, hf * 8:hf * 8 + ndc_h, tl * 128:(tl + 1) * 128],
                        ps7[:].rearrange("p (dc t) -> p dc t", t=128)[:, 0:ndc_h, :]))
                ps7_free = ev_ev2
            xn_free[xnslot] = ev_ev2
            ev_tp_tiles.append(ev_ev2)
        ev_xn_dma = None
        if own_st:
            ev_xn_dma = mk.dma(Y, ev_tp_tiles,
                               XN_SCR[:].rearrange("p (dc t) -> p dc t", t=c.OWN)
                               [:, :, so * 512:(so + 1) * 512],
                               xs[:].rearrange("p (dc t) -> p dc t", t=512))

        ev_k_oc = []
        for oc in range(NKV):
            bank = ps[oc % 2]
            prev = ([ev_k_oc[oc - 2]] if oc >= 2
                    else [ev_k_prev_st, ev_delta_prev_st])

            def kmm(E, oc=oc, bank=bank, xs=xs):
                out = None
                for dc in range(NDC):
                    out = E.matmul(bank[:],
                                   wk_bf[:, dc * KW + oc * 128: dc * KW + (oc + 1) * 128],
                                   xs[:, dc * 512:(dc + 1) * 512],
                                   start=(dc == 0), stop=(dc == NDC - 1))
                return out
            evm = mk.step(T, ev_tp_tiles + [ev_wkv] + prev, kmm)
            eve = mk.step(V, [evm, kraw_free if oc == 0 else None],
                          lambda E, oc=oc, bank=bank:
                          E.tensor_copy(kraw[:, oc * 512:(oc + 1) * 512], bank[:]))
            ev_k_oc.append(eve)
        if own_st:
            kout = k_own[:].rearrange("p (kv t) -> p kv t", t=c.OWN)[:, :, so * 512:(so + 1) * 512]
        else:
            kout = krot_scr[:].rearrange("p (kv t) -> p kv t", t=512)
        ev_krot = None
        for kv in range(NKV):
            cosl = tabsl(0, st * 512, (st + 1) * 512)
            sinl = tabsl(1, st * 512, (st + 1) * 512)
            p7s = ps[4 + kv % 2][:]
            ev_rt = mk.step(T, [ev_k_oc[kv], ev_rotm, rot_bank_free[kv % 2]],
                            lambda E, kv=kv, p7s=p7s:
                            E.matmul(p7s, rotm[:], kraw[:, kv * 512:(kv + 1) * 512],
                                     start=True, stop=True))
            ev_krot = mk.chain(V, [ev_rt, ev_tab_all, rsc_free, krot_free], [
                lambda E, kv=kv, cosl=cosl:
                    E.tensor_tensor(rsc[:, 0:512], kraw[:, kv * 512:(kv + 1) * 512],
                                    cosl, ALU.mult),
                lambda E, p7s=p7s, sinl=sinl:
                    E.tensor_tensor(rsc[:, 512:1024], p7s, sinl, ALU.mult),
                lambda E, kv=kv, kout=kout:
                    E.tensor_add(kout[:, kv], rsc[:, 0:512], rsc[:, 512:1024]),
            ])
            rot_bank_free[kv % 2] = ev_krot
            rsc_free = ev_krot
        kraw_free = ev_krot
        ev_k_prev_st = ev_k_oc[-1]

        ev_last_delta_tile = None
        for tl in range(4):
            t = st * 4 + tl
            vb = tl % 2
            vbank = ps[2 + vb]

            def vmm(E, xs=xs, tl=tl, vbank=vbank):
                out = None
                for dc in range(NDC):
                    out = E.matmul(vbank[:, 0:KW],
                                   xs[:, dc * 512 + tl * 128: dc * 512 + (tl + 1) * 128],
                                   wv_bf[:, dc * KW:(dc + 1) * KW],
                                   start=(dc == 0), stop=(dc == NDC - 1))
                return out
            evv = mk.step(T, [ev_tp_tiles[tl], ev_wkv, vbank_free[vb]], vmm)
            ev_last_vmm = evv
            if own_st:
                vcur = v_own[:, (so * 4 + tl) * KW:(so * 4 + tl + 1) * KW]
                evve = mk.step(V, [evv], lambda E, vcur=vcur, vbank=vbank:
                               E.tensor_copy(vcur, vbank[:, 0:KW]))
            else:
                vs_ = t % 2
                vcur = v_scr[vs_][:, 0:KW]
                evve = mk.step(V, [evv, vscr_free[vs_]],
                               lambda E, vcur=vcur, vbank=vbank:
                               E.tensor_copy(vcur, vbank[:, 0:KW]))
            vbank_free[vb] = evve

            kslot = t % 2
            ev_kt = mk.step(T, [ev_krot, kpos_free[kslot], ps7_free],
                            lambda E, kout=kout, tl=tl: [
                E.transpose(ps7[:, j * 128:(j + 1) * 128],
                            kout[:, j, tl * 128:(tl + 1) * 128], ident_bf[:])
                for j in range(NKV)])
            ev_kte = mk.step(S, [ev_kt], lambda E, kslot=kslot:
                             E.copy(kpos_t[kslot][:, 0:KW], ps7[:, 0:KW]))
            ps7_free = ev_kte
            ev_kp = mk.step(V, [ev_kte, kprime_free], lambda E, kslot=kslot: [
                E.tensor_single_scalar(kprime[:, h * 128:(h + 1) * 128],
                                       kpos_t[kslot][:, (h // 4) * 128:(h // 4 + 1) * 128],
                                       kdec_cols[:, h:h + 1], ALU.mult)
                for h in range(NH)])
            kpos_free[kslot] = ev_kp
            last_kt = ev_kt

            ev_sn = None
            if own_st:
                co = so * 4 + tl
                sslot = 0
                ev_sn = mk.step(V, [ev_kv_upd, snap_free[sslot]],
                                lambda E, sslot=sslot:
                                E.tensor_copy(snap[sslot][:], kv_sb[:]))
                ev_snd = mk.dma(Y, [ev_sn], KVSNAP[co], snap[sslot][:])
                snap_free[sslot] = ev_snd

            last_evd = None
            for gp in range(2):
                def dmm(E, gp=gp, vcur=vcur):
                    out = None
                    for gi in range(2):
                        g = 2 * gp + gi
                        bank = ps[6] if gi == 0 else ps[1]
                        for hp in range(4):
                            out = E.matmul(
                                bank[:, hp * 128:(hp + 1) * 128],
                                kprime[:, (g * 4 + hp) * 128:(g * 4 + hp + 1) * 128],
                                vcur[:, g * 128:(g + 1) * 128],
                                start=True, stop=True)
                    return out
                evd = mk.step(T, [ev_kp, evve, ps6_free], dmm)
                last_evd = evd
                ev_up = mk.step(V, [evd] + ([ev_sn] if ev_sn else []),
                                lambda E, gp=gp: [
                    E.scalar_tensor_tensor(
                        kv_sb[:, ((2 * gp + gi) * 4 + hp) * 128:
                              ((2 * gp + gi) * 4 + hp + 1) * 128],
                        kv_sb[:, ((2 * gp + gi) * 4 + hp) * 128:
                              ((2 * gp + gi) * 4 + hp + 1) * 128],
                        c.blk[(2 * gp + gi) * 4 + hp],
                        (ps[6] if gi == 0 else ps[1])[:, hp * 128:(hp + 1) * 128],
                        ALU.mult, ALU.add)
                    for gi in range(2) for hp in range(4)])
                ps6_free = ev_up
                ev_kv_upd = ev_up
                ev_last_delta_tile = ev_up
            kprime_free = last_evd
            if not own_st:
                vscr_free[t % 2] = last_evd
        xnst_free[st % 2] = [ev_last_vmm] + ([ev_xn_dma] if ev_xn_dma else [])
        krot_free = last_kt
        ev_delta_prev_st = ev_last_delta_tile

    pump.finish()
    a1.close()
    lb_scope.close()

    mk.phase()
    a2 = ExitStack()

    def sb2(name, shape, dtype):
        return a2.enter_context(nc.sbuf_tensor(name, shape, dtype))

    mask_pack = sb2("mask_pack", [128, NG * 512], BF16)
    qdec_pack = sb2("qdec_pack", [128, NG * 512], BF16)

    mp0 = ExitStack()
    rampb = mp0.enter_context(nc.sbuf_tensor("rampb2", [128, 128], F32))
    mscr = mp0.enter_context(nc.sbuf_tensor("mscr", [128, 512], F32))
    ev_rb = mk.step(T, [], lambda E:
                    E.matmul(ps[2][:, 0:128], ones_row[0:1, :], iota_row[0:1, :],
                             start=True, stop=True))
    ev_rbs = mk.step(V, [ev_rb], lambda E:
                     E.tensor_copy(rampb[:], ps[2][:, 0:128]))
    ev_diff = mk.step(V, [ev_rbs], lambda E:
                      E.tensor_single_scalar(mscr[:, 0:128], rampb[:], iota_p[:, 0:1],
                                             ALU.subtract))
    ev_h2 = ev_diff
    for h in range(NH):
        lam = c.lam[h]
        g, hp = h // 4, h % 4
        msl = slice(g * 512 + hp * 128, g * 512 + (hp + 1) * 128)
        e1 = mk.step(V, [ev_h2], lambda E: [
            E.tensor_single_scalar(mscr[:, 128:256], mscr[:, 0:128], 0.0, ALU.is_ge),
            E.tensor_single_scalar(mscr[:, 256:384], mscr[:, 0:128], 0.0, ALU.max),
        ])
        e2 = mk.step(S, [e1], lambda E, lam=lam: E.activation(
            out=mscr[:, 384:512], in_=mscr[:, 256:384], func=AF.Exp, scale=-lam))
        e3 = mk.step(V, [e2], lambda E, msl=msl:
                     E.tensor_mul(mask_pack[:, msl], mscr[:, 384:512], mscr[:, 128:256]))
        e4 = mk.step(S, [ev_rbs, e3], lambda E, lam=lam: E.activation(
            out=mscr[:, 384:512], in_=rampb[:], func=AF.Exp, scale=-lam))
        e5 = mk.step(V, [e4], lambda E, msl=msl, lam=lam:
                     E.tensor_scalar(qdec_pack[:, msl], mscr[:, 384:512],
                                     math.exp(-lam), None, ALU.mult))
        ev_h2 = e5
    mp0.close()

    wq_bf = sb2("wq_bf", [128, NDC * QW], BF16)
    wstg2 = [pstg[0][:, 0:QW // 2], pstg[1][:, 0:QW // 2]]
    xn_q = [sb2(f"xn_q{i}", [128, NDC * 512], BF16) for i in range(1)]
    q_st = sb2("q_st", [128, NH * 512], BF16)
    q_rot = sb2("q_rot", [128, NH * 512], BF16)
    rsc2 = sb2("rsc2", [128, 1024], BF16)
    qpk = [sb2(f"qpk{i}", [128, 512], BF16) for i in range(2)]
    scm = [sb2(f"scm{i}", [128, 512], BF16) for i in range(2)]
    kvsn = [sb2(f"kvsn{i}", [128, NH * 128], BF16) for i in range(2)]
    o_stage = [sb2(f"ostage{i}", [128, NH * 128], BF16) for i in range(1)]

    wfree2 = [None, None]
    last2 = None
    for dc2 in range(2 * NDC):
        dc, hh = dc2 // 2, dc2 % 2
        s = dc2 % 2
        HW = QW // 2
        ld = mk.dma(Y, [wfree2[s]], wstg2[s][:],
                    WQT[dc * 128:(dc + 1) * 128, hh * HW:(hh + 1) * HW])
        ce = mk.step(V, [ld], lambda E, s=s, dc=dc, hh=hh, HW=HW:
                     E.tensor_copy(wq_bf[:, dc * QW + hh * HW: dc * QW + (hh + 1) * HW],
                                   wstg2[s][:]))
        wfree2[s] = ce
        last2 = ce
    ev_wq = last2

    xnq_free = [None]
    kvsn_free = [None, None]
    ost_free = [None]
    qpk_free = [None, None]
    scm_free = [None, None]
    opack_free = [None, None]
    ev_q_last = None
    ev_qrot_free = None
    ps6a_free = None
    rot2_free = [None, None]

    for so in range(c.NST_OWN):
        xq = xn_q[0]
        ev_xq = mk.dma(Y, [xnq_free[0]],
                       xq[:].rearrange("p (dc t) -> p dc t", t=512),
                       XN_SCR[:].rearrange("p (dc t) -> p dc t", t=c.OWN)
                       [:, :, so * 512:(so + 1) * 512])
        ev_q = []
        for oc in range(NH):
            bank = ps[oc % 2]
            prev = ev_q[oc - 2][1] if oc >= 2 else ev_q_last

            def qmm(E, oc=oc, bank=bank, xq=xq):
                out = None
                for dc in range(NDC):
                    out = E.matmul(
                        bank[:],
                        wq_bf[:, dc * QW + oc * 128: dc * QW + (oc + 1) * 128],
                        xq[:, dc * 512:(dc + 1) * 512],
                        start=(dc == 0), stop=(dc == NDC - 1))
                return out
            evm = mk.step(T, [ev_xq, ev_wq, prev], qmm)
            eve = mk.step(S, [evm, ev_qrot_free if oc == 0 else None],
                          lambda E, oc=oc, bank=bank:
                          E.copy(q_st[:, oc * 512:(oc + 1) * 512], bank[:]))
            ev_q.append((evm, eve))
        xnq_free[0] = ev_q[-1][0]
        ev_q_last = ev_q[-1][1]

        cosl = cos_own[:, so * 512:(so + 1) * 512]
        sinl = sin_own[:, so * 512:(so + 1) * 512]
        ev_qrot = None
        for h in range(NH):
            p7s = ps[4 + h % 2][:]
            ev_rt = mk.step(T, [ev_q[h][1], ev_rotm, rot2_free[h % 2]],
                            lambda E, h=h, p7s=p7s:
                            E.matmul(p7s, rotm[:], q_st[:, h * 512:(h + 1) * 512],
                                     start=True, stop=True))
            ev_qrot = mk.chain(V, [ev_rt, ev_qrot, ev_qrot_free if h < 2 else None], [
                lambda E, h=h, cosl=cosl:
                    E.tensor_tensor(rsc2[:, 0:512], q_st[:, h * 512:(h + 1) * 512],
                                    cosl, ALU.mult),
                lambda E, p7s=p7s, sinl=sinl:
                    E.tensor_tensor(rsc2[:, 512:1024], p7s, sinl, ALU.mult),
                lambda E, h=h:
                    E.tensor_add(q_rot[:, h * 512:(h + 1) * 512],
                                 rsc2[:, 0:512], rsc2[:, 512:1024]),
            ])
            rot2_free[h % 2] = ev_qrot
        ev_qrot_free = None

        last_mix = None
        for tl in range(4):
            co = so * 4 + tl
            kslot = co % 2
            ev_ksd = mk.dma(Y, [kvsn_free[kslot]], kvsn[kslot][:], KVSNAP[co])
            oslot = 0
            ev_o_evs = []
            for g in range(NG):
                qslot = g % 2
                ev_qp = mk.step(V, [ev_qrot, qpk_free[qslot]],
                                lambda E, qslot=qslot, g=g, tl=tl:
                                E.tensor_tensor(
                    qpk[qslot][:].rearrange("p (a t) -> p a t", t=128),
                    q_rot[:].rearrange("p (h t) -> p h t", t=512)
                    [:, 4 * g:4 * g + 4, tl * 128:(tl + 1) * 128],
                    qdec_pack[:, g * 512:(g + 1) * 512].rearrange("p (a t) -> p a t", t=128),
                    ALU.mult))
                ev_sc = mk.step(T, [ev_qrot, ps6a_free],
                                lambda E, g=g, co=co, tl=tl:
                                E.matmul(ps[6][:],
                                         k_own[:].rearrange("p (kv t) -> p kv t", t=c.OWN)
                                         [:, g, co * 128:(co + 1) * 128],
                                         q_rot[:].rearrange("p (h t) -> p h t", t=512)
                                         [:, 4 * g:4 * g + 4, tl * 128:(tl + 1) * 128],
                                         start=True, stop=True))
                sslot = g % 2
                ev_ms = mk.step(V, [ev_sc, scm_free[sslot]],
                                lambda E, sslot=sslot, g=g:
                                E.tensor_mul(scm[sslot][:], ps[6][:],
                                             mask_pack[:, g * 512:(g + 1) * 512]))
                ps6a_free = ev_ms
                obank = ps[2 + g % 2]

                def omm(E, g=g, co=co, sslot=sslot, qslot=qslot, kslot=kslot, obank=obank):
                    E.matmul(obank[:],
                             v_own[:, co * KW + g * 128: co * KW + (g + 1) * 128],
                             scm[sslot][:], start=True, stop=False)
                    out = None
                    for hp in range(4):
                        h = 4 * g + hp
                        out = E.matmul(obank[:, hp * 128:(hp + 1) * 128],
                                       kvsn[kslot][:, h * 128:(h + 1) * 128],
                                       qpk[qslot][:, hp * 128:(hp + 1) * 128],
                                       start=False, stop=(hp == 3))
                    return out
                ev_om = mk.step(T, [ev_ms, ev_qp, ev_ksd, opack_free[g % 2]], omm)
                scm_free[sslot] = ev_om
                qpk_free[qslot] = ev_om
                ev_oe = mk.step(S, [ev_om] + ([ost_free[oslot]] if g == 0 else []),
                                lambda E, g=g, oslot=oslot, obank=obank:
                                E.copy(o_stage[oslot][:, g * 512:(g + 1) * 512],
                                       obank[:]))
                opack_free[g % 2] = ev_oe
                ev_o_evs.append(ev_oe)
                last_mix = ev_om
            kvsn_free[kslot] = ev_o_evs[-1]
            ev_od = mk.dma(Y, ev_o_evs, OFEAT[co], o_stage[oslot][:])
            ost_free[oslot] = ev_od
        ev_qrot_free = last_mix
    a2.close()
    a_scope.close()

    mk.phase()
    a2b = ExitStack()

    def sb3(name, shape, dtype):
        return a2b.enter_context(nc.sbuf_tensor(name, shape, dtype))

    wo_bf = sb3("wo_bf", [128, NH * D], BF16)
    wstg3 = [sb3(f"wstg3_{i}", [128, D], F32) for i in range(2)]
    o_buf = [sb3(f"obuf{i}", [128, 4 * NH * 128], BF16) for i in range(2)]
    att_f = sb3("att_f", [128, 512], F32)
    att_stage = [sb3(f"attst{i}", [128, D], F32) for i in range(4)]

    wfree3 = [None, None]
    last3 = None
    for ec in range(NH):
        s = ec % 2
        ld = mk.dma(Y, [wfree3[s]], wstg3[s][:], WOT[ec * 128:(ec + 1) * 128, :])
        ce = mk.step(V, [ld], lambda E, s=s, ec=ec:
                     E.tensor_copy(wo_bf[:, ec * D:(ec + 1) * D], wstg3[s][:]))
        wfree3[s] = ce
        last3 = ce
    ev_wo = last3

    obuf_free = [None, None]
    attst_free = [None] * 4
    attf_free = None
    pstp_free = None
    abank_free = [None, None]
    for so in range(c.NST_OWN):
        oslot = so % 2
        ev_ob = mk.dma(Y, [obuf_free[oslot]],
                       o_buf[oslot][:].rearrange("p (cc f) -> p cc f", f=NH * 128),
                       OFEAT[so * 4:(so + 1) * 4].transpose([1, 0, 2]))
        last_am = None
        for oc in range(NDC):
            bank = ps[oc % 2]

            def amm(E, oc=oc, oslot=oslot, bank=bank):
                out = None
                for ec in range(NH):
                    out = E.matmul(bank[:],
                                   wo_bf[:, ec * D + oc * 128: ec * D + (oc + 1) * 128],
                                   o_buf[oslot][:].rearrange("p (cc f) -> p cc f", f=NH * 128)
                                   [:, :, ec * 128:(ec + 1) * 128],
                                   start=(ec == 0), stop=(ec == NH - 1))
                return out
            ev_am = mk.step(T, [ev_ob, ev_wo, abank_free[oc % 2]], amm)
            last_am = ev_am
            ev_ae = mk.step(V, [ev_am, attf_free], lambda E, bank=bank:
                            E.tensor_copy(att_f[:], bank[:]))
            abank_free[oc % 2] = ev_ae
            ev_tp = mk.step(T, [ev_ae, pstp_free], lambda E: [
                E.transpose(ps[4][:, j * 128:(j + 1) * 128],
                            att_f[:, j * 128:(j + 1) * 128], ident_f32[:])
                for j in range(4)])
            attf_free = ev_tp
            ev_te = mk.step(S, [ev_tp] + (attst_free if oc == 0 else []),
                            lambda E, oc=oc: [
                E.copy(att_stage[j][:, oc * 128:(oc + 1) * 128],
                       ps[4][:, j * 128:(j + 1) * 128])
                for j in range(4)])
            pstp_free = ev_te
        for j in range(4):
            t = so * 4 + j
            evd = mk.dma(Y, [pstp_free], ATTN[t * 128:(t + 1) * 128, :],
                         att_stage[j][:])
            attst_free[j] = evd
        obuf_free[oslot] = last_am
    a2b.close()

    TOPK = c.TOPK
    yacc_scope = ExitStack()
    y_acc = yacc_scope.enter_context(nc.sbuf_tensor("y_acc", [128, NTT * D], F32))
    xnh_t = yacc_scope.enter_context(nc.sbuf_tensor("xnh", [128, NDC * HALF], BF16))

    ev_out_final = []

    def build_half(half):
        mk.phase()
        b = ExitStack()

        def sbb(name, shape, dtype):
            return b.enter_context(nc.sbuf_tensor(name + str(half), shape, dtype))

        combf = sbb("combf", [max(NEXP, 2), HALF], BF16)
        selm = sbb("selm", [128, 128], BF16)
        gsc_scope = ExitStack()

        def sbg(name, shape, dtype):
            return gsc_scope.enter_context(nc.sbuf_tensor(name + str(half), shape, dtype))

        gstg = sbg("gstg", [128, NDC * NEXP], F32)
        gate_bf = sbg("gate_bf", [128, NDC * NEXP], BF16)
        sgt = sbg("sgt", [128, NEXP], F32)
        gsc = sbg("gsc", [128, NEXP], F32)
        gm = sbg("gm", [128, 4], F32)
        comb_t = sbg("comb_t", [128, NEXP], F32)

        ev_xnh = mk.dma(Y, [],
                        xnh_t[:].rearrange("p (dc t) -> p dc t", t=HALF),
                        XN_SCR[:].rearrange("p (dc t) -> p dc t", t=c.OWN)
                        [:, :, half * HALF:(half + 1) * HALF])
        ev_gld = mk.dma(Y, [], gstg[:], GATE)
        ev_gc = mk.step(V, [ev_gld], lambda E: E.tensor_copy(gate_bf[:], gstg[:]))

        ps6b_free = None
        ps5_free = None
        ev_tk_prev = None
        ev_tk_prev2 = None
        ev_comb_all = None
        for tt in range(NTT):
            def gmm(E, tt=tt):
                out = None
                for dc in range(NDC):
                    out = E.matmul(ps[6][:, 0:NEXP],
                                   xnh_t[:, dc * HALF + tt * 128: dc * HALF + (tt + 1) * 128],
                                   gate_bf[:, dc * NEXP:(dc + 1) * NEXP],
                                   start=(dc == 0), stop=(dc == NDC - 1))
                return out
            ev_gm = mk.step(T, [ev_xnh, ev_gc, ps6b_free], gmm)
            ev_sig = mk.step(S, [ev_gm, ev_tk_prev], lambda E: E.activation(
                out=sgt[:], in_=ps[6][:, 0:NEXP], func=AF.Sigmoid))
            ps6b_free = ev_sig

            emits = [lambda E: E.tensor_copy(gsc[:], sgt[:])]
            for r in range(TOPK):
                emits.append(lambda E: E.reduce_max(gm[:, 0:1], gsc[:], axis=AX.X))
                if r < TOPK - 1:
                    emits.append(lambda E: E.tensor_single_scalar(
                        comb_t[:], gsc[:], gm[:, 0:1], ALU.is_ge))
                    emits.append(lambda E: E.scalar_tensor_tensor(
                        gsc[:], comb_t[:], -1e30, gsc[:], ALU.mult, ALU.add))
            emits += [
                lambda E: E.tensor_single_scalar(comb_t[:], sgt[:], gm[:, 0:1], ALU.is_ge),
                lambda E: E.tensor_mul(comb_t[:], comb_t[:], sgt[:]),
                lambda E: E.reduce_sum(gm[:, 1:2], comb_t[:], axis=AX.X),
                lambda E: E.tensor_scalar(gm[:, 2:3], gm[:, 1:2], 1e-20, None, ALU.add),
                lambda E: E.reciprocal(gm[:, 3:4], gm[:, 2:3]),
                lambda E: E.tensor_single_scalar(comb_t[:], comb_t[:], gm[:, 3:4], ALU.mult),
            ]
            ev_tk = mk.chain(V, [ev_sig, ev_tk_prev2], emits)
            ev_tk_prev = ev_tk
            ev_tk_prev2 = ev_ct if False else None
            ev_ct = mk.step(T, [ev_tk, ps5_free],
                            lambda E: E.transpose(ps[5][0:NEXP, 0:128], comb_t[:],
                                                  ident_f32[:]))
            ev_tk_prev2 = ev_ct
            ev_ce = mk.step(V, [ev_ct], lambda E, tt=tt:
                            E.tensor_copy(combf[0:NEXP, tt * 128:(tt + 1) * 128],
                                          ps[5][0:NEXP, 0:128]))
            ps5_free = ev_ce
            ev_comb_all = ev_ce

        if half == 0:
            pass
        mk.fence_all()
        gsc_scope.close()
        wgu = [sbb(f"wgu{i}", [128, NDC * FEXP], BF16) for i in range(2)]
        wdsl = [sbb(f"wdsl{i}", [128, NFC * 512], BF16) for i in range(2)]
        actp = [sbb(f"actp{i}", [128, NFC * HALF], BF16) for i in range(2)]
        sgb = sbb("sgb", [128, NTG * 512], BF16)
        ub = sbb("ub", [128, 512], BF16)
        cbc = [sbb(f"cbc{i}", [128, HALF], BF16) for i in range(1)]
        ev_sel = mk.step(V, [], lambda E: E.memset(selm[:], 0.0))
        sel_cur = [None]
        ev_selmm = [None]

        wgu_free = [None, None]
        wd_free = [None] * 2
        cbc_free = [None]
        actp_free = [None, None]
        gu_bank_free = [None] * 4
        ybank_free = [None, None, None]
        ybank_free[1] = ps5_free
        ev_y_evac = {}

        ybank_free[1] = ps5_free
        ybank_free[2] = ps6b_free
        for e in range(NEXP):
            first_pass = (e == 0)
            ap_i = e % 2
            nq = max(1, NDC // 4)
            ev_wg, ev_wu = [], []
            for ix, qq in enumerate(range(0, NDC, nq)):
                ev_wg.append(mk.dma(Y if ix % 2 == 0 else S,
                                    [wgu_free[0], pump.done[("wg", e)]],
                                    wgu[0][:].rearrange("p (dc f) -> p dc f", f=FEXP)
                                    [:, qq:qq + nq, :],
                                    WG_BF[e, qq:qq + nq].transpose([1, 0, 2])))
            for ix, qq in enumerate(range(0, NDC, nq)):
                ev_wu.append(mk.dma(Y if ix % 2 == 1 else S,
                                    [wgu_free[1], pump.done[("wu", e)]],
                                    wgu[1][:].rearrange("p (dc f) -> p dc f", f=FEXP)
                                    [:, qq:qq + nq, :],
                                    WU_BF[e, qq:qq + nq].transpose([1, 0, 2])))
            ev_selrow = mk.step(V, [ev_sel, ev_selmm[0]],
                                lambda E, e=e: E.tensor_scalar(
                selm[:], iota_p[:, 0:1].to_broadcast([128, 128]),
                float(e), None, ALU.is_equal))
            last_gmm = None
            for fc in range(NFC):
                for tg in range(NTG):
                    bk = (NTG * fc + tg) % 4
                    bank = ps[bk]

                    def gmm(E, fc=fc, tg=tg, bank=bank):
                        out = None
                        for dc in range(NDC):
                            out = E.matmul(
                                bank[:],
                                wgu[0][:, dc * FEXP + fc * 128: dc * FEXP + (fc + 1) * 128],
                                xnh_t[:, dc * HALF + tg * 512: dc * HALF + (tg + 1) * 512],
                                start=(dc == 0), stop=(dc == NDC - 1))
                        return out
                    evb = mk.step(T, ev_wg + [ev_xnh, gu_bank_free[bk]], gmm)
                    last_gmm = evb
                    ev_si = mk.step(S, [evb] +
                                    ([actp_free[ap_i]] if (fc == 0 and tg == 0) else []),
                                    lambda E, fc=fc, tg=tg, bank=bank, ap_i=ap_i:
                                    E.activation(
                        out=actp[ap_i][:, fc * HALF + tg * 512: fc * HALF + (tg + 1) * 512],
                        in_=bank[:], func=AF.Silu))
                    gu_bank_free[bk] = ev_si
            wgu_free[0] = last_gmm
            ev_cb = None
            for tg in range(NTG):
                bkc = tg % 2
                evb = mk.step(T, [ev_comb_all, gu_bank_free[bkc], ev_selrow] +
                              ([cbc_free[0]] if tg == 0 else []),
                              lambda E, tg=tg, bkc=bkc:
                              E.matmul(ps[bkc][:], selm[0:NEXP, :],
                                       combf[0:NEXP, tg * 512:(tg + 1) * 512],
                                       start=True, stop=True))
                ev_cb = mk.step(V, [evb], lambda E, tg=tg, bkc=bkc:
                                E.tensor_copy(cbc[0][:, tg * 512:(tg + 1) * 512],
                                              ps[bkc][:]))
                gu_bank_free[bkc] = ev_cb
            ev_selmm[0] = evb
            last_uv = None
            for fc in range(NFC):
                for tg in range(NTG):
                    bk = (NTG * fc + tg) % 4
                    bank = ps[bk]

                    def umm(E, fc=fc, tg=tg, bank=bank):
                        out = None
                        for dc in range(NDC):
                            out = E.matmul(
                                bank[:],
                                wgu[1][:, dc * FEXP + fc * 128: dc * FEXP + (fc + 1) * 128],
                                xnh_t[:, dc * HALF + tg * 512: dc * HALF + (tg + 1) * 512],
                                start=(dc == 0), stop=(dc == NDC - 1))
                        return out
                    evb = mk.step(T, ev_wu + [ev_xnh, gu_bank_free[bk]], umm)
                    asl = actp[ap_i][:, fc * HALF + tg * 512: fc * HALF + (tg + 1) * 512]
                    ev_uc = mk.chain(V, [evb, ev_cb, gu_bank_free[bk], last_uv], [
                        lambda E, tg=tg, bank=bank:
                            E.tensor_mul(ub[:, 0:512], bank[:],
                                         cbc[0][:, tg * 512:(tg + 1) * 512]),
                        lambda E, asl=asl:
                            E.tensor_mul(asl, asl, ub[:, 0:512]),
                    ])
                    gu_bank_free[bk] = ev_uc
                    last_uv = ev_uc
            wgu_free[1] = evb
            ev_act = last_uv
            cbc_free[0] = last_uv
            lastd = None
            for dg in range(NDG):
                slot = dg % 2
                wd_free[slot] = mk.dma(
                    Y if dg % 2 == 0 else S, [wd_free[slot], pump.done[("wd", e)]],
                    wdsl[slot][:].rearrange("p (fc f) -> p fc f", f=512),
                    WD_BF[e, :, :, dg * 512:(dg + 1) * 512].transpose([1, 0, 2]))
                for tt in range(NTT):
                    ybk = (dg * NTT + tt) % 3
                    ybank = ps[4 + ybk]

                    def ymm(E, dg=dg, tt=tt, ybank=ybank, slot=slot, ap_i=ap_i):
                        out = None
                        for fc in range(NFC):
                            out = E.matmul(
                                ybank[:],
                                actp[ap_i][:, fc * HALF + tt * 128: fc * HALF + (tt + 1) * 128],
                                wdsl[slot][:, fc * 512:(fc + 1) * 512],
                                start=(fc == 0), stop=(fc == NFC - 1))
                        return out
                    ev_ym = mk.step(T, [ev_act, wd_free[slot], ybank_free[ybk]], ymm)
                    ysl = y_acc[:, tt * D + dg * 512: tt * D + (dg + 1) * 512]
                    if first_pass:
                        ev_ye = mk.step(V, [ev_ym], lambda E, ysl=ysl, ybank=ybank:
                                        E.tensor_copy(ysl, ybank[:]))
                    else:
                        ev_ye = mk.step(V, [ev_ym, ev_y_evac.get((tt, dg))],
                                        lambda E, ysl=ysl, ybank=ybank:
                                        E.tensor_add(ysl, ysl, ybank[:]))
                    ev_y_evac[(tt, dg)] = ev_ye
                    ybank_free[ybk] = ev_ye
                    lastd = ev_ym
                wd_free[slot] = lastd
            actp_free[ap_i] = lastd
        mk.fence_all()
        b.close()

        bs = ExitStack()

        def sbs(name, shape, dtype):
            return bs.enter_context(nc.sbuf_tensor(name + str(half), shape, dtype))

        NSH = c.NSC // 2
        FSH2 = NSH * 128
        sgh = sbs("sgh", [128, NDC * FSH2], BF16)
        suh = sbs("suh", [128, NDC * FSH2], BF16)
        act_sh = sbs("act_sh", [128, c.NSC * HALF], BF16)
        sdsl = [sbs(f"sdsl{i}", [128, c.NSC * 512], BF16) for i in range(2)]

        sh_bank_free = [None] * 4
        shscr_free = None
        ev_shact = None
        sgh_free = None
        for sh in range(2):
            nq = max(1, NDC // 4)
            ev_sg, ev_su = [], []
            for qq in range(0, NDC, nq):
                ev_sg.append(mk.dma(Y, [sgh_free, pump.done[("sg", sh)]],
                                    sgh[:].rearrange("p (dc f) -> p dc f", f=FSH2)
                                    [:, qq:qq + nq, :],
                                    SG_BF[sh, qq:qq + nq].transpose([1, 0, 2])))
            for qq in range(0, NDC, nq):
                ev_su.append(mk.dma(Y, [sgh_free, pump.done[("su", sh)]],
                                    suh[:].rearrange("p (dc f) -> p dc f", f=FSH2)
                                    [:, qq:qq + nq, :],
                                    SU_BF[sh, qq:qq + nq].transpose([1, 0, 2])))
            last_h_mm = None
            for fc in range(NSH):
                sc = sh * NSH + fc
                bank_evs = []
                for mi in range(2):
                    wsb = sgh if mi == 0 else suh
                    wdep = ev_sg if mi == 0 else ev_su
                    for tg in range(NTG):
                        bank = ps[2 * mi + (tg % 2)]

                        def hmm(E, wsb=wsb, fc=fc, tg=tg, bank=bank):
                            out = None
                            for dc in range(NDC):
                                out = E.matmul(
                                    bank[:],
                                    wsb[:, dc * FSH2 + fc * 128: dc * FSH2 + (fc + 1) * 128],
                                    xnh_t[:, dc * HALF + tg * 512: dc * HALF + (tg + 1) * 512],
                                    start=(dc == 0), stop=(dc == NDC - 1))
                            return out
                        evb = mk.step(T, wdep + [ev_xnh,
                                                  sh_bank_free[2 * mi + (tg % 2)]], hmm)
                        bank_evs.append(evb)
                        last_h_mm = evb
                ev_si = mk.step(S, bank_evs[0:NTG] +
                                ([shscr_free] if shscr_free else []),
                                lambda E, sc=sc: [
                    E.activation(out=act_sh[:, sc * HALF + tg * 512: sc * HALF + (tg + 1) * 512],
                                 in_=ps[tg % 2][:], func=AF.Silu)
                    for tg in range(NTG)])
                ev_ac = mk.step(V, bank_evs[NTG:2 * NTG] + [ev_si],
                                lambda E, sc=sc: [
                    E.tensor_mul(act_sh[:, sc * HALF + tg * 512: sc * HALF + (tg + 1) * 512],
                                 act_sh[:, sc * HALF + tg * 512: sc * HALF + (tg + 1) * 512],
                                 ps[2 + (tg % 2)][:])
                    for tg in range(NTG)])
                for z in range(2):
                    sh_bank_free[z] = ev_si
                    sh_bank_free[2 + z] = ev_ac
                shscr_free = ev_ac
                ev_shact = ev_ac
            sgh_free = last_h_mm
        sd_free = [None, None]
        for dg in range(NDG):
            slot = dg % 2
            nq = max(1, c.NSC // 3)
            ev_sdw = []
            for qq in range(0, c.NSC, nq):
                ev_sdw.append(mk.dma(Y, [sd_free[slot], pump.done[("sd", 0)]],
                                     sdsl[slot][:].rearrange("p (fc f) -> p fc f", f=512)
                                     [:, qq:qq + nq, :],
                                     SD_BF[qq:qq + nq, :, dg * 512:(dg + 1) * 512]
                                     .transpose([1, 0, 2])))
            lastsd = None
            for tt in range(NTT):
                ybk = (dg * NTT + tt) % 3
                ybank = ps[4 + ybk]

                def smm(E, dg=dg, tt=tt, ybank=ybank, slot=slot):
                    out = None
                    for fc in range(c.NSC):
                        out = E.matmul(
                            ybank[:],
                            act_sh[:, fc * HALF + tt * 128: fc * HALF + (tt + 1) * 128],
                            sdsl[slot][:, fc * 512:(fc + 1) * 512],
                            start=(fc == 0), stop=(fc == c.NSC - 1))
                    return out
                ev_ym = mk.step(T, [ev_shact] + ev_sdw + [ybank_free[ybk]], smm)
                ysl = y_acc[:, tt * D + dg * 512: tt * D + (dg + 1) * 512]
                ev_ye = mk.step(V, [ev_ym, ev_y_evac.get((tt, dg))],
                                lambda E, ysl=ysl, ybank=ybank:
                                E.tensor_add(ysl, ysl, ybank[:]))
                ev_y_evac[(tt, dg)] = ev_ye
                ybank_free[ybk] = ev_ye
                lastsd = ev_ym
            sd_free[slot] = lastsd
        mk.fence_all()
        bs.close()

        cc = ExitStack()

        def sbc(name, shape, dtype):
            return cc.enter_context(nc.sbuf_tensor(name + str(half), shape, dtype))

        x_t = [sbc(f"cx{i}", [128, D], F32) for i in range(2)]
        at_t = [sbc(f"cat{i}", [128, D], F32) for i in range(2)]
        o_t = [sbc(f"cot{i}", [128, D], F32) for i in range(2)]
        xt_free = [None, None]
        at_free = [None, None]
        ot_free = [None, None]
        for tt in range(NTT):
            sl = tt % 2
            row0 = half * HALF + tt * 128
            ev_x = mk.dma(Y, [xt_free[sl]], x_t[sl][:],
                          X[c.LB + row0: c.LB + row0 + 128, :])
            ev_a = mk.dma(Y, [at_free[sl]], at_t[sl][:], ATTN[row0:row0 + 128, :])
            deps = [ev_x, ev_a, ot_free[sl]] + \
                   [ev_y_evac[(tt, dg)] for dg in range(NDG)]
            ev_s = mk.chain(V, deps, [
                lambda E, sl=sl: E.tensor_add(o_t[sl][:], x_t[sl][:], at_t[sl][:]),
                lambda E, sl=sl, tt=tt: E.tensor_add(o_t[sl][:], o_t[sl][:],
                                                     y_acc[:, tt * D:(tt + 1) * D]),
            ])
            xt_free[sl] = ev_s
            at_free[sl] = ev_s
            ev_od = mk.dma(Y, [ev_s], OUT[row0:row0 + 128, :], o_t[sl][:])
            ot_free[sl] = ev_od
            ev_out_final.append(ev_od)
        cc.close()

    for _h in range(2):
        build_half(_h)

    yacc_scope.close()

    for k, eng in mk.dma_eng.items():
        mk._wait(eng, (k, mk.cnt[k]))
    mk.step(Y, ev_out_final, lambda E: E.nop())
    mk.build_blocks()
    return nc


def shard_inputs(cfg, x, wqT, wkT, wvT, woT, gate_r, wg, wu, wd, sg, su, sd):
    in_maps = []
    ncores = cfg.B * cfg.NQ
    for core in range(ncores):
        bidx, q = core // cfg.NQ, core % cfg.NQ
        xp = np.zeros((cfg.SPT, cfg.D), np.float32)
        xp[cfg.LB - q * cfg.OWN:] = x[bidx, :(q + 1) * cfg.OWN]
        pos = np.zeros((1, cfg.SPT), np.float32)
        pos[0, cfg.LB - q * cfg.OWN:] = np.arange((q + 1) * cfg.OWN, dtype=np.float32)
        in_maps.append({
            "x": xp, "pos": pos, "wqT": wqT, "wkT": wkT, "wvT": wvT, "woT": woT,
            "gate_r": gate_r, "wg": wg, "wu": wu, "wd": wd,
            "sg": sg, "su": su, "sd": sd,
        })
    return in_maps


def prep_weights(cfg, wq, wk, wv, wo, gate_w, wg, wu, wd, sg, su, sd):
    f = np.float32
    gate = np.asarray(gate_w, f)
    gate_r = np.ascontiguousarray(
        gate.reshape(cfg.NDC, 128, cfg.NEXP).transpose(1, 0, 2)
        .reshape(128, cfg.NDC * cfg.NEXP))
    return dict(
        wqT=np.ascontiguousarray(np.asarray(wq, f).T),
        wkT=np.ascontiguousarray(np.asarray(wk, f).T),
        wvT=np.ascontiguousarray(np.asarray(wv, f).T),
        woT=np.ascontiguousarray(np.asarray(wo, f).T),
        gate_r=gate_r,
        wg=np.ascontiguousarray(np.asarray(wg, f)),
        wu=np.ascontiguousarray(np.asarray(wu, f)),
        wd=np.ascontiguousarray(np.asarray(wd, f)),
        sg=np.ascontiguousarray(np.asarray(sg, f)),
        su=np.ascontiguousarray(np.asarray(su, f)),
        sd=np.ascontiguousarray(np.asarray(sd, f)),
    )


def kernel(L0_attn_x, L0_attn_wq, L0_attn_wk, L0_attn_wv, L0_attn_wo,
           L0_ffn_gate_w, L0_ffn_wg, L0_ffn_wu, L0_ffn_wd,
           L0_ffn_sg, L0_ffn_su, L0_ffn_sd):
    cfg = CFG()
    x = np.asarray(L0_attn_x, dtype=np.float32)
    w = prep_weights(cfg, L0_attn_wq, L0_attn_wk, L0_attn_wv, L0_attn_wo,
                     L0_ffn_gate_w, L0_ffn_wg, L0_ffn_wu, L0_ffn_wd,
                     L0_ffn_sg, L0_ffn_su, L0_ffn_sd)
    nc = bass.Bass("TRN2", target_bir_lowering=False)
    build_program(nc, cfg)
    in_maps = shard_inputs(cfg, x, **w)
    res = run_bass_kernel_spmd(nc, in_maps, core_ids=list(range(8)))
    out = np.empty((cfg.B, cfg.S, cfg.D), np.float32)
    for core in range(8):
        bidx, q = core // cfg.NQ, core % cfg.NQ
        out[bidx, q * cfg.OWN:(q + 1) * cfg.OWN] = res.results[core]["out"]
    return out
